# version 14
# baseline (speedup 1.0000x reference)
import contextlib
import math

import numpy as np
import ml_dtypes

import concourse.bass as bass
import concourse.mybir as mybir
from concourse.bass_utils import run_bass_kernel_spmd

F32 = mybir.dt.float32
BF16 = mybir.dt.bfloat16
AF = mybir.ActivationFunctionType
ALU = mybir.AluOpType

NCORES = 8
B_PER_CORE = 2
TSEQ = 2048
DM = 1024
PSSM = 512
NT = B_PER_CORE * TSEQ
NTILE = NT // 128
TPB = TSEQ // 128
NHEAD = 16
HDIM = 64
EPS = 1e-5

ENGINES = ("tensor", "vector", "scalar", "gpsimd", "sync")


class Prog:

    def __init__(self, nc, sems, counts):
        self.nc = nc
        self.sems = sems
        self.counts = counts
        self.items = []

    def add(self, engine, fn, waits=(), inc=None, k=1, fence=False, relax=False):
        ws = []
        for w in waits:
            if w is None:
                continue
            peng, sname, cnt = w
            if peng == engine:
                continue
            ws.append((sname, cnt))
        tok = None
        if inc is not None:
            self.counts[inc] += k
            tok = (engine, inc, self.counts[inc])
        self.items.append((engine, fn, ws, inc, self.counts[inc] if inc else 0, k,
                           relax))
        return tok

    def dma(self, engine, out, in_, waits=(), slow_ok=False, sem=None):
        if sem is None:
            sem = "dL" if engine == "sync" else "dG"
        return self.add(engine,
                        lambda e, o=out, i=in_: e.dma_start(
                            out=o, in_=i, allow_slow_non_contiguous=slow_ok),
                        waits=waits, inc=sem, k=16)

    def barrier(self):
        toks = []
        for e in ENGINES:
            toks.append(self.add(e, lambda eng: eng.nop(), inc="bar"))
        n = self.counts["bar"]
        return ("__bar__", "bar", n)

    def emit(self):
        with self.nc.Block() as block:
            for engname in ENGINES:
                self._mk(block, engname)

    def _mk(self, block, engname):
        items = [it for it in self.items if it[0] == engname]
        sems = self.sems
        hazard = engname in ("vector", "gpsimd")

        def body(eng):
            waited = {}
            first = True
            for (_, fn, ws, inc, tgt, k, relax) in items:
                padded = False
                for (sname, cnt) in ws:
                    if waited.get(sname, 0) >= cnt:
                        continue
                    eng.wait_ge(sems[sname], cnt)
                    waited[sname] = cnt
                    padded = True
                if padded:
                    eng.nop(cycle_cnt=96)
                if hazard and not first:
                    if relax:
                        eng.nop()
                    else:
                        eng.drain()
                instr = fn(eng)
                if inc is not None:
                    instr.then_inc(sems[inc], k)
                first = False
        getattr(block, engname)(body)


class Core:

    def __init__(self, nc, es):
        self.nc = nc
        self.es = es
        self.sems = {}
        self.counts = {}
        for s in ("dL", "dG", "dP", "dX0", "dX1", "dX2", "dX3", "dY0", "dY1",
                  "sT", "sV", "sC", "sG", "bar"):
            self.sems[s] = es.enter_context(nc.semaphore(s))
            self.counts[s] = 0

    def prog(self):
        return Prog(self.nc, self.sems, self.counts)


def _col_ap(dram_ap, n):
    total = 1
    for s in dram_ap.shape:
        total *= s
    assert total == n
    flat = dram_ap.rearrange(" ".join(f"a{i}" for i in range(len(dram_ap.shape)))
                             + " -> (" + " ".join(f"a{i}" for i in range(len(dram_ap.shape))) + ")") \
        if len(dram_ap.shape) > 1 else dram_ap
    return flat.rearrange("(j p) -> p j", p=128)


def _row_ap(dram_ap, n):
    flat = dram_ap
    if len(dram_ap.shape) > 1:
        flat = dram_ap.rearrange(" ".join(f"a{i}" for i in range(len(dram_ap.shape)))
                                 + " -> (" + " ".join(f"a{i}" for i in range(len(dram_ap.shape))) + ")")
    return flat.rearrange("(one n) -> one n", one=1)


def _wslices_ap(w_ap, kin, kout):
    return w_ap.rearrange("(k p) o -> p k o", p=128)


def _wslices_dst(sb_handle, ktiles, kout):
    return sb_handle[:, :].rearrange("p (k o) -> p k o", k=ktiles)


def build_program(parts=("ssm", "ffn1", "att", "ffn2"), debug=False, dbg_tile=0):
    nc = bass.Bass("TRN2", target_bir_lowering=False)
    es = contextlib.ExitStack()
    co = Core(nc, es)

    dt_in = {}
    def din(name, shape, dtype=F32):
        dt_in[name] = nc.dram_tensor(name, list(shape), dtype, kind="ExternalInput").ap()
        return dt_in[name]

    x_in = din("x", (NT, DM))
    lam_re = din("L0_attn_a_Lambda_re", (PSSM,))
    lam_im = din("L0_attn_a_Lambda_im", (PSSM,))
    Bw = din("L0_attn_a_B", (PSSM, DM, 2))
    Cw = din("L0_attn_a_C", (DM, PSSM, 2))
    Dw = din("L0_attn_a_D", (DM,))
    lstep = din("L0_attn_a_log_step", (PSSM, 1))
    w_attn = din("L0_attn_b_w_attn", (DM, 3 * DM))
    b_attn = din("L0_attn_b_b_attn", (3 * DM,))
    w_proj = din("L0_attn_b_w_proj", (DM, DM))
    b_proj = din("L0_attn_b_b_proj", (DM,))
    cn_w1 = din("L0_ffn_cn_w1", (DM, DM))
    cn_b1 = din("L0_ffn_cn_b1", (DM,))
    cn_w2 = din("L0_ffn_cn_w2", (DM, 512))
    cn_b2 = din("L0_ffn_cn_b2", (512,))
    cn_w3 = din("L0_ffn_cn_w3", (512, 256))
    cn_b3 = din("L0_ffn_cn_b3", (256,))
    cn_w4 = din("L0_ffn_cn_w4", (256, 1))
    cn_b4 = din("L0_ffn_cn_b4", (1,))
    gc_w1 = din("L0_ffn_gc_w1", (DM, 512))
    gc_b1 = din("L0_ffn_gc_b1", (512,))
    gc_w2 = din("L0_ffn_gc_w2", (512, 256))
    gc_b2 = din("L0_ffn_gc_b2", (256,))
    gc_w3 = din("L0_ffn_gc_w3", (256, 1))
    gc_b3 = din("L0_ffn_gc_b3", (1,))
    a_range = din("L0_ffn_adaptive_range", (1,))
    identb = din("c_identb", (128, 128), BF16)
    identf = din("c_identf", (128, 128), F32)
    triuf = din("c_triuf", (128, 128), F32)
    trium = din("c_trium", (128, 128), BF16)

    y_out = nc.dram_tensor("y", [NT, DM], F32, kind="ExternalOutput").ap()
    r0 = nc.dram_tensor("r0", [NT, DM], F32, kind="Internal").ap()
    r1 = nc.dram_tensor("r1", [NT, DM], F32, kind="Internal").ap()
    rcscr = nc.dram_tensor("rcscr", [1, DM], F32, kind="Internal").ap()
    dbg = {}
    if debug:
        for nm, shape, dt in (("d_pcol", [128, 160], F32), ("d_ainvre", [128, 512], F32),
                              ("d_ainvim", [128, 512], F32), ("d_are", [128, 512], F32),
                              ("d_aim", [128, 512], F32), ("d_bbre", [128, 512], BF16),
                              ("d_bbim", [128, 512], BF16), ("d_cret", [128, 1024], BF16),
                              ("d_xln", [128, 1024], BF16), ("d_xlnT", [128, 1024], BF16),
                              ("d_vre", [128, 512], F32), ("d_vim", [128, 512], F32),
                              ("d_xsre", [128, 512], BF16), ("d_xsim", [128, 512], BF16),
                              ("d_du", [128, 1024], F32), ("d_dbc", [128, 1024], F32), ("d_xo", [128, 1024], F32),
                              ("a_qT", [128, 4096], BF16), ("a_kT", [128, 512], BF16),
                              ("a_vst", [128, 1040], BF16), ("a_e00", [128, 512], BF16),
                              ("a_yb", [64, 512], BF16), ("a_zbc", [64, 512], F32),
                              ("f_mrow", [1, 1024], F32), ("f_mcol", [128, 8], F32),
                              ("f_gcol1", [128, 4], F32), ("f_rows", [1, 16], F32),
                              ("f_rstore", [128, 32], F32)):
            dbg[nm] = nc.dram_tensor(nm, shape, dt, kind="ExternalOutput").ap()

    chain = {"ssm": (x_in, r0), "ffn1": (r0, r1), "att": (r1, r0), "ffn2": (r0, y_out)}
    io = {p: list(chain[p]) for p in parts}
    io[parts[0]][0] = x_in
    io[parts[-1]][1] = y_out

    sb = nc.alloc_sbuf_tensor
    psa = nc.alloc_psum_tensor

    ones_row = sb("ones_row", [1, 128], F32)
    ones_col = sb("ones_col", [128, 1], F32)
    idb = sb("idb", [128, 128], BF16)
    idf = sb("idf", [128, 128], F32)
    trf = sb("trf", [128, 128], F32)
    trm = sb("trm", [128, 128], BF16)

    pg = co.prog()
    t_ld = []
    for dst, src in ((idb, identb), (idf, identf), (trf, triuf), (trm, trium)):
        t_ld.append(pg.dma("sync", dst[:], src[:]))
    tok_ones = pg.add("vector", lambda e: e.memset(ones_row[:], 1.0), inc="sV")
    tok_onesc = pg.add("vector", lambda e: e.memset(ones_col[:], 1.0), inc="sV")
    bar0 = pg.barrier()
    pg.emit()

    ln_consts = {}

    def ln_tile(pr, xt, xln_bf, cols, dummy, wait_x, xlnf=None, wait_buf=()):
        sx, sxx, m, biasc, r, nrm = (cols[:, i:i + 1] for i in range(6))
        t1 = pr.add("scalar", lambda e: e.activation(dummy[:], xt[:], AF.Copy, accum_out=sx),
                    waits=[wait_x] + list(wait_buf), inc="sC")
        t2 = pr.add("scalar", lambda e: e.activation(dummy[:], xt[:], AF.Square, accum_out=sxx),
                    waits=[wait_x], inc="sC")
        t3 = pr.add("vector", lambda e: e.tensor_scalar(m, sx, 1.0 / DM, None, ALU.mult),
                    waits=[t1], inc="sV")
        t4 = pr.add("scalar", lambda e: e.activation(biasc, m, AF.Square), waits=[t3], inc="sC")
        t5 = pr.add("vector", lambda e: e.tensor_scalar(biasc, biasc, -1.0, EPS, ALU.mult, ALU.add),
                    waits=[t4], inc="sV")
        t6 = pr.add("scalar", lambda e: e.activation(r, sxx, AF.Ln, bias=biasc, scale=1.0 / DM),
                    waits=[t2, t5], inc="sC")
        t7 = pr.add("scalar", lambda e: e.activation(r, r, AF.Exp, scale=-0.5), waits=[t6], inc="sC")
        t8 = pr.add("vector", lambda e: e.tensor_scalar(nrm, r, m, -1.0, ALU.mult, ALU.mult),
                    waits=[t7], inc="sV")
        tok = pr.add("scalar", lambda e: e.activation(xln_bf[:], xt[:], AF.Identity, bias=nrm, scale=r),
                     waits=[t8], inc="sC")
        tok2 = tok
        if xlnf is not None:
            tok2 = pr.add("scalar", lambda e: e.activation(xlnf[:], xt[:], AF.Identity, bias=nrm, scale=r),
                          waits=[t8], inc="sC")
        return tok, tok2, r, nrm

    def ln_stats(pr, xt, cols, rcol, ncol, scrap, wait_x):
        sx, sxx, m, biasc = (cols[:, i:i + 1] for i in range(4))
        t1 = pr.add("scalar", lambda e: e.activation(scrap[:], xt[:], AF.Copy, accum_out=sx),
                    waits=[wait_x], inc="sC")
        t2 = pr.add("scalar", lambda e: e.activation(scrap[:], xt[:], AF.Square, accum_out=sxx),
                    waits=[wait_x], inc="sC")
        t3 = pr.add("vector", lambda e: e.tensor_scalar(m, sx, 1.0 / DM, None, ALU.mult),
                    waits=[t1], inc="sV")
        t4 = pr.add("scalar", lambda e: e.activation(biasc, m, AF.Square), waits=[t3], inc="sC")
        t5 = pr.add("vector", lambda e: e.tensor_scalar(biasc, biasc, -1.0, EPS, ALU.mult, ALU.add),
                    waits=[t4], inc="sV")
        t6 = pr.add("scalar", lambda e: e.activation(rcol, sxx, AF.Ln, bias=biasc, scale=1.0 / DM),
                    waits=[t2, t5], inc="sC")
        t7 = pr.add("scalar", lambda e: e.activation(rcol, rcol, AF.Exp, scale=-0.5),
                    waits=[t6], inc="sC")
        t8 = pr.add("vector", lambda e: e.tensor_scalar(ncol, rcol, m, -1.0, ALU.mult, ALU.mult),
                    waits=[t7], inc="sV")
        return t8

    def transpose8(pr, src_bf, psT, dst_bf, wait_src, wait_dst_free, blocks=8, ident=None,
                   relax=False):
        idt = ident if ident is not None else idb
        def tfn(e):
            last = None
            for j in range(blocks):
                last = e.transpose(psT[:, j * 128:(j + 1) * 128], src_bf[:, j * 128:(j + 1) * 128], idt[:])
            return last
        tk = pr.add("tensor", tfn, waits=[wait_src, wait_dst_free], inc="sT")
        ev = pr.add("vector", lambda e: e.tensor_copy(dst_bf[:, 0:blocks * 128], psT[:, 0:blocks * 128]),
                    waits=[tk], inc="sV", relax=relax)
        return ev

    def ssm_phase(prev_bar):
        src, dst = io["ssm"]
        with contextlib.ExitStack() as ps:
            def S(name, shape, dtype):
                return ps.enter_context(nc.sbuf_tensor(name, shape, dtype))
            pcol = S("pcol", [128, 160], F32)
            AinvT_re = S("AinvT_re", [128, 512], F32)
            AinvT_im = S("AinvT_im", [128, 512], F32)
            Are = S("Are", [128, 512], F32)
            Aim = S("Aim", [128, 512], F32)
            AinvP_re = S("AinvP_re", [128, 512], F32)
            AinvP_im = S("AinvP_im", [128, 512], F32)
            BbT_re = S("BbT_re", [128, 8 * 512], BF16)
            BbT_im = S("BbT_im", [128, 8 * 512], BF16)
            CreT = S("CreT", [128, 4 * 1024], BF16)
            CimNT = S("CimNT", [128, 4 * 1024], BF16)
            Dbc = S("Dbc", [128, 1024], F32)
            braw = S("braw", [128, 2048], F32)
            bstg_re = S("bstg_re", [128, 1024], BF16)
            bstg_im = S("bstg_im", [128, 1024], BF16)
            xinb = [S(f"xin{i}", [128, 1024], F32) for i in range(2)]
            xoutb = [S(f"xout{i}", [128, 1024], F32) for i in range(2)]
            xln = S("xln", [128, 1024], BF16)
            xlnf2 = [S(f"xlnf{i}", [128, 1024], F32) for i in range(2)]
            xlnT = S("xlnT", [128, 1024], BF16)
            dummy = S("scrap", [128, 1024], F32)
            lncols = S("lncols", [128, 8], F32)
            vre2 = [S(f"vre{i}", [128, 512], F32) for i in range(2)]
            vim2 = [S(f"vim{i}", [128, 512], F32) for i in range(2)]
            tmp1 = S("tmp1", [128, 512], F32)
            tmp2 = S("tmp2", [128, 512], F32)
            tmp3 = S("tmp3", [128, 512], F32)
            tmp4 = S("tmp4", [128, 512], F32)
            xs_re = S("xs_re", [128, 512], BF16)
            xs_im = S("xs_im", [128, 512], BF16)
            du2 = [S(f"du{i}", [128, 1024], F32) for i in range(2)]
            psT = ps.enter_context(nc.psum_tensor("psT", [128, 1024], BF16))
            psBUre = ps.enter_context(nc.psum_tensor("psBUre", [128, 512], F32))
            psBUim = ps.enter_context(nc.psum_tensor("psBUim", [128, 512], F32))
            psSre = ps.enter_context(nc.psum_tensor("psSre", [128, 512], F32))
            psSim = ps.enter_context(nc.psum_tensor("psSim", [128, 512], F32))
            psYS = ps.enter_context(nc.psum_tensor("psYS", [128, 1024], F32))

            c = {}
            for i, nm in enumerate(("Lre", "Lim", "lst", "lr", "stp", "th", "sh2", "cth",
                                    "sth", "em", "er", "are", "aim", "nre", "ml", "inv",
                                    "fre", "fim", "ainvre", "ainvim", "pwre", "pwim",
                                    "a128re", "a128im", "acre", "acim", "t1", "t2",
                                    "t3", "t4", "t5", "t6", "q1", "q2", "q3")):
                c[nm] = pcol[:, 4 * i:4 * i + 4]

            pr = co.prog()

            w0 = pr.dma("sync", c["Lre"], _col_ap(lam_re, 512), waits=[prev_bar], slow_ok=True, sem="dP")
            w1 = pr.dma("sync", c["Lim"], _col_ap(lam_im, 512), waits=[prev_bar], slow_ok=True, sem="dP")
            w2 = pr.dma("sync", c["lst"], _col_ap(lstep, 512), waits=[prev_bar], slow_ok=True, sem="dP")
            drow_t = pr.dma("sync", xinb[0][0:1, 0:1024], _row_ap(Dw, 1024), waits=[prev_bar], sem="dP")
            dp_all = drow_t
            w0 = w1 = w2 = drow_t = dp_all

            def V(fn, waits, fence=False, relax=False):
                return pr.add("vector", fn, waits=waits, inc="sV", fence=fence, relax=relax)

            def Cc(fn, waits):
                return pr.add("scalar", fn, waits=waits, inc="sC")

            t = V(lambda e: e.tensor_scalar_min(c["lr"], c["Lre"], -1e-4), [w0])
            tstp = Cc(lambda e: e.activation(c["stp"], c["lst"], AF.Exp), [w2])
            tth = V(lambda e: e.tensor_mul(c["th"], c["Lim"], c["stp"]), [w1, tstp])
            trr = V(lambda e: e.tensor_mul(c["t1"], c["lr"], c["stp"]), [t, tstp])
            tsh = Cc(lambda e: e.activation(c["sh2"], c["th"], AF.Sin, scale=0.5), [tth])
            tsth = Cc(lambda e: e.activation(c["sth"], c["th"], AF.Sin), [tth])
            tsq = Cc(lambda e: e.activation(c["sh2"], c["sh2"], AF.Square), [tsh])
            tcth = V(lambda e: e.tensor_scalar(c["cth"], c["sh2"], -2.0, 1.0, ALU.mult, ALU.add), [tsq])
            fact = [40320.0, 5040.0, 720.0, 120.0, 24.0, 6.0, 2.0, 1.0]
            tt = V(lambda e: e.tensor_scalar(c["em"], c["t1"], 1.0 / fact[0], 1.0 / fact[1],
                                             ALU.mult, ALU.add), [trr])
            for kf in range(2, 8):
                tt = V(lambda e: e.tensor_mul(c["em"], c["em"], c["t1"]), [tt])
                tt = V(lambda e, k=kf: e.tensor_scalar(c["em"], c["em"], 1.0 / fact[k], None, ALU.add), [tt])
            tt = V(lambda e: e.tensor_mul(c["em"], c["em"], c["t1"]), [tt])
            ter = V(lambda e: e.tensor_scalar(c["er"], c["em"], 1.0, None, ALU.add), [tt])
            tare = V(lambda e: e.tensor_mul(c["are"], c["er"], c["cth"]), [ter, tcth])
            taim = V(lambda e: e.tensor_mul(c["aim"], c["er"], c["sth"]), [ter, tsth])
            tn1 = V(lambda e: e.tensor_mul(c["nre"], c["em"], c["cth"]), [tt, tcth])
            tn2 = V(lambda e: e.tensor_scalar(c["t2"], c["sh2"], -2.0, None, ALU.mult), [tsq])
            tnre = V(lambda e: e.tensor_add(c["nre"], c["nre"], c["t2"]), [tn1, tn2])
            tl1 = Cc(lambda e: e.activation(c["q1"], c["lr"], AF.Square), [t])
            tl2 = Cc(lambda e: e.activation(c["q2"], c["Lim"], AF.Square), [w1])
            tl3 = V(lambda e: e.tensor_add(c["ml"], c["q1"], c["q2"]), [tl1, tl2])
            tl4 = Cc(lambda e: e.activation(c["ml"], c["ml"], AF.Ln), [tl3])
            tinv = Cc(lambda e: e.activation(c["inv"], c["ml"], AF.Exp, scale=-1.0), [tl4])
            tf1 = V(lambda e: e.tensor_mul(c["t3"], c["nre"], c["lr"]), [tnre])
            tf2 = V(lambda e: e.tensor_mul(c["t4"], c["aim"], c["Lim"]), [taim])
            tf3 = V(lambda e: e.tensor_add(c["t3"], c["t3"], c["t4"]), [tf1, tf2])
            tfre = V(lambda e: e.tensor_mul(c["fre"], c["t3"], c["inv"]), [tf3, tinv])
            tf4 = V(lambda e: e.tensor_mul(c["t3"], c["aim"], c["lr"]), [taim])
            tf5 = V(lambda e: e.tensor_mul(c["t4"], c["nre"], c["Lim"]), [tnre])
            tf6 = V(lambda e: e.tensor_sub(c["t3"], c["t3"], c["t4"]), [tf4, tf5])
            tfim = V(lambda e: e.tensor_mul(c["fim"], c["t3"], c["inv"]), [tf6, tinv])
            ta1 = Cc(lambda e: e.activation(c["q1"], c["are"], AF.Square), [tare, tl3])
            ta2 = Cc(lambda e: e.activation(c["q2"], c["aim"], AF.Square), [taim, tl3])
            ta3 = V(lambda e: e.tensor_add(c["q3"], c["q1"], c["q2"]), [ta1, ta2])
            ta4 = Cc(lambda e: e.activation(c["q3"], c["q3"], AF.Ln), [ta3])
            ta5 = Cc(lambda e: e.activation(c["q3"], c["q3"], AF.Exp, scale=-1.0), [ta4])
            tai1 = V(lambda e: e.tensor_mul(c["ainvre"], c["are"], c["q3"]), [ta5, tare])
            tai2 = V(lambda e: e.tensor_scalar(c["t6"], c["q3"], -1.0, None, ALU.mult), [ta5])
            tai3 = V(lambda e: e.tensor_mul(c["ainvim"], c["aim"], c["t6"]), [tai2, taim])

            def build_table(Tre, Tim, base_re, base_im, waits):
                toks = []
                tz1 = V(lambda e: e.memset(Tre[:, 0:512:128], 1.0), waits)
                tz2 = V(lambda e: e.memset(Tim[:, 0:512:128], 0.0), waits)
                tp1 = V(lambda e: e.tensor_copy(c["pwre"], base_re), waits)
                tp2 = V(lambda e: e.tensor_copy(c["pwim"], base_im), waits)
                last = [tz1, tz2, tp1, tp2]
                for k in range(7):
                    w = 1 << k
                    nl = []
                    for cc in range(4):
                        o = cc * 128
                        pr_ = c["pwre"][:, cc:cc + 1]
                        pi_ = c["pwim"][:, cc:cc + 1]
                        fc = w < 8
                        a1 = V(lambda e, o=o, w=w, pr_=pr_: e.tensor_scalar(
                            tmp1[:, 0:w], Tre[:, o:o + w], pr_, None, ALU.mult), last, fence=fc)
                        a2 = V(lambda e, o=o, w=w, pi_=pi_: e.tensor_scalar(
                            tmp2[:, 0:w], Tim[:, o:o + w], pi_, None, ALU.mult), last, fence=fc)
                        a3 = V(lambda e, o=o, w=w: e.tensor_sub(
                            Tre[:, o + w:o + 2 * w], tmp1[:, 0:w], tmp2[:, 0:w]), [a1, a2], fence=fc)
                        a4 = V(lambda e, o=o, w=w, pi_=pi_: e.tensor_scalar(
                            tmp1[:, 0:w], Tre[:, o:o + w], pi_, None, ALU.mult), [a3], fence=fc)
                        a5 = V(lambda e, o=o, w=w, pr_=pr_: e.tensor_scalar(
                            tmp2[:, 0:w], Tim[:, o:o + w], pr_, None, ALU.mult), [a4], fence=fc)
                        a6 = V(lambda e, o=o, w=w: e.tensor_add(
                            Tim[:, o + w:o + 2 * w], tmp1[:, 0:w], tmp2[:, 0:w]), [a5], fence=fc)
                        nl.append(a6)
                    s1 = V(lambda e: e.tensor_mul(c["t1"], c["pwre"], c["pwre"]), nl, fence=True)
                    s2 = V(lambda e: e.tensor_mul(c["t2"], c["pwim"], c["pwim"]), nl)
                    s3 = V(lambda e: e.tensor_mul(c["t3"], c["pwre"], c["pwim"]), nl)
                    s4 = V(lambda e: e.tensor_sub(c["pwre"], c["t1"], c["t2"]), [s1, s2], fence=True)
                    s5 = V(lambda e: e.tensor_scalar(c["pwim"], c["t3"], 2.0, None, ALU.mult), [s3, s4], fence=True)
                    last = [s4, s5]
                return last

            tA = build_table(Are, Aim, c["are"], c["aim"], [tare, taim, tnre, tfre, tfim])
            tA2 = V(lambda e: e.tensor_copy(c["a128re"], c["pwre"]), tA)
            tA3 = V(lambda e: e.tensor_copy(c["a128im"], c["pwim"]), tA)
            tI = build_table(AinvP_re, AinvP_im, c["ainvre"], c["ainvim"],
                             [tai1, tai3, tA2, tA3])
            tIT = []
            prev_psf = None
            for part, (srcT, dstT) in enumerate(((AinvP_re, AinvT_re), (AinvP_im, AinvT_im))):
                psf = psYS
                def tfn(e, srcT=srcT):
                    last = None
                    for cc in range(4):
                        last = e.transpose(psf[:, cc * 128:(cc + 1) * 128],
                                           srcT[:, cc * 128:(cc + 1) * 128], idf[:])
                    return last
                tk = pr.add("tensor", tfn, waits=tI + ([prev_psf] if prev_psf else []), inc="sT")
                ev = V(lambda e, dstT=dstT: e.tensor_copy(dstT[:], psf[:, 0:512]), [tk])
                prev_psf = ev
                tIT.append(ev)

            bb_done = []
            prev_ev = tIT
            prev_braw_rd = None
            for j in range(4):
                lb = pr.dma("sync", braw[:], Bw.rearrange("p h c -> p (h c)")[j * 128:(j + 1) * 128, :],
                            waits=[prev_bar] + ([prev_braw_rd] if prev_braw_rd else []), sem="dX0")
                d1 = V(lambda e: e.tensor_copy(tmp1[:, 0:512], braw[:, 0:1024:2]), [lb])
                d2 = V(lambda e: e.tensor_copy(tmp2[:, 0:512], braw[:, 1:1024:2]), [lb])
                d3 = V(lambda e: e.tensor_copy(tmp3[:, 0:512], braw[:, 1024:2048:2]), [lb])
                d4 = V(lambda e: e.tensor_copy(tmp4[:, 0:512], braw[:, 1025:2048:2]), [lb])
                prev_braw_rd = d4
                fre_j = c["fre"][:, j:j + 1]
                fim_j = c["fim"][:, j:j + 1]
                e1 = V(lambda e, f=fre_j: e.tensor_scalar(dummy[:, 0:512], tmp1[:, 0:512], f, None, ALU.mult), [d1])
                e2 = V(lambda e, f=fim_j: e.tensor_scalar(du2[0][:, 0:512], tmp2[:, 0:512], f, None, ALU.mult), [d2])
                e3 = V(lambda e: e.tensor_sub(bstg_re[:, 0:512], dummy[:, 0:512], du2[0][:, 0:512]), [e1, e2])
                e4 = V(lambda e, f=fre_j: e.tensor_scalar(dummy[:, 0:512], tmp3[:, 0:512], f, None, ALU.mult), [d3])
                e5 = V(lambda e, f=fim_j: e.tensor_scalar(du2[0][:, 0:512], tmp4[:, 0:512], f, None, ALU.mult), [d4])
                e6 = V(lambda e: e.tensor_sub(bstg_re[:, 512:1024], dummy[:, 0:512], du2[0][:, 0:512]), [e4, e5])
                e7 = V(lambda e, f=fim_j: e.tensor_scalar(dummy[:, 0:512], tmp1[:, 0:512], f, None, ALU.mult), [e3])
                e8 = V(lambda e, f=fre_j: e.tensor_scalar(du2[0][:, 0:512], tmp2[:, 0:512], f, None, ALU.mult), [e3])
                e9 = V(lambda e: e.tensor_add(bstg_im[:, 0:512], dummy[:, 0:512], du2[0][:, 0:512]), [e7, e8])
                ea = V(lambda e, f=fim_j: e.tensor_scalar(dummy[:, 0:512], tmp3[:, 0:512], f, None, ALU.mult), [e6])
                eb = V(lambda e, f=fre_j: e.tensor_scalar(du2[0][:, 0:512], tmp4[:, 0:512], f, None, ALU.mult), [e6])
                ec = V(lambda e: e.tensor_add(bstg_im[:, 512:1024], dummy[:, 0:512], du2[0][:, 0:512]), [e9, ea, eb])
                for (stg, dstT, wsrc) in ((bstg_re, BbT_re, e6), (bstg_im, BbT_im, ec)):
                    def tfn(e, stg=stg):
                        last = None
                        for k in range(8):
                            last = e.transpose(psT[:, k * 128:(k + 1) * 128],
                                               stg[:, k * 128:(k + 1) * 128], idb[:])
                        return last
                    tk = pr.add("tensor", tfn, waits=[wsrc, ec] + prev_ev, inc="sT")
                    def evfn(e, dstT=dstT, j=j):
                        return e.tensor_copy(
                            dstT[:, :].rearrange("p (k q) -> p k q", k=8)[:, :, j * 128:(j + 1) * 128],
                            psT[:, 0:1024].rearrange("p (k q) -> p k q", k=8))
                    ev = V(evfn, [tk])
                    prev_ev = [ev]
                    bb_done.append(ev)

            for k in range(8):
                lc = pr.dma("sync", braw[:, 0:1024], Cw.rearrange("h p c -> h (p c)")[k * 128:(k + 1) * 128, :],
                            waits=[prev_bar, prev_braw_rd], sem="dX0")
                d1 = V(lambda e: e.tensor_copy(bstg_re[:, 0:512], braw[:, 0:1024:2]), [lc] + prev_ev)
                d2 = V(lambda e: e.tensor_scalar(bstg_im[:, 0:512], braw[:, 1:1024:2], -1.0, None, ALU.mult),
                       [lc])
                prev_braw_rd = d2
                for (stg, dstT, wsrc) in ((bstg_re, CreT, d1), (bstg_im, CimNT, d2)):
                    def tfn(e, stg=stg):
                        last = None
                        for j in range(4):
                            last = e.transpose(psT[:, j * 128:(j + 1) * 128],
                                               stg[:, j * 128:(j + 1) * 128], idb[:])
                        return last
                    tk = pr.add("tensor", tfn, waits=[wsrc, d1, d2] + prev_ev, inc="sT")
                    def evfn(e, dstT=dstT, k=k):
                        return e.tensor_copy(
                            dstT[:, :].rearrange("p (j q) -> p j q", j=4)[:, :, k * 128:(k + 1) * 128],
                            psT[:, 0:512].rearrange("p (j q) -> p j q", j=4))
                    ev = V(evfn, [tk])
                    prev_ev = [ev]

            tdm = pr.add("tensor", lambda e: (
                e.matmul(psBUre[:], ones_row[:], xinb[0][0:1, 0:512], start=True, stop=True),
                e.matmul(psBUim[:], ones_row[:], xinb[0][0:1, 512:1024], start=True, stop=True))[-1],
                waits=[drow_t, tok_ones], inc="sT")
            tdv1 = V(lambda e: e.tensor_copy(Dbc[:, 0:512], psBUre[:]), [tdm])
            tdv2 = V(lambda e: e.tensor_copy(Dbc[:, 512:1024], psBUim[:]), [tdm])

            prologue_done = pr.barrier()

            for b in range(B_PER_CORE):
                zc1 = V(lambda e: e.memset(c["acre"], 0.0), [prologue_done])
                zc2 = V(lambda e: e.memset(c["acim"], 0.0), [prologue_done])
                stt = {
                    "free_xin": [prologue_done, prologue_done],
                    "free_xout": [prologue_done, prologue_done],
                    "xlnT_free": prologue_done,
                    "bu_free": prologue_done,
                    "s_free": prologue_done,
                    "ys_free": [tdv1, tdv2],
                    "xs_free": prologue_done,
                    "carry": [zc1, zc2],
                    "xln_free": [prologue_done],
                    "prev_o1": prologue_done,
                }
                A = {}

                def stageA1(t, b=b):
                    g = b * TPB + t
                    bi = g % 2
                    m = t % 2
                    xt = xinb[bi]
                    ld = pr.dma("sync", xt[:], src[g * 128:(g + 1) * 128, :],
                                waits=[stt["free_xin"][bi]], sem=f"dX{bi}")
                    tok_xln, tok_xlnf, _, _ = ln_tile(pr, xt, xln, lncols, dummy, ld,
                                                      xlnf=xlnf2[m], wait_buf=stt["xln_free"])
                    A[("ln", t)] = (tok_xln, tok_xlnf)
                    A[("xt", t)] = (xt, ld)
                    A[("bi", t)] = bi

                def stageA2(t, b=b):
                    m = t % 2
                    tok_xln, tok_xlnf = A[("ln", t)]
                    ev_xlnT = transpose8(pr, xln, psT, xlnT, tok_xln, stt["xlnT_free"],
                                         relax=True)

                    def bufn(e):
                        last = None
                        for j in range(8):
                            last = e.matmul(psBUre[:], xlnT[:, j * 128:(j + 1) * 128],
                                            BbT_re[:, j * 512:(j + 1) * 512],
                                            start=(j == 0), stop=(j == 7))
                        for j in range(8):
                            last = e.matmul(psBUim[:], xlnT[:, j * 128:(j + 1) * 128],
                                            BbT_im[:, j * 512:(j + 1) * 512],
                                            start=(j == 0), stop=(j == 7))
                        return last
                    t_bu = pr.add("tensor", bufn, waits=[ev_xlnT, stt["bu_free"]], inc="sT")
                    v1 = V(lambda e: e.tensor_mul(tmp1[:], AinvT_re[:], psBUre[:]), [t_bu], relax=True)
                    v2 = V(lambda e: e.tensor_mul(tmp2[:], AinvT_im[:], psBUim[:]), [t_bu], relax=True)
                    v3 = V(lambda e, m=m: e.tensor_sub(vre2[m][:], tmp1[:], tmp2[:]),
                           [v1, v2, A.get(("sc", t - 2))], relax=True)
                    v4 = V(lambda e: e.tensor_mul(tmp1[:], AinvT_im[:], psBUre[:]), [v3], relax=True)
                    v5 = V(lambda e: e.tensor_mul(tmp2[:], AinvT_re[:], psBUim[:]), [v3], relax=True)
                    v6 = V(lambda e, m=m: e.tensor_add(vim2[m][:], tmp1[:], tmp2[:]), [v4, v5], relax=True)
                    stt["bu_free"] = v6
                    g1 = pr.add("gpsimd", lambda e, m=m: e.tensor_tensor(
                        du2[m][:], Dbc[:], xlnf2[m][:], ALU.mult),
                        waits=[tok_xlnf, stt["prev_o1"]], inc="sG")
                    stt["prev_o1"] = g1
                    stt["xlnT_free"] = t_bu
                    stt["xln_free"] = [ev_xlnT]
                    A[("v", t)] = (v3, v6)
                    A[("g1", t)] = g1

                def stageB(t, b=b):
                    g = b * TPB + t
                    m = t % 2
                    bi = A[("bi", t)]
                    xt, ld = A[("xt", t)]
                    xo = xoutb[bi]
                    v3, v6 = A[("v", t)]
                    g1 = A[("g1", t)]

                    def scfn(e, m=m):
                        last = None
                        for cc in range(4):
                            last = e.matmul(psSre[:, cc * 128:(cc + 1) * 128],
                                            vre2[m][:, cc * 128:(cc + 1) * 128], trf[:],
                                            start=True, stop=True)
                        for cc in range(4):
                            last = e.matmul(psSim[:, cc * 128:(cc + 1) * 128],
                                            vim2[m][:, cc * 128:(cc + 1) * 128], trf[:],
                                            start=True, stop=True)
                        return last
                    t_sc = pr.add("tensor", scfn, waits=[v3, v6, stt["s_free"]], inc="sT")
                    A[("sc", t)] = t_sc
                    rs = []
                    for cc in range(4):
                        o = cc * 128
                        acr = c["acre"][:, cc:cc + 1]
                        aci = c["acim"][:, cc:cc + 1]
                        rs.append(V(lambda e, o=o, acr=acr: e.scalar_tensor_tensor(
                            tmp1[:, o:o + 128], psSre[:, o:o + 128], acr, Are[:, o:o + 128],
                            ALU.add, ALU.mult), [t_sc, stt["xs_free"]] + stt["carry"], relax=True))
                        rs.append(V(lambda e, o=o, aci=aci: e.scalar_tensor_tensor(
                            tmp2[:, o:o + 128], psSim[:, o:o + 128], aci, Aim[:, o:o + 128],
                            ALU.add, ALU.mult), [t_sc] + stt["carry"], relax=True))
                        rs.append(V(lambda e, o=o, acr=acr: e.scalar_tensor_tensor(
                            tmp3[:, o:o + 128], psSre[:, o:o + 128], acr, Aim[:, o:o + 128],
                            ALU.add, ALU.mult), [t_sc] + stt["carry"], relax=True))
                        rs.append(V(lambda e, o=o, aci=aci: e.scalar_tensor_tensor(
                            tmp4[:, o:o + 128], psSim[:, o:o + 128], aci, Are[:, o:o + 128],
                            ALU.add, ALU.mult), [t_sc] + stt["carry"], relax=True))
                    x1 = V(lambda e: e.tensor_sub(xs_re[:], tmp1[:], tmp2[:]), rs, relax=True)
                    x2 = V(lambda e: e.tensor_add(xs_im[:], tmp3[:], tmp4[:]), rs, relax=True)
                    u1 = V(lambda e: e.tensor_add(c["t1"], psSre[:, 127:512:128], c["acre"]), rs)
                    u2 = V(lambda e: e.tensor_add(c["t2"], psSim[:, 127:512:128], c["acim"]), rs)
                    u3 = V(lambda e: e.tensor_mul(c["t3"], c["a128re"], c["t1"]), [u1])
                    u4 = V(lambda e: e.tensor_mul(c["t4"], c["a128im"], c["t2"]), [u2])
                    u5 = V(lambda e: e.tensor_sub(c["acre"], c["t3"], c["t4"]), [u3, u4])
                    u6 = V(lambda e: e.tensor_mul(c["t3"], c["a128im"], c["t1"]), [u5])
                    u7 = V(lambda e: e.tensor_mul(c["t4"], c["a128re"], c["t2"]), [u5])
                    u8 = V(lambda e: e.tensor_add(c["acim"], c["t3"], c["t4"]), [u6, u7])
                    stt["carry"] = [u5, u8]
                    stt["s_free"] = u8

                    def ysfn(e):
                        last = None
                        for half in range(2):
                            for cc in range(4):
                                last = e.matmul(psYS[:, half * 512:(half + 1) * 512],
                                                xs_re[:, cc * 128:(cc + 1) * 128],
                                                CreT[:, cc * 1024 + half * 512: cc * 1024 + (half + 1) * 512],
                                                start=(cc == 0), stop=False)
                            for cc in range(4):
                                last = e.matmul(psYS[:, half * 512:(half + 1) * 512],
                                                xs_im[:, cc * 128:(cc + 1) * 128],
                                                CimNT[:, cc * 1024 + half * 512: cc * 1024 + (half + 1) * 512],
                                                start=False, stop=(cc == 3))
                        return last
                    t_ys = pr.add("tensor", ysfn, waits=[x1, x2] + stt["ys_free"], inc="sT")
                    stt["xs_free"] = t_ys
                    o1 = V(lambda e, xo=xo, m=m: e.scalar_tensor_tensor(
                        xo[:], psYS[:], 1.0, du2[m][:], ALU.mult, ALU.add),
                        [t_ys, g1, stt["free_xout"][bi]], relax=True)
                    o2 = V(lambda e, xo=xo, xt=xt: e.tensor_add(xo[:], xo[:], xt[:]), [o1])
                    stt["ys_free"] = [o2]
                    stt["free_xin"][bi] = o2
                    st = pr.dma("sync", dst[g * 128:(g + 1) * 128, :], xo[:], waits=[o2],
                                sem=f"dY{bi}")
                    stt["free_xout"][bi] = st

                stageA1(0)
                stageA2(0)
                for t in range(TPB):
                    if t + 1 < TPB:
                        stageA1(t + 1)
                    stageB(t)
                    if t + 1 < TPB:
                        stageA2(t + 1)

            pr.add("sync", lambda e: e.nop(),
                   waits=[("x", s, co.counts[s]) for s in
                          ("dL", "dG", "dP", "dX0", "dX1", "dX2", "dX3", "dY0", "dY1")
                          if co.counts[s] > 0])
            endbar = pr.barrier()
            pr.emit()
            return endbar

    def ffn_phase(prev_bar, which):
        src, dst = io[which]
        SG = (1, 1, -1, -1, -1, -1, 1, 1)

        with contextlib.ExitStack() as ps:
            def S(name, shape, dtype):
                return ps.enter_context(nc.sbuf_tensor(which + name, shape, dtype))
            w1b = S("w1b", [128, 8 * 1024], BF16)
            w2b = S("w2b", [128, 8 * 512], BF16)
            w3b = S("w3b", [128, 4 * 256], BF16)
            w4c = S("w4c", [128, 2], BF16)
            gw1b = S("gw1b", [128, 8 * 512], BF16)
            gw2b = S("gw2b", [128, 4 * 256], BF16)
            gw3c = S("gw3c", [128, 2], BF16)
            b1c = S("b1c", [128, 8], F32)
            b2c = S("b2c", [128, 4], F32)
            b3c = S("b3c", [128, 2], F32)
            gb1c = S("gb1c", [128, 4], F32)
            gb2c = S("gb2c", [128, 2], F32)
            rows = S("rows", [1, 16], F32)
            b4h = rows[0:1, 0:1]; gb3h = rows[0:1, 1:2]; arr = rows[0:1, 2:3]
            c1s = rows[0:1, 3:4]; c0s = rows[0:1, 4:5]; gfr = rows[0:1, 5:6]
            rt1 = rows[0:1, 6:7]; rt2 = rows[0:1, 7:8]; snrm = rows[0:1, 8:9]
            rstore = S("rstore", [128, 32], F32)
            cols = S("cols", [128, 8], F32)
            scrap = S("scrap2", [128, 1024], F32)
            sa = [S(f"sta{i}", [128, 8 * 512], F32) for i in range(2)]
            sb = [S(f"stb{i}", [128, 8 * 512], F32) for i in range(2)]
            sB = [S(f"stB{i}", [128, 8 * 512], BF16) for i in range(2)]
            gh1 = S("gh1", [128, 8 * 512], BF16)
            gh2 = S("gh2", [128, 4 * 512], BF16)
            gh3 = S("gh3", [128, 2 * 512], BF16)
            cb = S("cb", [128, 512], F32)
            twt = S("twt", [128, 512], F32)
            xs2 = [S(f"xs2{i}", [128, 1024], F32) for i in range(2)]
            xr = [S(f"xr{i}", [128, 1024], F32) for i in range(2)]
            xio = [S(f"xio{i}", [128, 1024], F32) for i in range(2)]
            xlnf = S("fxlnf", [128, 1024], F32)
            mrow = S("mrow", [1, 1024], F32)
            mcol = S("mcol", [128, 8], F32)
            mcolb = S("mcolb", [128, 8], BF16)
            gcol1 = S("gcol1", [128, 4], F32)
            gcol1b = S("gcol1b", [128, 4], BF16)
            gcol2b = S("gcol2b", [128, 2], BF16)
            trow = S("trow", [1, 512], F32)
            grow = S("grow", [1, 512], F32)
            psH = [ps.enter_context(nc.psum_tensor(which + f"psH{i}", [128, 512], F32))
                   for i in range(2)]
            psTf = ps.enter_context(nc.psum_tensor(which + "psTf", [1, 512], F32))
            psCB = ps.enter_context(nc.psum_tensor(which + "psCB", [128, 512], F32))
            psTr = ps.enter_context(nc.psum_tensor(which + "psTr", [128, 1024], F32))
            psG1 = ps.enter_context(nc.psum_tensor(which + "psG1", [1, 512], F32))
            psG2 = ps.enter_context(nc.psum_tensor(which + "psG2", [1, 512], F32))

            pr = co.prog()

            def V(fn, waits, relax=False):
                return pr.add("vector", fn, waits=waits, inc="sV", relax=relax)

            def Cc(fn, waits):
                return pr.add("scalar", fn, waits=waits, inc="sC")

            def T(fn, waits):
                return pr.add("tensor", fn, waits=waits, inc="sT")

            def G(fn, waits):
                return pr.add("gpsimd", fn, waits=waits, inc="sG")

            wl_raw = []
            wl = wl_raw
            wl.append(pr.dma("gpsimd", _wslices_dst(w1b, 8, DM), _wslices_ap(cn_w1, DM, DM), waits=[prev_bar]))
            wl.append(pr.dma("gpsimd", _wslices_dst(w2b, 8, 512), _wslices_ap(cn_w2, DM, 512), waits=[prev_bar]))
            wl.append(pr.dma("gpsimd", _wslices_dst(w3b, 4, 256), _wslices_ap(cn_w3, 512, 256), waits=[prev_bar]))
            wl.append(pr.dma("gpsimd", w4c[:], _col_ap(cn_w4, 256), waits=[prev_bar], slow_ok=True))
            wl.append(pr.dma("gpsimd", _wslices_dst(gw1b, 8, 512), _wslices_ap(gc_w1, DM, 512), waits=[prev_bar]))
            wl.append(pr.dma("gpsimd", _wslices_dst(gw2b, 4, 256), _wslices_ap(gc_w2, 512, 256), waits=[prev_bar]))
            wl.append(pr.dma("gpsimd", gw3c[:], _col_ap(gc_w3, 256), waits=[prev_bar], slow_ok=True))
            wl = [wl_raw[-1]]
            bl = []
            for dstb, srcb, n in ((b1c, cn_b1, DM), (b2c, cn_b2, 512), (b3c, cn_b3, 256),
                                  (gb1c, gc_b1, 512), (gb2c, gc_b2, 256)):
                bl.append(pr.dma("sync", dstb[:], _col_ap(srcb, n), waits=[prev_bar],
                                 slow_ok=True, sem="dP"))
            lb4 = pr.dma("sync", b4h[:], _row_ap(cn_b4, 1), waits=[prev_bar], slow_ok=True, sem="dP")
            lg3 = pr.dma("sync", gb3h[:], _row_ap(gc_b3, 1), waits=[prev_bar], slow_ok=True, sem="dP")
            lar = pr.dma("sync", arr[:], _row_ap(a_range, 1), waits=[prev_bar], slow_ok=True, sem="dP")
            dp_all = lar
            bl = [dp_all]
            h1 = V(lambda e: e.tensor_scalar(b4h, b4h, 0.5, None, ALU.mult), [dp_all])
            h2 = V(lambda e: e.tensor_scalar(gb3h, gb3h, 0.5, None, ALU.mult), [dp_all])
            h3 = V(lambda e: e.tensor_scalar(c1s, arr, 0.3, None, ALU.mult), [dp_all])

            tok_out_final = []
            for b in range(B_PER_CORE):
                bstart = tok_out_final[-1] if tok_out_final else prev_bar
                free_x = [bstart, bstart]
                t_cs_last = None
                st_toks = []
                for t in range(TPB):
                    g = b * TPB + t
                    bi = t % 2
                    xt = xio[bi]
                    ld = pr.dma("sync", xt[:], src[g * 128:(g + 1) * 128, :], waits=[free_x[bi]],
                                sem=f"dX{bi}")
                    rcol = rstore[:, 2 * t:2 * t + 1]
                    ncol = rstore[:, 2 * t + 1:2 * t + 2]
                    s_tok = ln_stats(pr, xt, cols, rcol, ncol, scrap, ld)
                    def csfn(e, xt=xt, rcol=rcol, ncol=ncol, t=t):
                        e.matmul(psG1[:], rcol, xt[:, 0:512], start=(t == 0), stop=(t == TPB - 1))
                        e.matmul(psG2[:], rcol, xt[:, 512:1024], start=(t == 0), stop=(t == TPB - 1))
                        return e.matmul(psTf[0:1, 0:1], ncol, ones_col[:, 0:1],
                                        start=(t == 0), stop=(t == TPB - 1))
                    t_cs = T(csfn, [s_tok, prev_bar if t < 2 else None,
                                    tok_out_final[-1] if (t < 2 and tok_out_final) else None])
                    free_x[bi] = t_cs
                    t_cs_last = t_cs
                m1 = V(lambda e: e.tensor_copy(snrm, psTf[0:1, 0:1]), [t_cs_last])
                m2 = V(lambda e: e.tensor_scalar(mrow[0:1, 0:512], psG1[:], snrm, 1.0 / TSEQ,
                                                 ALU.add, ALU.mult), [m1])
                m3 = V(lambda e: e.tensor_scalar(mrow[0:1, 512:1024], psG2[:], snrm, 1.0 / TSEQ,
                                                 ALU.add, ALU.mult), [m1])
                m4a = pr.dma("sync", rcscr[0:1, :], mrow[0:1, :], waits=[m2, m3], sem="dP")
                m4 = pr.dma("sync", mcol[:], _col_ap(rcscr, DM), waits=[m4a], slow_ok=True, sem="dP")
                gfb0 = pr.barrier()
                m5 = V(lambda e: e.tensor_copy(mcolb[:], mcol[:]), [m4, gfb0])
                def g1fn(e):
                    last = None
                    for j in range(4):
                        for k in range(8):
                            last = e.matmul(psH[0][:, j:j + 1],
                                            gw1b[:, k * 512 + j * 128:k * 512 + (j + 1) * 128],
                                            mcolb[:, k:k + 1], start=(k == 0), stop=(k == 7))
                    return last
                tg1 = T(g1fn, [m5] + wl)
                ge1 = []
                for j in range(4):
                    ge1.append(Cc(lambda e, j=j: e.activation(gcol1[:, j:j + 1], psH[0][:, j:j + 1],
                                                              AF.Gelu, bias=gb1c[:, j:j + 1]),
                                  [tg1] + bl))
                gc1 = V(lambda e: e.tensor_copy(gcol1b[:], gcol1[:]), ge1)
                def g2fn(e):
                    last = None
                    for j in range(2):
                        for k in range(4):
                            last = e.matmul(psH[1][:, j:j + 1],
                                            gw2b[:, k * 256 + j * 128:k * 256 + (j + 1) * 128],
                                            gcol1b[:, k:k + 1], start=(k == 0), stop=(k == 3))
                    return last
                tg2 = T(g2fn, [gc1])
                ge2 = []
                for j in range(2):
                    ge2.append(Cc(lambda e, j=j: e.activation(scrap[:, j:j + 1], psH[1][:, j:j + 1],
                                                              AF.Gelu, bias=gb2c[:, j:j + 1]), [tg2]))
                gc2 = V(lambda e: e.tensor_copy(gcol2b[:], scrap[:, 0:2]), ge2)
                def g3fn(e):
                    last = None
                    for k in range(2):
                        last = e.matmul(psTf[0:1, 1:2], gw3c[:, k:k + 1], gcol2b[:, k:k + 1],
                                        start=(k == 0), stop=(k == 1))
                    return last
                tg3 = T(g3fn, [gc2])
                dpre = V(lambda e: e.tensor_copy(rows[0:1, 9:10], psTf[0:1, 1:2]), [tg3])
                tgf = Cc(lambda e: e.activation(gfr, psTf[0:1, 1:2], AF.Tanh, scale=0.5,
                                                bias=gb3h), [tg3, h2, dpre])
                x1_ = V(lambda e: e.tensor_scalar(gfr, gfr, 0.5, 0.5, ALU.mult, ALU.add), [tgf])
                x2_ = V(lambda e: e.tensor_mul(rt1, gfr, arr), [x1_, h3])
                x3_ = V(lambda e: e.tensor_scalar(rt2, arr, -0.7, None, ALU.mult), [x2_])
                x4_ = V(lambda e: e.tensor_scalar(rt1, rt1, 1.4, 0.1, ALU.mult, ALU.add), [x3_])
                x5_ = V(lambda e: e.tensor_add(c0s, rt1, rt2), [x4_])
                gfb1 = pr.barrier()
                x5_ = gfb1
                if debug and b == 0:
                    pr.dma("sync", dbg["f_mrow"][:, :], mrow[:], waits=[x5_])
                    pr.dma("sync", dbg["f_mcol"][:, :], mcol[:], waits=[x5_])
                    pr.dma("sync", dbg["f_gcol1"][:, :], gcol1[:], waits=[x5_])
                    pr.dma("sync", dbg["f_rows"][:, :], rows[:], waits=[x5_])
                    pr.dma("sync", dbg["f_rstore"][:, :], rstore[:], waits=[x5_])

                slot2_free = [x5_, x5_]
                xr_free = [x5_, x5_]
                free_state = [[x5_], [x5_]]
                free_psTr = x5_
                prev_casc_end = [x5_]
                cb_free = x5_
                psTf_free = x5_
                trow_free = x5_
                grow_free = x5_
                xlnf_free = x5_
                for cpair in range(2):
                    setups = []
                    rc0s = []
                    for m in range(2):
                        cc2 = cpair * 2 + m
                        base_t = b * TPB + cc2 * 4
                        setup_ev = []
                        for ti in range(4):
                            g = base_t + ti
                            xslot = xs2[ti % 2]
                            ld = pr.dma("sync", xslot[:], src[g * 128:(g + 1) * 128, :],
                                        waits=[slot2_free[ti % 2]], sem=f"dX{ti % 2}")
                            t_idx = cc2 * 4 + ti
                            nrm = Cc(lambda e, xslot=xslot, t_idx=t_idx: e.activation(
                                xlnf[:], xslot[:], AF.Identity,
                                bias=rstore[:, 2 * t_idx + 1:2 * t_idx + 2],
                                scale=rstore[:, 2 * t_idx:2 * t_idx + 1]),
                                [ld, setup_ev[-1] if setup_ev else xlnf_free])
                            slot2_free[ti % 2] = nrm
                            def trfn(e):
                                last = None
                                for j in range(8):
                                    last = e.transpose(psTr[:, j * 128:(j + 1) * 128],
                                                       xlnf[:, j * 128:(j + 1) * 128], idf[:])
                                return last
                            tk = T(trfn, [nrm, free_psTr])
                            def evfn(e, ti=ti, m=m):
                                return e.tensor_copy(
                                    sa[m][:, :].rearrange("p (j q) -> p j q", j=8)[:, :, ti * 128:(ti + 1) * 128],
                                    psTr[:, :].rearrange("p (j q) -> p j q", j=8))
                            ev = V(evfn, [tk] + free_state[m], relax=True)
                            setup_ev.append(ev)
                            free_psTr = ev
                            xlnf_free = ev
                        rc0s.append(V(lambda e, m=m: e.tensor_copy(sB[m][:], sa[m][:]),
                                      setup_ev, relax=True))
                        setups.append(setup_ev)

                    cyc_tok = [rc0s[0], rc0s[1]]
                    tw_last = [setups[0], setups[1]]
                    for c in range(3):
                        for m in range(2):
                            cur = sa[m] if c % 2 == 0 else sb[m]
                            nxt = sb[m] if c % 2 == 0 else sa[m]
                            ev1 = []
                            for j in range(8):
                                def h1fn(e, j=j, m=m):
                                    last = None
                                    for k in range(8):
                                        last = e.matmul(psH[j % 2][:],
                                                        w1b[:, k * 1024 + j * 128:k * 1024 + (j + 1) * 128],
                                                        sB[m][:, k * 512:(k + 1) * 512],
                                                        start=(k == 0), stop=(k == 7))
                                    return last
                                mm = T(h1fn, [cyc_tok[m],
                                              ev1[-2] if len(ev1) >= 2 else (prev_casc_end[-1] if j < 2 else None)]
                                       + (wl if (c == 0 and m == 0 and j == 0) else []))
                                ev1.append(Cc(lambda e, j=j: e.activation(
                                    gh1[:, j * 512:(j + 1) * 512], psH[j % 2][:], AF.Gelu,
                                    bias=b1c[:, j:j + 1]), [mm]))
                            ev2 = []
                            for j in range(4):
                                def h2fn(e, j=j):
                                    last = None
                                    for k in range(8):
                                        last = e.matmul(psH[j % 2][:],
                                                        w2b[:, k * 512 + j * 128:k * 512 + (j + 1) * 128],
                                                        gh1[:, k * 512:(k + 1) * 512],
                                                        start=(k == 0), stop=(k == 7))
                                    return last
                                mm = T(h2fn, [ev1[-1] if j == 0 else None,
                                              ev2[-2] if len(ev2) >= 2 else (ev1[-2] if j == 0 else None)])
                                ev2.append(Cc(lambda e, j=j: e.activation(
                                    gh2[:, j * 512:(j + 1) * 512], psH[j % 2][:], AF.Gelu,
                                    bias=b2c[:, j:j + 1]), [mm]))
                            ev3 = []
                            for j in range(2):
                                def h3fn(e, j=j):
                                    last = None
                                    for k in range(4):
                                        last = e.matmul(psH[j % 2][:],
                                                        w3b[:, k * 256 + j * 128:k * 256 + (j + 1) * 128],
                                                        gh2[:, k * 512:(k + 1) * 512],
                                                        start=(k == 0), stop=(k == 3))
                                    return last
                                mm = T(h3fn, [ev2[-1] if j == 0 else None,
                                              ev2[-2] if j == 0 else None])
                                ev3.append(Cc(lambda e, j=j: e.activation(
                                    gh3[:, j * 512:(j + 1) * 512], psH[j % 2][:], AF.Gelu,
                                    bias=b3c[:, j:j + 1]), [mm]))
                            def tffn(e):
                                last = None
                                for k in range(2):
                                    last = e.matmul(psTf[0:1, 0:512], w4c[:, k:k + 1],
                                                    gh3[:, k * 512:(k + 1) * 512],
                                                    start=(k == 0), stop=(k == 1))
                                return last
                            ttf = T(tffn, [ev3[-1], ev3[0], psTf_free])
                            ttr = Cc(lambda e: e.activation(trow[:], psTf[0:1, 0:512], AF.Tanh,
                                                            scale=0.5, bias=b4h),
                                     [ttf, h1, trow_free])
                            psTf_free = ttr
                            tgr = V(lambda e: e.tensor_scalar(grow[:], trow[:], c1s, c0s,
                                                              ALU.mult, ALU.add),
                                    [ttr, x5_, grow_free])
                            trow_free = tgr
                            tcb = T(lambda e: e.matmul(psCB[:], ones_row[:], grow[:],
                                                       start=True, stop=True), [tgr, cb_free])
                            grow_free = tcb
                            ecb = V(lambda e: e.tensor_copy(cb[:], psCB[:]), [tcb, cb_free],
                                    relax=True)
                            tw_new = []
                            t1_last = None
                            for i in range(8):
                                si = SG[i]
                                sj = (i + 4) % 8
                                t1 = V(lambda e, sj=sj, cur=cur: e.tensor_mul(
                                    twt[:], cb[:], cur[:, sj * 512:(sj + 1) * 512]),
                                    [ecb, cyc_tok[m]], relax=(i > 0))
                                t1_last = t1
                                op = ALU.add if si > 0 else ALU.subtract
                                t2 = V(lambda e, i=i, cur=cur, nxt=nxt, op=op: e.tensor_tensor(
                                    nxt[:, i * 512:(i + 1) * 512], cur[:, i * 512:(i + 1) * 512],
                                    twt[:], op), [t1])
                                tw_new.append(t2)
                            cb_free = t1_last
                            tw_last[m] = tw_new
                            prev_casc_end = [ev3[-1]]
                            if c < 2:
                                cyc_tok[m] = V(lambda e, m=m, nxt=nxt: e.tensor_copy(
                                    sB[m][:], nxt[:]), tw_new + [ev1[-1]], relax=True)
                            else:
                                cyc_tok[m] = None
                    for m in range(2):
                        cc2 = cpair * 2 + m
                        base_t = b * TPB + cc2 * 4
                        ad_last = None
                        for ti in range(4):
                            g = base_t + ti
                            bi = ti % 2
                            xo = xio[bi]
                            ldr = pr.dma("sync", xr[bi][:], src[g * 128:(g + 1) * 128, :],
                                         waits=[xr_free[bi]], sem=f"dX{2 + bi}")
                            def trfn2(e, ti=ti, m=m):
                                last = None
                                for j in range(8):
                                    last = e.transpose(psTr[:, j * 128:(j + 1) * 128],
                                                       sb[m][:, j * 512 + ti * 128:j * 512 + (ti + 1) * 128],
                                                       idf[:])
                                return last
                            tk = T(trfn2, tw_last[m] + [free_psTr])
                            ad = V(lambda e, xo=xo, bi=bi: e.tensor_add(
                                xo[:], psTr[:], xr[bi][:]),
                                [tk, ldr, st_toks[-2] if len(st_toks) >= 2 else None], relax=True)
                            st = pr.dma("sync", dst[g * 128:(g + 1) * 128, :], xo[:], waits=[ad],
                                        sem=f"dY{bi}")
                            st_toks.append(st)
                            xr_free[bi] = ad
                            free_psTr = ad
                            ad_last = ad
                        free_state[m] = [ad_last]
                tok_out_final = [st_toks[-1]]


            pr.add("sync", lambda e: e.nop(),
                   waits=[("x", s, co.counts[s]) for s in
                          ("dL", "dG", "dP", "dX0", "dX1", "dX2", "dX3", "dY0", "dY1")
                          if co.counts[s] > 0])
            endbar = pr.barrier()
            pr.emit()
            return endbar

    def att_phase(prev_bar):
        src, dst = io["att"]
        with contextlib.ExitStack() as ps:
            def S(name, shape, dtype):
                return ps.enter_context(nc.sbuf_tensor("a" + name, shape, dtype))
            wab = S("wab", [128, 8 * 3072], BF16)
            wp64 = S("wp64", [64, 16 * 1024], BF16)
            baq = S("baq", [128, 24], F32)
            bvbc = S("bvbc", [128, 1024], BF16)
            hs = S("hs", [128, 2], F32)
            chsb = S("chsb", [8, 2], F32)
            cbias = S("cbias", [128, 16], F32)
            kT = S("kT", [128, 8 * 2048], BF16)
            vst = S("vst", [128, 16 * 1040], BF16)
            qT = S("qT", [128, 8 * 512], BF16)
            yb64 = S("yb64", [128, 16 * 512], BF16)
            bprow = yb64[64:65, 0:1024]
            obr = yb64[64:65, 1024:1152]
            xlnT = S("xlnT", [128, 8 * 512], BF16)
            xin2 = [S(f"xin{i}", [128, 1024], F32) for i in range(2)]
            xio = [S(f"xio{i}", [128, 1024], F32) for i in range(2)]
            xln4 = S("xln4", [128, 4096], BF16)
            scrap = None
            cols = S("cols", [128, 8], F32)
            ebuf = [xlnT[:, i * 512:(i + 1) * 512] for i in range(5)]
            zbc = S("zbc", [64, 512], F32)
            zrow = zbc[0:1, 0:512]
            zinv = zrow
            psA = ps.enter_context(nc.psum_tensor("apsA", [128, 512], F32))
            psB = ps.enter_context(nc.psum_tensor("apsB", [128, 512], F32))
            psC = ps.enter_context(nc.psum_tensor("apsC", [128, 512], F32))
            psD = ps.enter_context(nc.psum_tensor("apsD", [128, 512], F32))
            psV = ps.enter_context(nc.psum_tensor("apsV", [128, 1024], F32))
            psY = ps.enter_context(nc.psum_tensor("apsY", [65, 512], F32))
            psZ = psV[0:64, 0:512]
            psTb = ps.enter_context(nc.psum_tensor("apsTb", [128, 1024], BF16))

            pr = co.prog()

            def V(fn, waits, relax=False):
                return pr.add("vector", fn, waits=waits, inc="sV", relax=relax)

            def Cc(fn, waits):
                return pr.add("scalar", fn, waits=waits, inc="sC")

            def T(fn, waits):
                return pr.add("tensor", fn, waits=waits, inc="sT")

            wl0 = pr.dma("gpsimd", _wslices_dst(wab, 8, 3072), _wslices_ap(w_attn, DM, 3 * DM),
                         waits=[prev_bar])
            wl1 = pr.dma("gpsimd", wp64[:, :].rearrange("d (h o) -> d h o", h=16),
                         w_proj.rearrange("(h d) o -> d h o", d=64), waits=[prev_bar])
            wl = [wl1]
            l1 = pr.dma("sync", baq[:], _col_ap(b_attn, 3 * DM), waits=[prev_bar],
                        slow_ok=True, sem="dP")
            bvrow = xio[0][0:1, 0:1024]
            chrow = xio[1][0:1, 0:16]
            l2 = pr.dma("sync", bvrow, b_attn[2048:3072].rearrange("(one n) -> one n", one=1),
                        waits=[prev_bar], sem="dP")
            l3 = pr.dma("gpsimd", bprow[:], _row_ap(b_proj, DM), waits=[prev_bar])
            wl = [l3]
            dp_all = l2

            hz = []
            hz.append(V(lambda e: e.memset(obr, 1.0), [prev_bar]))
            hz.append(V(lambda e: e.memset(hs[0:64, 0:1], 1.0), [prev_bar]))
            hz.append(V(lambda e: e.memset(hs[64:128, 0:1], 0.0), [prev_bar]))
            hz.append(V(lambda e: e.memset(hs[0:64, 1:2], 0.0), [prev_bar]))
            hz.append(V(lambda e: e.memset(hs[64:128, 1:2], 1.0), [prev_bar]))
            tq = V(lambda e: e.tensor_mul(baq[:, 16:24], baq[:, 0:8], baq[:, 8:16]), [dp_all])
            tm1 = T(lambda e: e.matmul(psY[0:8, 0:2], baq[:, 16:24], hs[:, 0:2],
                                       start=True, stop=True), [tq] + hz)
            tm2 = V(lambda e: e.tensor_scalar(chsb[:], psY[0:8, 0:2], -0.125, None, ALU.mult), [tm1])
            tm3 = pr.dma("sync", rcscr[0:1, 0:16].rearrange("one (a b) -> (one a) b", a=8),
                         chsb[0:8, 0:2], waits=[tm2], sem="dP", slow_ok=True)
            tm4 = pr.dma("sync", chrow, rcscr[0:1, 0:16], waits=[tm3], sem="dP")
            tm5 = T(lambda e: e.matmul(psA[:, 0:16], ones_row[0:1, 0:128], chrow,
                                       start=True, stop=True), [tm4])
            tm5b = T(lambda e: e.matmul(psV[:, 0:512], ones_row[:], xio[0][0:1, 0:512],
                                        start=True, stop=True), [dp_all, tm5])
            tm5c = T(lambda e: e.matmul(psV[:, 512:1024], ones_row[:], xio[0][0:1, 512:1024],
                                        start=True, stop=True), [tm5b])
            tm6 = V(lambda e: e.tensor_copy(cbias[:], psA[:, 0:16]), [tm5])
            tm7 = tm6
            tm8 = V(lambda e: e.tensor_copy(bvbc[:], psV[:]), [tm5c])
            tm9 = V(lambda e: e.memset(vst[:, 64:16 * 1040:65], 1.0), [prev_bar])
            setup_done = pr.barrier()

            st_toks = []
            last_smm_prev_chunk = setup_done
            last_vmm_prev_chunk = prev_bar
            last_ye_prev_chunk = prev_bar
            last_proj_prev_chunk = setup_done
            prev_exp_chunk = [setup_done]
            slot_free = [prev_bar, prev_bar]
            free_xio = [setup_done, setup_done]
            tok_psTb_free = prev_bar
            prev_expD = setup_done
            pfx = None
            NCH = B_PER_CORE * 4
            for gc in range(NCH):
                b = gc // 4
                cc = gc % 4
                base_g = gc * 4
                ev_xlnT = []
                qkv_x_free = [None]
                if pfx is None:
                    lntoks = []
                    for ti in range(4):
                        g = base_g + ti
                        xslot = xin2[ti % 2]
                        ld = pr.dma("sync", xslot[:], src[g * 128:(g + 1) * 128, :],
                                    waits=[slot_free[ti % 2]], sem=f"dX{ti % 2}")
                        xl = xln4[:, ti * 1024:(ti + 1) * 1024]
                        tok_xln, _, _, _ = ln_tile(pr, xslot, xl, cols, xl, ld)
                        slot_free[ti % 2] = tok_xln
                        qkv_x_free.append(tok_xln)
                        lntoks.append(tok_xln)
                else:
                    lntoks = pfx
                for ti in range(4):
                    def trfn(e, ti=ti):
                        last = None
                        for j in range(8):
                            last = e.transpose(psTb[:, j * 128:(j + 1) * 128],
                                               xln4[:, ti * 1024 + j * 128:
                                                     ti * 1024 + (j + 1) * 128], idb[:])
                        return last
                    tk = T(trfn, [lntoks[ti], tok_psTb_free])
                    def evfn(e, ti=ti):
                        return e.tensor_copy(
                            xlnT[:, :].rearrange("p (j q) -> p j q", j=8)[:, :, ti * 128:(ti + 1) * 128],
                            psTb[:, :].rearrange("p (j q) -> p j q", j=8))
                    ev = V(evfn, [tk, last_vmm_prev_chunk if ti == 0 else None,
                                  last_ye_prev_chunk if ti == 0 else None], relax=True)
                    ev_xlnT.append(ev)
                    tok_psTb_free = ev
                qk_ev = []
                for part, (coff, dstT, dst_off) in enumerate((
                        (0, qT, lambda j, cc=cc: (j * 512, 512)),
                        (1024, kT, lambda j, cc=cc: (j * 2048 + cc * 512, 512)))):
                    for j in range(8):
                        idx = part * 8 + j
                        psX = (psA, psB, psC)[idx % 3]
                        def qkfn(e, coff=coff, j=j, psX=psX):
                            last = None
                            for k in range(8):
                                last = e.matmul(psX[:],
                                                wab[:, k * 3072 + coff + j * 128:
                                                    k * 3072 + coff + (j + 1) * 128],
                                                xlnT[:, k * 512:(k + 1) * 512],
                                                start=(k == 0), stop=(k == 7))
                            return last
                        mm = T(qkfn, ev_xlnT + [wl0 if (part == 0 and j == 0) else None,
                                                qk_ev[-3] if len(qk_ev) >= 3 else None]
                               + (prev_exp_chunk if idx < 3 else []))
                        off, width = dst_off(j)
                        qk_ev.append(Cc(lambda e, dstT=dstT, off=off, width=width, j=j,
                                        part=part, psX=psX: e.activation(
                            dstT[:, off:off + width], psX[:], AF.Identity,
                            bias=baq[:, 8 * part + j:8 * part + j + 1]), [mm, dp_all]))
                v_ev = []
                for ti in range(4):
                    kt_g = cc * 4 + ti
                    def vfn(e, ti=ti):
                        last = None
                        for half in range(2):
                            for k in range(8):
                                last = e.matmul(
                                    psV[:, half * 512:(half + 1) * 512],
                                    xlnT[:, k * 512 + ti * 128:k * 512 + (ti + 1) * 128],
                                    wab[:, k * 3072 + 2048 + half * 512:
                                        k * 3072 + 2048 + (half + 1) * 512],
                                    start=(k == 0), stop=(k == 7))
                        return last
                    mm = T(vfn, ev_xlnT + [v_ev[-1] if v_ev else tm8,
                                           last_proj_prev_chunk if ti == 0 else None])
                    last_vmm = mm
                    def vefn(e, kt_g=kt_g):
                        return e.scalar_tensor_tensor(
                            vst[:, kt_g * 1040:kt_g * 1040 + 1040]
                            .rearrange("p (h c) -> p h c", h=16)[:, :, 0:64],
                            psV[:, :].rearrange("p (h c) -> p h c", h=16),
                            1.0, bvbc[:, :].rearrange("p (h c) -> p h c", h=16),
                            ALU.mult, ALU.add)
                    v_ev.append(V(vefn, [mm, tm9], relax=True))
                if debug and b == 0 and cc == 0:
                    pr.dma("sync", dbg["a_qT"][:, :], qT[:], waits=[qk_ev[-1]])
                    pr.dma("sync", dbg["a_kT"][:, :], kT[:, 0:512], waits=[qk_ev[-1]])
                    pr.dma("sync", dbg["a_vst"][:, :], vst[:, 0:1040], waits=[v_ev[-1]])
                    pr.add("sync", lambda e: e.nop(), waits=[("x", "dL", co.counts["dL"])])
                    dbgbar = pr.barrier()
                    last_smm_prev_chunk = dbgbar
                    last_proj_prev_chunk = dbgbar
                    prev_exp_chunk = [dbgbar]
                nkt = cc * 4 + 4
                ebn = 5
                new_pfx = []
                stt_ = {"last_yevac": last_proj_prev_chunk, "prev_z4": None,
                        "pend": None}
                prev_exp = [None, None, None, prev_expD]
                ytok = {}
                last_smm = None
                for h in range(16):
                    hp = h % 2
                    hg = h // 2

                    def emit_y(kt, h=h, nkt=nkt, cc=cc):
                        delta = kt - cc * 4
                        o = max(delta, 0) * 128
                        def yfn(e, kt=kt, h=h, nkt=nkt, o=o):
                            return e.matmul(psY[:, o:512],
                                            vst[:, kt * 1040 + h * 65:kt * 1040 + (h + 1) * 65],
                                            ebuf[kt % ebn][:, o:512],
                                            start=(kt == 0), stop=(kt == nkt - 1))
                        w = [ytok[(h, kt, "m")]]
                        if kt == 0:
                            w.append(stt_["last_yevac"])
                        t = T(yfn, w)
                        ytok[(h, kt, "y")] = t
                        return t

                    for kt in range(nkt):
                        pp = kt % 4
                        pe = kt % ebn
                        delta = kt - cc * 4
                        o = max(delta, 0) * 128
                        psS = (psA, psB, psC, psD)[pp]
                        def sfn(e, kt=kt, psS=psS, hp=hp, hg=hg, o=o):
                            return e.matmul(
                                psS[:, o:512],
                                kT[64 * hp:64 * hp + 64, hg * 2048 + kt * 128:
                                   hg * 2048 + (kt + 1) * 128],
                                qT[64 * hp:64 * hp + 64, hg * 512 + o:(hg + 1) * 512],
                                start=True, stop=True)
                        smm = T(sfn, [qk_ev[-1] if (h == 0 and kt == 0) else None,
                                      v_ev[-1] if (h == 0 and kt < 3) else None,
                                      prev_exp[pp]])
                        last_smm = smm
                        gk = h * nkt + kt - ebn
                        ebfree = ytok.get((gk // nkt, gk % nkt, "y")) if gk >= 0 else None
                        ex = Cc(lambda e, pe=pe, h=h, psS=psS, o=o: e.activation(
                            ebuf[pe][:, o:512], psS[:, o:512], AF.Exp, scale=0.125,
                            bias=cbias[:, h:h + 1]), [smm, tm7, ebfree])
                        mask_tok = ex
                        if delta >= 0:
                            mask_tok = V(lambda e, pe=pe, delta=delta: e.tensor_mul(
                                ebuf[pe][:, delta * 128:(delta + 1) * 128],
                                ebuf[pe][:, delta * 128:(delta + 1) * 128],
                                trm[:]), [ex], relax=True)
                        prev_exp[pp] = mask_tok
                        ytok[(h, kt, "m")] = mask_tok
                        if kt == 3 and stt_["pend"] is not None:
                            stt_["pend"]()
                            stt_["pend"] = None
                        if kt >= 4:
                            emit_y(kt - 4)
                    emit_y(nkt - 4)
                    emit_y(nkt - 3)
                    emit_y(nkt - 2)
                    y_last = emit_y(nkt - 1)
                    z2 = Cc(lambda e: e.activation(zinv, psY[64:65, :], AF.Ln),
                            [y_last, stt_["prev_z4"]])
                    z3 = Cc(lambda e: e.activation(zinv, zinv, AF.Exp, scale=-1.0), [z2])

                    def mk_fin(h=h, z3=z3, y_last=y_last, prev_ye=stt_["last_yevac"]):
                        def fin():
                            z4 = T(lambda e: e.matmul(psZ, ones_row[0:1, 0:64], zinv,
                                                      start=True, stop=True),
                                   [z3, prev_ye])
                            z5 = V(lambda e: e.tensor_copy(zbc[:], psZ), [z4])
                            ye = V(lambda e, h=h: e.tensor_mul(
                                yb64[0:64, h * 512:(h + 1) * 512],
                                psY[0:64, :], zbc[:]), [z5, y_last], relax=True)
                            stt_["last_yevac"] = ye
                            stt_["prev_z4"] = ye
                        return fin
                    stt_["pend"] = mk_fin()
                    if h == 15:
                        stt_["pend"]()
                        stt_["pend"] = None
                    if gc + 1 < NCH and h in (1, 3, 5, 7):
                        ti_e = (h - 1) // 2
                        bi_e = ti_e % 2
                        g2 = (gc + 1) * 4 + ti_e
                        eld = pr.dma("sync", xio[bi_e][:],
                                     src[g2 * 128:(g2 + 1) * 128, :],
                                     waits=[free_xio[bi_e]], sem=f"dX{2 + bi_e}")
                        xl = xln4[:, ti_e * 1024:(ti_e + 1) * 1024]
                        tokE, _, _, _ = ln_tile(pr, xio[bi_e], xl, cols, xl, eld,
                                                wait_buf=[ev_xlnT[ti_e]])
                        new_pfx.append(tokE)
                        free_xio[bi_e] = tokE
                last_yevac = stt_["last_yevac"]
                proj_last = None
                for ti in range(4):
                    g = base_g + ti
                    bi = ti % 2
                    xo = xio[bi]
                    ldr = pr.dma("sync", xin2[bi][:], src[g * 128:(g + 1) * 128, :],
                                 waits=[slot_free[bi], qkv_x_free[-1]], sem=f"dX{bi}")
                    def pfn(e, ti=ti):
                        last = None
                        for half in range(2):
                            for h in range(16):
                                last = e.matmul(psV[:, half * 512:(half + 1) * 512],
                                                yb64[0:64, h * 512 + ti * 128:
                                                     h * 512 + (ti + 1) * 128],
                                                wp64[0:64, h * 1024 + half * 512:
                                                     h * 1024 + (half + 1) * 512],
                                                start=(h == 0), stop=False)
                            last = e.matmul(psV[:, half * 512:(half + 1) * 512],
                                            obr,
                                            bprow[0:1, half * 512:(half + 1) * 512],
                                            start=False, stop=True)
                        return last
                    mmp = T(pfn, [last_yevac, v_ev[-1] if ti == 0 else None,
                                  proj_last if ti >= 1 else None,
                                  wl[-1] if not st_toks else None,
                                  st_toks[-2] if len(st_toks) >= 2 else None])
                    ad = V(lambda e, xo=xo, bi=bi: e.tensor_add(
                        xo[:], psV[:], xin2[bi][:]),
                        [mmp, ldr, free_xio[bi]], relax=True)
                    st = pr.dma("sync", dst[g * 128:(g + 1) * 128, :], xo[:],
                                waits=[ad], sem=f"dY{bi}")
                    st_toks.append(st)
                    free_xio[bi] = st
                    slot_free[bi] = ad
                    proj_last = ad
                pfx = new_pfx if len(new_pfx) == 4 else None
                last_smm_prev_chunk = last_smm
                last_vmm_prev_chunk = last_vmm
                last_ye_prev_chunk = last_yevac
                last_proj_prev_chunk = proj_last
                prev_expD = prev_exp[3]
                prev_exp_chunk = [t for t in prev_exp[0:3] if t is not None]

            pr.add("sync", lambda e: e.nop(),
                   waits=[("x", s, co.counts[s]) for s in
                          ("dL", "dG", "dP", "dX0", "dX1", "dX2", "dX3", "dY0", "dY1")
                          if co.counts[s] > 0])
            endbar = pr.barrier()
            pr.emit()
            return endbar

    cur_bar = bar0
    if "ssm" in parts:
        cur_bar = ssm_phase(cur_bar)
    if "ffn1" in parts:
        cur_bar = ffn_phase(cur_bar, "ffn1")
    if "att" in parts:
        cur_bar = att_phase(cur_bar)
    if "ffn2" in parts:
        cur_bar = ffn_phase(cur_bar, "ffn2")

    pf = co.prog()
    pf.add("sync", lambda e: e.nop(),
           waits=[cur_bar] + [("x", s, co.counts[s]) for s in
                              ("dL", "dG", "dP", "dX0", "dX1", "dX2", "dX3", "dY0", "dY1")
                              if co.counts[s] > 0])
    pf.emit()
    es.close()
    return nc


def _make_consts():
    iden = np.eye(128)
    tri = (np.arange(128)[:, None] <= np.arange(128)[None, :]).astype(np.float32)
    return {
        "c_identb": iden.astype(ml_dtypes.bfloat16),
        "c_identf": iden.astype(np.float32),
        "c_triuf": tri,
        "c_trium": tri.astype(ml_dtypes.bfloat16),
    }


def kernel(**inputs):
    parts = ("ssm", "ffn1", "att", "ffn2")
    nc = build_program(parts)
    consts = _make_consts()
    x = np.ascontiguousarray(np.asarray(inputs["L0_attn_a_u"], dtype=np.float32))
    in_maps = []
    for core in range(NCORES):
        m = {"x": x[2 * core:2 * core + 2].reshape(NT, DM)}
        for k, v in inputs.items():
            if k == "L0_attn_a_u":
                continue
            m[k] = np.ascontiguousarray(np.asarray(v, dtype=np.float32))
        m.update(consts)
        in_maps.append(m)
    res = run_bass_kernel_spmd(nc, in_maps, core_ids=list(range(NCORES)))
    out = np.concatenate([np.asarray(r["y"]).reshape(2, TSEQ, DM) for r in res.results], axis=0)
    return out


# version 16
# speedup vs baseline: 1.0088x; 1.0088x over previous
import contextlib
import math

import numpy as np
import ml_dtypes

import concourse.bass as bass
import concourse.mybir as mybir
from concourse.bass_utils import run_bass_kernel_spmd

F32 = mybir.dt.float32
BF16 = mybir.dt.bfloat16
AF = mybir.ActivationFunctionType
ALU = mybir.AluOpType

NCORES = 8
B_PER_CORE = 2
TSEQ = 2048
DM = 1024
PSSM = 512
NT = B_PER_CORE * TSEQ
NTILE = NT // 128
TPB = TSEQ // 128
NHEAD = 16
HDIM = 64
EPS = 1e-5

ENGINES = ("tensor", "vector", "scalar", "gpsimd", "sync")


class Prog:

    def __init__(self, nc, sems, counts):
        self.nc = nc
        self.sems = sems
        self.counts = counts
        self.items = []

    def add(self, engine, fn, waits=(), inc=None, k=1, fence=False, relax=False):
        ws = []
        for w in waits:
            if w is None:
                continue
            peng, sname, cnt = w
            if peng == engine:
                continue
            ws.append((sname, cnt))
        tok = None
        if inc is not None:
            self.counts[inc] += k
            tok = (engine, inc, self.counts[inc])
        self.items.append((engine, fn, ws, inc, self.counts[inc] if inc else 0, k,
                           relax))
        return tok

    def dma(self, engine, out, in_, waits=(), slow_ok=False, sem=None):
        if sem is None:
            sem = "dL" if engine == "sync" else "dG"
        return self.add(engine,
                        lambda e, o=out, i=in_: e.dma_start(
                            out=o, in_=i, allow_slow_non_contiguous=slow_ok),
                        waits=waits, inc=sem, k=16)

    def barrier(self):
        toks = []
        for e in ENGINES:
            toks.append(self.add(e, lambda eng: eng.nop(), inc="bar"))
        n = self.counts["bar"]
        return ("__bar__", "bar", n)

    def emit(self):
        with self.nc.Block() as block:
            for engname in ENGINES:
                self._mk(block, engname)

    def _mk(self, block, engname):
        items = [it for it in self.items if it[0] == engname]
        sems = self.sems
        hazard = engname in ("vector", "gpsimd")

        def body(eng):
            waited = {}
            first = True
            for (_, fn, ws, inc, tgt, k, relax) in items:
                padded = False
                for (sname, cnt) in ws:
                    if waited.get(sname, 0) >= cnt:
                        continue
                    eng.wait_ge(sems[sname], cnt)
                    waited[sname] = cnt
                    padded = True
                if padded:
                    eng.nop(cycle_cnt=96)
                if hazard and not first:
                    if relax:
                        eng.nop()
                    else:
                        eng.drain()
                instr = fn(eng)
                if inc is not None:
                    instr.then_inc(sems[inc], k)
                first = False
        getattr(block, engname)(body)


class Core:

    def __init__(self, nc, es):
        self.nc = nc
        self.es = es
        self.sems = {}
        self.counts = {}
        for s in ("dL", "dG", "dP", "dX0", "dX1", "dX2", "dX3", "dY0", "dY1",
                  "sT", "sV", "sC", "sG", "bar"):
            self.sems[s] = es.enter_context(nc.semaphore(s))
            self.counts[s] = 0

    def prog(self):
        return Prog(self.nc, self.sems, self.counts)


def _col_ap(dram_ap, n):
    total = 1
    for s in dram_ap.shape:
        total *= s
    assert total == n
    flat = dram_ap.rearrange(" ".join(f"a{i}" for i in range(len(dram_ap.shape)))
                             + " -> (" + " ".join(f"a{i}" for i in range(len(dram_ap.shape))) + ")") \
        if len(dram_ap.shape) > 1 else dram_ap
    return flat.rearrange("(j p) -> p j", p=128)


def _row_ap(dram_ap, n):
    flat = dram_ap
    if len(dram_ap.shape) > 1:
        flat = dram_ap.rearrange(" ".join(f"a{i}" for i in range(len(dram_ap.shape)))
                                 + " -> (" + " ".join(f"a{i}" for i in range(len(dram_ap.shape))) + ")")
    return flat.rearrange("(one n) -> one n", one=1)


def _wslices_ap(w_ap, kin, kout):
    return w_ap.rearrange("(k p) o -> p k o", p=128)


def _wslices_dst(sb_handle, ktiles, kout):
    return sb_handle[:, :].rearrange("p (k o) -> p k o", k=ktiles)


def build_program(parts=("ssm", "ffn1", "att", "ffn2"), debug=False, dbg_tile=0):
    nc = bass.Bass("TRN2", target_bir_lowering=False)
    es = contextlib.ExitStack()
    co = Core(nc, es)

    dt_in = {}
    def din(name, shape, dtype=F32):
        dt_in[name] = nc.dram_tensor(name, list(shape), dtype, kind="ExternalInput").ap()
        return dt_in[name]

    x_in = din("x", (NT, DM))
    lam_re = din("L0_attn_a_Lambda_re", (PSSM,))
    lam_im = din("L0_attn_a_Lambda_im", (PSSM,))
    Bw = din("L0_attn_a_B", (PSSM, DM, 2))
    Cw = din("L0_attn_a_C", (DM, PSSM, 2))
    Dw = din("L0_attn_a_D", (DM,))
    lstep = din("L0_attn_a_log_step", (PSSM, 1))
    w_attn = din("L0_attn_b_w_attn", (DM, 3 * DM))
    b_attn = din("L0_attn_b_b_attn", (3 * DM,))
    w_proj = din("L0_attn_b_w_proj", (DM, DM))
    b_proj = din("L0_attn_b_b_proj", (DM,))
    cn_w1 = din("L0_ffn_cn_w1", (DM, DM))
    cn_b1 = din("L0_ffn_cn_b1", (DM,))
    cn_w2 = din("L0_ffn_cn_w2", (DM, 512))
    cn_b2 = din("L0_ffn_cn_b2", (512,))
    cn_w3 = din("L0_ffn_cn_w3", (512, 256))
    cn_b3 = din("L0_ffn_cn_b3", (256,))
    cn_w4 = din("L0_ffn_cn_w4", (256, 1))
    cn_b4 = din("L0_ffn_cn_b4", (1,))
    gc_w1 = din("L0_ffn_gc_w1", (DM, 512))
    gc_b1 = din("L0_ffn_gc_b1", (512,))
    gc_w2 = din("L0_ffn_gc_w2", (512, 256))
    gc_b2 = din("L0_ffn_gc_b2", (256,))
    gc_w3 = din("L0_ffn_gc_w3", (256, 1))
    gc_b3 = din("L0_ffn_gc_b3", (1,))
    a_range = din("L0_ffn_adaptive_range", (1,))
    identb = din("c_identb", (128, 128), BF16)
    identf = din("c_identf", (128, 128), F32)
    triuf = din("c_triuf", (128, 128), F32)
    trium = din("c_trium", (128, 128), BF16)

    y_out = nc.dram_tensor("y", [NT, DM], F32, kind="ExternalOutput").ap()
    r0 = nc.dram_tensor("r0", [NT, DM], F32, kind="Internal").ap()
    r1 = nc.dram_tensor("r1", [NT, DM], F32, kind="Internal").ap()
    rcscr = nc.dram_tensor("rcscr", [1, DM], F32, kind="Internal").ap()
    dbg = {}
    if debug:
        for nm, shape, dt in (("d_pcol", [128, 160], F32), ("d_ainvre", [128, 512], F32),
                              ("d_ainvim", [128, 512], F32), ("d_are", [128, 512], F32),
                              ("d_aim", [128, 512], F32), ("d_bbre", [128, 512], BF16),
                              ("d_bbim", [128, 512], BF16), ("d_cret", [128, 1024], BF16),
                              ("d_xln", [128, 1024], BF16), ("d_xlnT", [128, 1024], BF16),
                              ("d_vre", [128, 512], F32), ("d_vim", [128, 512], F32),
                              ("d_xsre", [128, 512], BF16), ("d_xsim", [128, 512], BF16),
                              ("d_du", [128, 1024], F32), ("d_dbc", [128, 1024], F32), ("d_xo", [128, 1024], F32),
                              ("a_qT", [128, 4096], BF16), ("a_kT", [128, 512], BF16),
                              ("a_vst", [128, 1040], BF16), ("a_e00", [128, 512], BF16),
                              ("a_yb", [64, 512], BF16), ("a_zbc", [64, 512], F32),
                              ("f_mrow", [1, 1024], F32), ("f_mcol", [128, 8], F32),
                              ("f_gcol1", [128, 4], F32), ("f_rows", [1, 16], F32),
                              ("f_rstore", [128, 32], F32)):
            dbg[nm] = nc.dram_tensor(nm, shape, dt, kind="ExternalOutput").ap()

    chain = {"ssm": (x_in, r0), "ffn1": (r0, r1), "att": (r1, r0), "ffn2": (r0, y_out)}
    io = {p: list(chain[p]) for p in parts}
    io[parts[0]][0] = x_in
    io[parts[-1]][1] = y_out

    sb = nc.alloc_sbuf_tensor
    psa = nc.alloc_psum_tensor

    ones_row = sb("ones_row", [1, 128], F32)
    ones_col = sb("ones_col", [128, 1], F32)
    idb = sb("idb", [128, 128], BF16)
    idf = sb("idf", [128, 128], F32)
    trf = sb("trf", [128, 128], F32)
    trm = sb("trm", [128, 128], BF16)

    pg = co.prog()
    t_ld = []
    for dst, src in ((idb, identb), (idf, identf), (trf, triuf), (trm, trium)):
        t_ld.append(pg.dma("sync", dst[:], src[:]))
    tok_ones = pg.add("vector", lambda e: e.memset(ones_row[:], 1.0), inc="sV")
    tok_onesc = pg.add("vector", lambda e: e.memset(ones_col[:], 1.0), inc="sV")
    bar0 = pg.barrier()
    pg.emit()

    ln_consts = {}

    def ln_tile(pr, xt, xln_bf, cols, dummy, wait_x, xlnf=None, wait_buf=()):
        sx, sxx, m, biasc, r, nrm = (cols[:, i:i + 1] for i in range(6))
        t1 = pr.add("scalar", lambda e: e.activation(dummy[:], xt[:], AF.Copy, accum_out=sx),
                    waits=[wait_x] + list(wait_buf), inc="sC")
        t2 = pr.add("scalar", lambda e: e.activation(dummy[:], xt[:], AF.Square, accum_out=sxx),
                    waits=[wait_x], inc="sC")
        t3 = pr.add("vector", lambda e: e.tensor_scalar(m, sx, 1.0 / DM, None, ALU.mult),
                    waits=[t1], inc="sV")
        t4 = pr.add("scalar", lambda e: e.activation(biasc, m, AF.Square), waits=[t3], inc="sC")
        t5 = pr.add("vector", lambda e: e.tensor_scalar(biasc, biasc, -1.0, EPS, ALU.mult, ALU.add),
                    waits=[t4], inc="sV")
        t6 = pr.add("scalar", lambda e: e.activation(r, sxx, AF.Ln, bias=biasc, scale=1.0 / DM),
                    waits=[t2, t5], inc="sC")
        t7 = pr.add("scalar", lambda e: e.activation(r, r, AF.Exp, scale=-0.5), waits=[t6], inc="sC")
        t8 = pr.add("vector", lambda e: e.tensor_scalar(nrm, r, m, -1.0, ALU.mult, ALU.mult),
                    waits=[t7], inc="sV")
        tok = pr.add("scalar", lambda e: e.activation(xln_bf[:], xt[:], AF.Identity, bias=nrm, scale=r),
                     waits=[t8], inc="sC")
        tok2 = tok
        if xlnf is not None:
            tok2 = pr.add("scalar", lambda e: e.activation(xlnf[:], xt[:], AF.Identity, bias=nrm, scale=r),
                          waits=[t8], inc="sC")
        return tok, tok2, r, nrm

    def ln_stats(pr, xt, cols, rcol, ncol, scrap, wait_x):
        sx, sxx, m, biasc = (cols[:, i:i + 1] for i in range(4))
        t1 = pr.add("scalar", lambda e: e.activation(scrap[:], xt[:], AF.Copy, accum_out=sx),
                    waits=[wait_x], inc="sC")
        t2 = pr.add("scalar", lambda e: e.activation(scrap[:], xt[:], AF.Square, accum_out=sxx),
                    waits=[wait_x], inc="sC")
        t3 = pr.add("vector", lambda e: e.tensor_scalar(m, sx, 1.0 / DM, None, ALU.mult),
                    waits=[t1], inc="sV")
        t4 = pr.add("scalar", lambda e: e.activation(biasc, m, AF.Square), waits=[t3], inc="sC")
        t5 = pr.add("vector", lambda e: e.tensor_scalar(biasc, biasc, -1.0, EPS, ALU.mult, ALU.add),
                    waits=[t4], inc="sV")
        t6 = pr.add("scalar", lambda e: e.activation(rcol, sxx, AF.Ln, bias=biasc, scale=1.0 / DM),
                    waits=[t2, t5], inc="sC")
        t7 = pr.add("scalar", lambda e: e.activation(rcol, rcol, AF.Exp, scale=-0.5),
                    waits=[t6], inc="sC")
        t8 = pr.add("vector", lambda e: e.tensor_scalar(ncol, rcol, m, -1.0, ALU.mult, ALU.mult),
                    waits=[t7], inc="sV")
        return t8

    def transpose8(pr, src_bf, psT, dst_bf, wait_src, wait_dst_free, blocks=8, ident=None,
                   relax=False):
        idt = ident if ident is not None else idb
        def tfn(e):
            last = None
            for j in range(blocks):
                last = e.transpose(psT[:, j * 128:(j + 1) * 128], src_bf[:, j * 128:(j + 1) * 128], idt[:])
            return last
        tk = pr.add("tensor", tfn, waits=[wait_src, wait_dst_free], inc="sT")
        ev = pr.add("vector", lambda e: e.tensor_copy(dst_bf[:, 0:blocks * 128], psT[:, 0:blocks * 128]),
                    waits=[tk], inc="sV", relax=relax)
        return ev

    def ssm_phase(prev_bar):
        src, dst = io["ssm"]
        with contextlib.ExitStack() as ps:
            def S(name, shape, dtype):
                return ps.enter_context(nc.sbuf_tensor(name, shape, dtype))
            pcol = S("pcol", [128, 160], F32)
            AinvT_re = S("AinvT_re", [128, 512], F32)
            AinvT_im = S("AinvT_im", [128, 512], F32)
            Are = S("Are", [128, 512], F32)
            Aim = S("Aim", [128, 512], F32)
            AinvP_re = S("AinvP_re", [128, 512], F32)
            AinvP_im = S("AinvP_im", [128, 512], F32)
            BbT_re = S("BbT_re", [128, 8 * 512], BF16)
            BbT_im = S("BbT_im", [128, 8 * 512], BF16)
            CreT = S("CreT", [128, 4 * 1024], BF16)
            CimNT = S("CimNT", [128, 4 * 1024], BF16)
            Dbc = S("Dbc", [128, 1024], F32)
            braw = S("braw", [128, 2048], F32)
            bstg_re = S("bstg_re", [128, 1024], BF16)
            bstg_im = S("bstg_im", [128, 1024], BF16)
            xinb = [S(f"xin{i}", [128, 1024], F32) for i in range(2)]
            xoutb = [S(f"xout{i}", [128, 1024], F32) for i in range(2)]
            xln = S("xln", [128, 1024], BF16)
            xlnf2 = [S(f"xlnf{i}", [128, 1024], F32) for i in range(2)]
            xlnT = S("xlnT", [128, 1024], BF16)
            dummy = S("scrap", [128, 1024], F32)
            lncols = S("lncols", [128, 8], F32)
            vre2 = [S(f"vre{i}", [128, 512], F32) for i in range(2)]
            vim2 = [S(f"vim{i}", [128, 512], F32) for i in range(2)]
            tmp1 = S("tmp1", [128, 512], F32)
            tmp2 = S("tmp2", [128, 512], F32)
            tmp3 = S("tmp3", [128, 512], F32)
            tmp4 = S("tmp4", [128, 512], F32)
            xs_re = S("xs_re", [128, 512], BF16)
            xs_im = S("xs_im", [128, 512], BF16)
            du2 = [S(f"du{i}", [128, 1024], F32) for i in range(2)]
            psT = ps.enter_context(nc.psum_tensor("psT", [128, 1024], BF16))
            psBUre = ps.enter_context(nc.psum_tensor("psBUre", [128, 512], F32))
            psBUim = ps.enter_context(nc.psum_tensor("psBUim", [128, 512], F32))
            psSre = ps.enter_context(nc.psum_tensor("psSre", [128, 512], F32))
            psSim = ps.enter_context(nc.psum_tensor("psSim", [128, 512], F32))
            psYS = ps.enter_context(nc.psum_tensor("psYS", [128, 1024], F32))

            c = {}
            for i, nm in enumerate(("Lre", "Lim", "lst", "lr", "stp", "th", "sh2", "cth",
                                    "sth", "em", "er", "are", "aim", "nre", "ml", "inv",
                                    "fre", "fim", "ainvre", "ainvim", "pwre", "pwim",
                                    "a128re", "a128im", "acre", "acim", "t1", "t2",
                                    "t3", "t4", "t5", "t6", "q1", "q2", "q3")):
                c[nm] = pcol[:, 4 * i:4 * i + 4]

            pr = co.prog()

            w0 = pr.dma("sync", c["Lre"], _col_ap(lam_re, 512), waits=[prev_bar], slow_ok=True, sem="dP")
            w1 = pr.dma("sync", c["Lim"], _col_ap(lam_im, 512), waits=[prev_bar], slow_ok=True, sem="dP")
            w2 = pr.dma("sync", c["lst"], _col_ap(lstep, 512), waits=[prev_bar], slow_ok=True, sem="dP")
            drow_t = pr.dma("sync", xinb[0][0:1, 0:1024], _row_ap(Dw, 1024), waits=[prev_bar], sem="dP")
            dp_all = drow_t
            w0 = w1 = w2 = drow_t = dp_all

            def V(fn, waits, fence=False, relax=False):
                return pr.add("vector", fn, waits=waits, inc="sV", fence=fence, relax=relax)

            def Cc(fn, waits):
                return pr.add("scalar", fn, waits=waits, inc="sC")

            t = V(lambda e: e.tensor_scalar_min(c["lr"], c["Lre"], -1e-4), [w0])
            tstp = Cc(lambda e: e.activation(c["stp"], c["lst"], AF.Exp), [w2])
            tth = V(lambda e: e.tensor_mul(c["th"], c["Lim"], c["stp"]), [w1, tstp])
            trr = V(lambda e: e.tensor_mul(c["t1"], c["lr"], c["stp"]), [t, tstp])
            tsh = Cc(lambda e: e.activation(c["sh2"], c["th"], AF.Sin, scale=0.5), [tth])
            tsth = Cc(lambda e: e.activation(c["sth"], c["th"], AF.Sin), [tth])
            tsq = Cc(lambda e: e.activation(c["sh2"], c["sh2"], AF.Square), [tsh])
            tcth = V(lambda e: e.tensor_scalar(c["cth"], c["sh2"], -2.0, 1.0, ALU.mult, ALU.add), [tsq])
            fact = [40320.0, 5040.0, 720.0, 120.0, 24.0, 6.0, 2.0, 1.0]
            tt = V(lambda e: e.tensor_scalar(c["em"], c["t1"], 1.0 / fact[0], 1.0 / fact[1],
                                             ALU.mult, ALU.add), [trr])
            for kf in range(2, 8):
                tt = V(lambda e: e.tensor_mul(c["em"], c["em"], c["t1"]), [tt])
                tt = V(lambda e, k=kf: e.tensor_scalar(c["em"], c["em"], 1.0 / fact[k], None, ALU.add), [tt])
            tt = V(lambda e: e.tensor_mul(c["em"], c["em"], c["t1"]), [tt])
            ter = V(lambda e: e.tensor_scalar(c["er"], c["em"], 1.0, None, ALU.add), [tt])
            tare = V(lambda e: e.tensor_mul(c["are"], c["er"], c["cth"]), [ter, tcth])
            taim = V(lambda e: e.tensor_mul(c["aim"], c["er"], c["sth"]), [ter, tsth])
            tn1 = V(lambda e: e.tensor_mul(c["nre"], c["em"], c["cth"]), [tt, tcth])
            tn2 = V(lambda e: e.tensor_scalar(c["t2"], c["sh2"], -2.0, None, ALU.mult), [tsq])
            tnre = V(lambda e: e.tensor_add(c["nre"], c["nre"], c["t2"]), [tn1, tn2])
            tl1 = Cc(lambda e: e.activation(c["q1"], c["lr"], AF.Square), [t])
            tl2 = Cc(lambda e: e.activation(c["q2"], c["Lim"], AF.Square), [w1])
            tl3 = V(lambda e: e.tensor_add(c["ml"], c["q1"], c["q2"]), [tl1, tl2])
            tl4 = Cc(lambda e: e.activation(c["ml"], c["ml"], AF.Ln), [tl3])
            tinv = Cc(lambda e: e.activation(c["inv"], c["ml"], AF.Exp, scale=-1.0), [tl4])
            tf1 = V(lambda e: e.tensor_mul(c["t3"], c["nre"], c["lr"]), [tnre])
            tf2 = V(lambda e: e.tensor_mul(c["t4"], c["aim"], c["Lim"]), [taim])
            tf3 = V(lambda e: e.tensor_add(c["t3"], c["t3"], c["t4"]), [tf1, tf2])
            tfre = V(lambda e: e.tensor_mul(c["fre"], c["t3"], c["inv"]), [tf3, tinv])
            tf4 = V(lambda e: e.tensor_mul(c["t3"], c["aim"], c["lr"]), [taim])
            tf5 = V(lambda e: e.tensor_mul(c["t4"], c["nre"], c["Lim"]), [tnre])
            tf6 = V(lambda e: e.tensor_sub(c["t3"], c["t3"], c["t4"]), [tf4, tf5])
            tfim = V(lambda e: e.tensor_mul(c["fim"], c["t3"], c["inv"]), [tf6, tinv])
            ta1 = Cc(lambda e: e.activation(c["q1"], c["are"], AF.Square), [tare, tl3])
            ta2 = Cc(lambda e: e.activation(c["q2"], c["aim"], AF.Square), [taim, tl3])
            ta3 = V(lambda e: e.tensor_add(c["q3"], c["q1"], c["q2"]), [ta1, ta2])
            ta4 = Cc(lambda e: e.activation(c["q3"], c["q3"], AF.Ln), [ta3])
            ta5 = Cc(lambda e: e.activation(c["q3"], c["q3"], AF.Exp, scale=-1.0), [ta4])
            tai1 = V(lambda e: e.tensor_mul(c["ainvre"], c["are"], c["q3"]), [ta5, tare])
            tai2 = V(lambda e: e.tensor_scalar(c["t6"], c["q3"], -1.0, None, ALU.mult), [ta5])
            tai3 = V(lambda e: e.tensor_mul(c["ainvim"], c["aim"], c["t6"]), [tai2, taim])

            def build_table(Tre, Tim, base_re, base_im, waits):
                toks = []
                tz1 = V(lambda e: e.memset(Tre[:, 0:512:128], 1.0), waits)
                tz2 = V(lambda e: e.memset(Tim[:, 0:512:128], 0.0), waits)
                tp1 = V(lambda e: e.tensor_copy(c["pwre"], base_re), waits)
                tp2 = V(lambda e: e.tensor_copy(c["pwim"], base_im), waits)
                last = [tz1, tz2, tp1, tp2]
                for k in range(7):
                    w = 1 << k
                    nl = []
                    for cc in range(4):
                        o = cc * 128
                        pr_ = c["pwre"][:, cc:cc + 1]
                        pi_ = c["pwim"][:, cc:cc + 1]
                        fc = w < 8
                        a1 = V(lambda e, o=o, w=w, pr_=pr_: e.tensor_scalar(
                            tmp1[:, 0:w], Tre[:, o:o + w], pr_, None, ALU.mult), last, fence=fc)
                        a2 = V(lambda e, o=o, w=w, pi_=pi_: e.tensor_scalar(
                            tmp2[:, 0:w], Tim[:, o:o + w], pi_, None, ALU.mult), last, fence=fc)
                        a3 = V(lambda e, o=o, w=w: e.tensor_sub(
                            Tre[:, o + w:o + 2 * w], tmp1[:, 0:w], tmp2[:, 0:w]), [a1, a2], fence=fc)
                        a4 = V(lambda e, o=o, w=w, pi_=pi_: e.tensor_scalar(
                            tmp1[:, 0:w], Tre[:, o:o + w], pi_, None, ALU.mult), [a3], fence=fc)
                        a5 = V(lambda e, o=o, w=w, pr_=pr_: e.tensor_scalar(
                            tmp2[:, 0:w], Tim[:, o:o + w], pr_, None, ALU.mult), [a4], fence=fc)
                        a6 = V(lambda e, o=o, w=w: e.tensor_add(
                            Tim[:, o + w:o + 2 * w], tmp1[:, 0:w], tmp2[:, 0:w]), [a5], fence=fc)
                        nl.append(a6)
                    s1 = V(lambda e: e.tensor_mul(c["t1"], c["pwre"], c["pwre"]), nl, fence=True)
                    s2 = V(lambda e: e.tensor_mul(c["t2"], c["pwim"], c["pwim"]), nl)
                    s3 = V(lambda e: e.tensor_mul(c["t3"], c["pwre"], c["pwim"]), nl)
                    s4 = V(lambda e: e.tensor_sub(c["pwre"], c["t1"], c["t2"]), [s1, s2], fence=True)
                    s5 = V(lambda e: e.tensor_scalar(c["pwim"], c["t3"], 2.0, None, ALU.mult), [s3, s4], fence=True)
                    last = [s4, s5]
                return last

            tA = build_table(Are, Aim, c["are"], c["aim"], [tare, taim, tnre, tfre, tfim])
            tA2 = V(lambda e: e.tensor_copy(c["a128re"], c["pwre"]), tA)
            tA3 = V(lambda e: e.tensor_copy(c["a128im"], c["pwim"]), tA)
            tI = build_table(AinvP_re, AinvP_im, c["ainvre"], c["ainvim"],
                             [tai1, tai3, tA2, tA3])
            tIT = []
            prev_psf = None
            for part, (srcT, dstT) in enumerate(((AinvP_re, AinvT_re), (AinvP_im, AinvT_im))):
                psf = psYS
                def tfn(e, srcT=srcT):
                    last = None
                    for cc in range(4):
                        last = e.transpose(psf[:, cc * 128:(cc + 1) * 128],
                                           srcT[:, cc * 128:(cc + 1) * 128], idf[:])
                    return last
                tk = pr.add("tensor", tfn, waits=tI + ([prev_psf] if prev_psf else []), inc="sT")
                ev = V(lambda e, dstT=dstT: e.tensor_copy(dstT[:], psf[:, 0:512]), [tk])
                prev_psf = ev
                tIT.append(ev)

            bb_done = []
            prev_ev = tIT
            prev_braw_rd = None
            for j in range(4):
                lb = pr.dma("sync", braw[:], Bw.rearrange("p h c -> p (h c)")[j * 128:(j + 1) * 128, :],
                            waits=[prev_bar] + ([prev_braw_rd] if prev_braw_rd else []), sem="dX0")
                d1 = V(lambda e: e.tensor_copy(tmp1[:, 0:512], braw[:, 0:1024:2]), [lb])
                d2 = V(lambda e: e.tensor_copy(tmp2[:, 0:512], braw[:, 1:1024:2]), [lb])
                d3 = V(lambda e: e.tensor_copy(tmp3[:, 0:512], braw[:, 1024:2048:2]), [lb])
                d4 = V(lambda e: e.tensor_copy(tmp4[:, 0:512], braw[:, 1025:2048:2]), [lb])
                prev_braw_rd = d4
                fre_j = c["fre"][:, j:j + 1]
                fim_j = c["fim"][:, j:j + 1]
                e1 = V(lambda e, f=fre_j: e.tensor_scalar(dummy[:, 0:512], tmp1[:, 0:512], f, None, ALU.mult), [d1])
                e2 = V(lambda e, f=fim_j: e.tensor_scalar(du2[0][:, 0:512], tmp2[:, 0:512], f, None, ALU.mult), [d2])
                e3 = V(lambda e: e.tensor_sub(bstg_re[:, 0:512], dummy[:, 0:512], du2[0][:, 0:512]), [e1, e2])
                e4 = V(lambda e, f=fre_j: e.tensor_scalar(dummy[:, 0:512], tmp3[:, 0:512], f, None, ALU.mult), [d3])
                e5 = V(lambda e, f=fim_j: e.tensor_scalar(du2[0][:, 0:512], tmp4[:, 0:512], f, None, ALU.mult), [d4])
                e6 = V(lambda e: e.tensor_sub(bstg_re[:, 512:1024], dummy[:, 0:512], du2[0][:, 0:512]), [e4, e5])
                e7 = V(lambda e, f=fim_j: e.tensor_scalar(dummy[:, 0:512], tmp1[:, 0:512], f, None, ALU.mult), [e3])
                e8 = V(lambda e, f=fre_j: e.tensor_scalar(du2[0][:, 0:512], tmp2[:, 0:512], f, None, ALU.mult), [e3])
                e9 = V(lambda e: e.tensor_add(bstg_im[:, 0:512], dummy[:, 0:512], du2[0][:, 0:512]), [e7, e8])
                ea = V(lambda e, f=fim_j: e.tensor_scalar(dummy[:, 0:512], tmp3[:, 0:512], f, None, ALU.mult), [e6])
                eb = V(lambda e, f=fre_j: e.tensor_scalar(du2[0][:, 0:512], tmp4[:, 0:512], f, None, ALU.mult), [e6])
                ec = V(lambda e: e.tensor_add(bstg_im[:, 512:1024], dummy[:, 0:512], du2[0][:, 0:512]), [e9, ea, eb])
                for (stg, dstT, wsrc) in ((bstg_re, BbT_re, e6), (bstg_im, BbT_im, ec)):
                    def tfn(e, stg=stg):
                        last = None
                        for k in range(8):
                            last = e.transpose(psT[:, k * 128:(k + 1) * 128],
                                               stg[:, k * 128:(k + 1) * 128], idb[:])
                        return last
                    tk = pr.add("tensor", tfn, waits=[wsrc, ec] + prev_ev, inc="sT")
                    def evfn(e, dstT=dstT, j=j):
                        return e.tensor_copy(
                            dstT[:, :].rearrange("p (k q) -> p k q", k=8)[:, :, j * 128:(j + 1) * 128],
                            psT[:, 0:1024].rearrange("p (k q) -> p k q", k=8))
                    ev = V(evfn, [tk])
                    prev_ev = [ev]
                    bb_done.append(ev)

            for k in range(8):
                lc = pr.dma("sync", braw[:, 0:1024], Cw.rearrange("h p c -> h (p c)")[k * 128:(k + 1) * 128, :],
                            waits=[prev_bar, prev_braw_rd], sem="dX0")
                d1 = V(lambda e: e.tensor_copy(bstg_re[:, 0:512], braw[:, 0:1024:2]), [lc] + prev_ev)
                d2 = V(lambda e: e.tensor_scalar(bstg_im[:, 0:512], braw[:, 1:1024:2], -1.0, None, ALU.mult),
                       [lc])
                prev_braw_rd = d2
                for (stg, dstT, wsrc) in ((bstg_re, CreT, d1), (bstg_im, CimNT, d2)):
                    def tfn(e, stg=stg):
                        last = None
                        for j in range(4):
                            last = e.transpose(psT[:, j * 128:(j + 1) * 128],
                                               stg[:, j * 128:(j + 1) * 128], idb[:])
                        return last
                    tk = pr.add("tensor", tfn, waits=[wsrc, d1, d2] + prev_ev, inc="sT")
                    def evfn(e, dstT=dstT, k=k):
                        return e.tensor_copy(
                            dstT[:, :].rearrange("p (j q) -> p j q", j=4)[:, :, k * 128:(k + 1) * 128],
                            psT[:, 0:512].rearrange("p (j q) -> p j q", j=4))
                    ev = V(evfn, [tk])
                    prev_ev = [ev]

            tdm = pr.add("tensor", lambda e: (
                e.matmul(psBUre[:], ones_row[:], xinb[0][0:1, 0:512], start=True, stop=True),
                e.matmul(psBUim[:], ones_row[:], xinb[0][0:1, 512:1024], start=True, stop=True))[-1],
                waits=[drow_t, tok_ones], inc="sT")
            tdv1 = V(lambda e: e.tensor_copy(Dbc[:, 0:512], psBUre[:]), [tdm])
            tdv2 = V(lambda e: e.tensor_copy(Dbc[:, 512:1024], psBUim[:]), [tdm])

            prologue_done = pr.barrier()

            for b in range(B_PER_CORE):
                zc1 = V(lambda e: e.memset(c["acre"], 0.0), [prologue_done])
                zc2 = V(lambda e: e.memset(c["acim"], 0.0), [prologue_done])
                stt = {
                    "free_xin": [prologue_done, prologue_done],
                    "free_xout": [prologue_done, prologue_done],
                    "xlnT_free": prologue_done,
                    "bu_free": prologue_done,
                    "s_free": prologue_done,
                    "ys_free": [tdv1, tdv2],
                    "xs_free": prologue_done,
                    "carry": [zc1, zc2],
                    "xln_free": [prologue_done],
                    "prev_o1": prologue_done,
                }
                A = {}

                def stageA1(t, b=b):
                    g = b * TPB + t
                    bi = g % 2
                    m = t % 2
                    xt = xinb[bi]
                    ld = pr.dma("sync", xt[:], src[g * 128:(g + 1) * 128, :],
                                waits=[stt["free_xin"][bi]], sem=f"dX{bi}")
                    tok_xln, tok_xlnf, _, _ = ln_tile(pr, xt, xln, lncols, dummy, ld,
                                                      xlnf=xlnf2[m], wait_buf=stt["xln_free"])
                    A[("ln", t)] = (tok_xln, tok_xlnf)
                    A[("xt", t)] = (xt, ld)
                    A[("bi", t)] = bi

                def stageA2(t, b=b):
                    m = t % 2
                    tok_xln, tok_xlnf = A[("ln", t)]
                    ev_xlnT = transpose8(pr, xln, psT, xlnT, tok_xln, stt["xlnT_free"],
                                         relax=True)

                    def bufn(e):
                        last = None
                        for j in range(8):
                            last = e.matmul(psBUre[:], xlnT[:, j * 128:(j + 1) * 128],
                                            BbT_re[:, j * 512:(j + 1) * 512],
                                            start=(j == 0), stop=(j == 7))
                        for j in range(8):
                            last = e.matmul(psBUim[:], xlnT[:, j * 128:(j + 1) * 128],
                                            BbT_im[:, j * 512:(j + 1) * 512],
                                            start=(j == 0), stop=(j == 7))
                        return last
                    t_bu = pr.add("tensor", bufn, waits=[ev_xlnT, stt["bu_free"]], inc="sT")
                    v1 = V(lambda e: e.tensor_mul(tmp1[:], AinvT_re[:], psBUre[:]), [t_bu], relax=True)
                    v2 = V(lambda e: e.tensor_mul(tmp2[:], AinvT_im[:], psBUim[:]), [t_bu], relax=True)
                    v3 = V(lambda e, m=m: e.tensor_sub(vre2[m][:], tmp1[:], tmp2[:]),
                           [v1, v2, A.get(("sc", t - 2))], relax=True)
                    v4 = V(lambda e: e.tensor_mul(tmp1[:], AinvT_im[:], psBUre[:]), [v3], relax=True)
                    v5 = V(lambda e: e.tensor_mul(tmp2[:], AinvT_re[:], psBUim[:]), [v3], relax=True)
                    v6 = V(lambda e, m=m: e.tensor_add(vim2[m][:], tmp1[:], tmp2[:]), [v4, v5], relax=True)
                    stt["bu_free"] = v6
                    g1 = pr.add("gpsimd", lambda e, m=m: e.tensor_tensor(
                        du2[m][:], Dbc[:], xlnf2[m][:], ALU.mult),
                        waits=[tok_xlnf, stt["prev_o1"]], inc="sG")
                    stt["prev_o1"] = g1
                    stt["xlnT_free"] = t_bu
                    stt["xln_free"] = [ev_xlnT]
                    A[("v", t)] = (v3, v6)
                    A[("g1", t)] = g1

                def stageB(t, b=b):
                    g = b * TPB + t
                    m = t % 2
                    bi = A[("bi", t)]
                    xt, ld = A[("xt", t)]
                    xo = xoutb[bi]
                    v3, v6 = A[("v", t)]
                    g1 = A[("g1", t)]

                    def scfn(e, m=m):
                        last = None
                        for cc in range(4):
                            last = e.matmul(psSre[:, cc * 128:(cc + 1) * 128],
                                            vre2[m][:, cc * 128:(cc + 1) * 128], trf[:],
                                            start=True, stop=True)
                        for cc in range(4):
                            last = e.matmul(psSim[:, cc * 128:(cc + 1) * 128],
                                            vim2[m][:, cc * 128:(cc + 1) * 128], trf[:],
                                            start=True, stop=True)
                        return last
                    t_sc = pr.add("tensor", scfn, waits=[v3, v6, stt["s_free"]], inc="sT")
                    A[("sc", t)] = t_sc
                    rs = []
                    for cc in range(4):
                        o = cc * 128
                        acr = c["acre"][:, cc:cc + 1]
                        aci = c["acim"][:, cc:cc + 1]
                        rs.append(V(lambda e, o=o, acr=acr: e.scalar_tensor_tensor(
                            tmp1[:, o:o + 128], psSre[:, o:o + 128], acr, Are[:, o:o + 128],
                            ALU.add, ALU.mult), [t_sc, stt["xs_free"]] + stt["carry"], relax=True))
                        rs.append(V(lambda e, o=o, aci=aci: e.scalar_tensor_tensor(
                            tmp2[:, o:o + 128], psSim[:, o:o + 128], aci, Aim[:, o:o + 128],
                            ALU.add, ALU.mult), [t_sc] + stt["carry"], relax=True))
                        rs.append(V(lambda e, o=o, acr=acr: e.scalar_tensor_tensor(
                            tmp3[:, o:o + 128], psSre[:, o:o + 128], acr, Aim[:, o:o + 128],
                            ALU.add, ALU.mult), [t_sc] + stt["carry"], relax=True))
                        rs.append(V(lambda e, o=o, aci=aci: e.scalar_tensor_tensor(
                            tmp4[:, o:o + 128], psSim[:, o:o + 128], aci, Are[:, o:o + 128],
                            ALU.add, ALU.mult), [t_sc] + stt["carry"], relax=True))
                    x1 = V(lambda e: e.tensor_sub(xs_re[:], tmp1[:], tmp2[:]), rs, relax=True)
                    x2 = V(lambda e: e.tensor_add(xs_im[:], tmp3[:], tmp4[:]), rs, relax=True)
                    u1 = V(lambda e: e.tensor_add(c["t1"], psSre[:, 127:512:128], c["acre"]), rs)
                    u2 = V(lambda e: e.tensor_add(c["t2"], psSim[:, 127:512:128], c["acim"]), rs)
                    u3 = V(lambda e: e.tensor_mul(c["t3"], c["a128re"], c["t1"]), [u1])
                    u4 = V(lambda e: e.tensor_mul(c["t4"], c["a128im"], c["t2"]), [u2])
                    u5 = V(lambda e: e.tensor_sub(c["acre"], c["t3"], c["t4"]), [u3, u4])
                    u6 = V(lambda e: e.tensor_mul(c["t3"], c["a128im"], c["t1"]), [u5])
                    u7 = V(lambda e: e.tensor_mul(c["t4"], c["a128re"], c["t2"]), [u5])
                    u8 = V(lambda e: e.tensor_add(c["acim"], c["t3"], c["t4"]), [u6, u7])
                    stt["carry"] = [u5, u8]
                    stt["s_free"] = u8

                    def ysfn(e):
                        last = None
                        for half in range(2):
                            for cc in range(4):
                                last = e.matmul(psYS[:, half * 512:(half + 1) * 512],
                                                xs_re[:, cc * 128:(cc + 1) * 128],
                                                CreT[:, cc * 1024 + half * 512: cc * 1024 + (half + 1) * 512],
                                                start=(cc == 0), stop=False)
                            for cc in range(4):
                                last = e.matmul(psYS[:, half * 512:(half + 1) * 512],
                                                xs_im[:, cc * 128:(cc + 1) * 128],
                                                CimNT[:, cc * 1024 + half * 512: cc * 1024 + (half + 1) * 512],
                                                start=False, stop=(cc == 3))
                        return last
                    t_ys = pr.add("tensor", ysfn, waits=[x1, x2] + stt["ys_free"], inc="sT")
                    stt["xs_free"] = t_ys
                    o1 = V(lambda e, xo=xo, m=m: e.scalar_tensor_tensor(
                        xo[:], psYS[:], 1.0, du2[m][:], ALU.mult, ALU.add),
                        [t_ys, g1, stt["free_xout"][bi]], relax=True)
                    o2 = V(lambda e, xo=xo, xt=xt: e.tensor_add(xo[:], xo[:], xt[:]), [o1])
                    stt["ys_free"] = [o2]
                    stt["free_xin"][bi] = o2
                    st = pr.dma("sync", dst[g * 128:(g + 1) * 128, :], xo[:], waits=[o2],
                                sem=f"dY{bi}")
                    stt["free_xout"][bi] = st

                stageA1(0)
                stageA2(0)
                for t in range(TPB):
                    if t + 1 < TPB:
                        stageA1(t + 1)
                    stageB(t)
                    if t + 1 < TPB:
                        stageA2(t + 1)

            pr.add("sync", lambda e: e.nop(),
                   waits=[("x", s, co.counts[s]) for s in
                          ("dL", "dG", "dP", "dX0", "dX1", "dX2", "dX3", "dY0", "dY1")
                          if co.counts[s] > 0])
            endbar = pr.barrier()
            pr.emit()
            return endbar

    def ffn_phase(prev_bar, which):
        src, dst = io[which]
        SG = (1, 1, -1, -1, -1, -1, 1, 1)

        with contextlib.ExitStack() as ps:
            def S(name, shape, dtype):
                return ps.enter_context(nc.sbuf_tensor(which + name, shape, dtype))
            w1b = S("w1b", [128, 8 * 1024], BF16)
            w2b = S("w2b", [128, 8 * 512], BF16)
            w3b = S("w3b", [128, 4 * 256], BF16)
            w4c = S("w4c", [128, 2], BF16)
            gw1b = S("gw1b", [128, 8 * 512], BF16)
            gw2b = S("gw2b", [128, 4 * 256], BF16)
            gw3c = S("gw3c", [128, 2], BF16)
            b1c = S("b1c", [128, 8], F32)
            b2c = S("b2c", [128, 4], F32)
            b3c = S("b3c", [128, 2], F32)
            gb1c = S("gb1c", [128, 4], F32)
            gb2c = S("gb2c", [128, 2], F32)
            rows = S("rows", [1, 16], F32)
            b4h = rows[0:1, 0:1]; gb3h = rows[0:1, 1:2]; arr = rows[0:1, 2:3]
            c1s = rows[0:1, 3:4]; c0s = rows[0:1, 4:5]; gfr = rows[0:1, 5:6]
            rt1 = rows[0:1, 6:7]; rt2 = rows[0:1, 7:8]; snrm = rows[0:1, 8:9]
            rstore = S("rstore", [128, 32], F32)
            cols = S("cols", [128, 8], F32)
            scrap = S("scrap2", [128, 1024], F32)
            sa = [S(f"sta{i}", [128, 8 * 512], F32) for i in range(2)]
            sb = [S(f"stb{i}", [128, 8 * 512], F32) for i in range(2)]
            sB = [S(f"stB{i}", [128, 8 * 512], BF16) for i in range(2)]
            gh1 = S("gh1", [128, 8 * 512], BF16)
            gh2 = S("gh2", [128, 4 * 512], BF16)
            gh3 = S("gh3", [128, 2 * 512], BF16)
            cb = S("cb", [128, 512], F32)
            twt = S("twt", [128, 512], F32)
            grow = S("grow", [1, 512], F32)
            c1col = cols[:, 6:7]
            c0col = cols[:, 7:8]
            xs2 = [S(f"xs2{i}", [128, 1024], F32) for i in range(2)]
            xr = [S(f"xr{i}", [128, 1024], F32) for i in range(2)]
            xio = [S(f"xio{i}", [128, 1024], F32) for i in range(2)]
            xlnf = S("fxlnf", [128, 1024], F32)
            mrow = S("mrow", [1, 1024], F32)
            mcol = S("mcol", [128, 8], F32)
            mcolb = S("mcolb", [128, 8], BF16)
            gcol1 = S("gcol1", [128, 4], F32)
            gcol1b = S("gcol1b", [128, 4], BF16)
            gcol2b = S("gcol2b", [128, 2], BF16)
            trow = S("trow", [1, 512], F32)
            psH = [ps.enter_context(nc.psum_tensor(which + f"psH{i}", [128, 512], F32))
                   for i in range(2)]
            psTf = ps.enter_context(nc.psum_tensor(which + "psTf", [1, 512], F32))
            psCB = ps.enter_context(nc.psum_tensor(which + "psCB", [128, 512], F32))
            psTr = ps.enter_context(nc.psum_tensor(which + "psTr", [128, 1024], F32))
            psG1 = ps.enter_context(nc.psum_tensor(which + "psG1", [1, 512], F32))
            psG2 = ps.enter_context(nc.psum_tensor(which + "psG2", [1, 512], F32))

            pr = co.prog()

            def V(fn, waits, relax=False):
                return pr.add("vector", fn, waits=waits, inc="sV", relax=relax)

            def Cc(fn, waits):
                return pr.add("scalar", fn, waits=waits, inc="sC")

            def T(fn, waits):
                return pr.add("tensor", fn, waits=waits, inc="sT")

            def G(fn, waits):
                return pr.add("gpsimd", fn, waits=waits, inc="sG")

            wl_raw = []
            wl = wl_raw
            wl.append(pr.dma("gpsimd", _wslices_dst(w1b, 8, DM), _wslices_ap(cn_w1, DM, DM), waits=[prev_bar]))
            wl.append(pr.dma("gpsimd", _wslices_dst(w2b, 8, 512), _wslices_ap(cn_w2, DM, 512), waits=[prev_bar]))
            wl.append(pr.dma("gpsimd", _wslices_dst(w3b, 4, 256), _wslices_ap(cn_w3, 512, 256), waits=[prev_bar]))
            wl.append(pr.dma("gpsimd", w4c[:], _col_ap(cn_w4, 256), waits=[prev_bar], slow_ok=True))
            wl.append(pr.dma("gpsimd", _wslices_dst(gw1b, 8, 512), _wslices_ap(gc_w1, DM, 512), waits=[prev_bar]))
            wl.append(pr.dma("gpsimd", _wslices_dst(gw2b, 4, 256), _wslices_ap(gc_w2, 512, 256), waits=[prev_bar]))
            wl.append(pr.dma("gpsimd", gw3c[:], _col_ap(gc_w3, 256), waits=[prev_bar], slow_ok=True))
            wl = [wl_raw[-1]]
            bl = []
            for dstb, srcb, n in ((b1c, cn_b1, DM), (b2c, cn_b2, 512), (b3c, cn_b3, 256),
                                  (gb1c, gc_b1, 512), (gb2c, gc_b2, 256)):
                bl.append(pr.dma("sync", dstb[:], _col_ap(srcb, n), waits=[prev_bar],
                                 slow_ok=True, sem="dP"))
            lb4 = pr.dma("sync", b4h[:], _row_ap(cn_b4, 1), waits=[prev_bar], slow_ok=True, sem="dP")
            lg3 = pr.dma("sync", gb3h[:], _row_ap(gc_b3, 1), waits=[prev_bar], slow_ok=True, sem="dP")
            lar = pr.dma("sync", arr[:], _row_ap(a_range, 1), waits=[prev_bar], slow_ok=True, sem="dP")
            dp_all = lar
            bl = [dp_all]
            h1 = V(lambda e: e.tensor_scalar(b4h, b4h, 0.5, None, ALU.mult), [dp_all])
            h2 = V(lambda e: e.tensor_scalar(gb3h, gb3h, 0.5, None, ALU.mult), [dp_all])
            h3 = V(lambda e: e.tensor_scalar(c1s, arr, 0.3, None, ALU.mult), [dp_all])

            tok_out_final = []
            for b in range(B_PER_CORE):
                bstart = tok_out_final[-1] if tok_out_final else prev_bar
                free_x = [bstart, bstart]
                t_cs_last = None
                st_toks = []
                for t in range(TPB):
                    g = b * TPB + t
                    bi = t % 2
                    xt = xio[bi]
                    ld = pr.dma("sync", xt[:], src[g * 128:(g + 1) * 128, :], waits=[free_x[bi]],
                                sem=f"dX{bi}")
                    rcol = rstore[:, 2 * t:2 * t + 1]
                    ncol = rstore[:, 2 * t + 1:2 * t + 2]
                    s_tok = ln_stats(pr, xt, cols, rcol, ncol, scrap, ld)
                    def csfn(e, xt=xt, rcol=rcol, ncol=ncol, t=t):
                        e.matmul(psG1[:], rcol, xt[:, 0:512], start=(t == 0), stop=(t == TPB - 1))
                        e.matmul(psG2[:], rcol, xt[:, 512:1024], start=(t == 0), stop=(t == TPB - 1))
                        return e.matmul(psTf[0:1, 0:1], ncol, ones_col[:, 0:1],
                                        start=(t == 0), stop=(t == TPB - 1))
                    t_cs = T(csfn, [s_tok, prev_bar if t < 2 else None,
                                    tok_out_final[-1] if (t < 2 and tok_out_final) else None])
                    free_x[bi] = t_cs
                    t_cs_last = t_cs
                m1 = V(lambda e: e.tensor_copy(snrm, psTf[0:1, 0:1]), [t_cs_last])
                m2 = V(lambda e: e.tensor_scalar(mrow[0:1, 0:512], psG1[:], snrm, 1.0 / TSEQ,
                                                 ALU.add, ALU.mult), [m1])
                m3 = V(lambda e: e.tensor_scalar(mrow[0:1, 512:1024], psG2[:], snrm, 1.0 / TSEQ,
                                                 ALU.add, ALU.mult), [m1])
                m4a = pr.dma("sync", rcscr[0:1, :], mrow[0:1, :], waits=[m2, m3], sem="dP")
                m4 = pr.dma("sync", mcol[:], _col_ap(rcscr, DM), waits=[m4a], slow_ok=True, sem="dP")
                gfb0 = pr.barrier()
                m5 = V(lambda e: e.tensor_copy(mcolb[:], mcol[:]), [m4, gfb0])
                def g1fn(e):
                    last = None
                    for j in range(4):
                        for k in range(8):
                            last = e.matmul(psH[0][:, j:j + 1],
                                            gw1b[:, k * 512 + j * 128:k * 512 + (j + 1) * 128],
                                            mcolb[:, k:k + 1], start=(k == 0), stop=(k == 7))
                    return last
                tg1 = T(g1fn, [m5] + wl)
                ge1 = []
                for j in range(4):
                    ge1.append(Cc(lambda e, j=j: e.activation(gcol1[:, j:j + 1], psH[0][:, j:j + 1],
                                                              AF.Gelu, bias=gb1c[:, j:j + 1]),
                                  [tg1] + bl))
                gc1 = V(lambda e: e.tensor_copy(gcol1b[:], gcol1[:]), ge1)
                def g2fn(e):
                    last = None
                    for j in range(2):
                        for k in range(4):
                            last = e.matmul(psH[1][:, j:j + 1],
                                            gw2b[:, k * 256 + j * 128:k * 256 + (j + 1) * 128],
                                            gcol1b[:, k:k + 1], start=(k == 0), stop=(k == 3))
                    return last
                tg2 = T(g2fn, [gc1])
                ge2 = []
                for j in range(2):
                    ge2.append(Cc(lambda e, j=j: e.activation(scrap[:, j:j + 1], psH[1][:, j:j + 1],
                                                              AF.Gelu, bias=gb2c[:, j:j + 1]), [tg2]))
                gc2 = V(lambda e: e.tensor_copy(gcol2b[:], scrap[:, 0:2]), ge2)
                def g3fn(e):
                    last = None
                    for k in range(2):
                        last = e.matmul(psTf[0:1, 1:2], gw3c[:, k:k + 1], gcol2b[:, k:k + 1],
                                        start=(k == 0), stop=(k == 1))
                    return last
                tg3 = T(g3fn, [gc2])
                dpre = V(lambda e: e.tensor_copy(rows[0:1, 9:10], psTf[0:1, 1:2]), [tg3])
                tgf = Cc(lambda e: e.activation(gfr, psTf[0:1, 1:2], AF.Tanh, scale=0.5,
                                                bias=gb3h), [tg3, h2, dpre])
                x1_ = V(lambda e: e.tensor_scalar(gfr, gfr, 0.5, 0.5, ALU.mult, ALU.add), [tgf])
                x2_ = V(lambda e: e.tensor_mul(rt1, gfr, arr), [x1_, h3])
                x3_ = V(lambda e: e.tensor_scalar(rt2, arr, -0.7, None, ALU.mult), [x2_])
                x4_ = V(lambda e: e.tensor_scalar(rt1, rt1, 1.4, 0.1, ALU.mult, ALU.add), [x3_])
                x5_ = V(lambda e: e.tensor_add(c0s, rt1, rt2), [x4_])
                gfb1 = pr.barrier()
                x5_ = gfb1
                tcc = T(lambda e: e.matmul(psH[0][:, 0:2], ones_row[0:1, 0:128],
                                           rows[0:1, 3:5], start=True, stop=True), [x5_])
                ecc = V(lambda e: e.tensor_copy(cols[:, 6:8], psH[0][:, 0:2]), [tcc])
                if debug and b == 0:
                    pr.dma("sync", dbg["f_mrow"][:, :], mrow[:], waits=[x5_])
                    pr.dma("sync", dbg["f_mcol"][:, :], mcol[:], waits=[x5_])
                    pr.dma("sync", dbg["f_gcol1"][:, :], gcol1[:], waits=[x5_])
                    pr.dma("sync", dbg["f_rows"][:, :], rows[:], waits=[x5_])
                    pr.dma("sync", dbg["f_rstore"][:, :], rstore[:], waits=[x5_])

                slot2_free = [x5_, x5_]
                xr_free = [x5_, x5_]
                free_state = [[x5_], [x5_]]
                free_psTr = x5_
                prev_casc_end = [ecc]
                cb_free = ecc
                psTf_free = x5_
                trow_free = x5_
                grow_free = x5_
                xlnf_free = x5_
                for cpair in range(2):
                    setups = []
                    rc0s = []
                    for m in range(2):
                        cc2 = cpair * 2 + m
                        base_t = b * TPB + cc2 * 4
                        setup_ev = []
                        for ti in range(4):
                            g = base_t + ti
                            xslot = xs2[ti % 2]
                            ld = pr.dma("sync", xslot[:], src[g * 128:(g + 1) * 128, :],
                                        waits=[slot2_free[ti % 2]], sem=f"dX{ti % 2}")
                            t_idx = cc2 * 4 + ti
                            nrm = Cc(lambda e, xslot=xslot, t_idx=t_idx: e.activation(
                                xlnf[:], xslot[:], AF.Identity,
                                bias=rstore[:, 2 * t_idx + 1:2 * t_idx + 2],
                                scale=rstore[:, 2 * t_idx:2 * t_idx + 1]),
                                [ld, setup_ev[-1] if setup_ev else xlnf_free])
                            slot2_free[ti % 2] = nrm
                            def trfn(e):
                                last = None
                                for j in range(8):
                                    last = e.transpose(psTr[:, j * 128:(j + 1) * 128],
                                                       xlnf[:, j * 128:(j + 1) * 128], idf[:])
                                return last
                            tk = T(trfn, [nrm, free_psTr])
                            def evfn(e, ti=ti, m=m):
                                return e.tensor_copy(
                                    sa[m][:, :].rearrange("p (j q) -> p j q", j=8)[:, :, ti * 128:(ti + 1) * 128],
                                    psTr[:, :].rearrange("p (j q) -> p j q", j=8))
                            ev = V(evfn, [tk] + free_state[m], relax=True)
                            setup_ev.append(ev)
                            free_psTr = ev
                            xlnf_free = ev
                        rc0s.append(V(lambda e, m=m: e.tensor_copy(sB[m][:], sa[m][:]),
                                      setup_ev, relax=True))
                        setups.append(setup_ev)

                    cyc_tok = [rc0s[0], rc0s[1]]
                    tw_last = [setups[0], setups[1]]
                    for c in range(3):
                        for m in range(2):
                            cur = sa[m] if c % 2 == 0 else sb[m]
                            nxt = sb[m] if c % 2 == 0 else sa[m]
                            ev1 = []
                            for j in range(8):
                                def h1fn(e, j=j, m=m):
                                    last = None
                                    for k in range(8):
                                        last = e.matmul(psH[j % 2][:],
                                                        w1b[:, k * 1024 + j * 128:k * 1024 + (j + 1) * 128],
                                                        sB[m][:, k * 512:(k + 1) * 512],
                                                        start=(k == 0), stop=(k == 7))
                                    return last
                                mm = T(h1fn, [cyc_tok[m],
                                              ev1[-2] if len(ev1) >= 2 else (prev_casc_end[-1] if j < 2 else None)]
                                       + (wl if (c == 0 and m == 0 and j == 0) else []))
                                ev1.append(Cc(lambda e, j=j: e.activation(
                                    gh1[:, j * 512:(j + 1) * 512], psH[j % 2][:], AF.Gelu,
                                    bias=b1c[:, j:j + 1]), [mm]))
                            ev2 = []
                            for j in range(4):
                                def h2fn(e, j=j):
                                    last = None
                                    for k in range(8):
                                        last = e.matmul(psH[j % 2][:],
                                                        w2b[:, k * 512 + j * 128:k * 512 + (j + 1) * 128],
                                                        gh1[:, k * 512:(k + 1) * 512],
                                                        start=(k == 0), stop=(k == 7))
                                    return last
                                mm = T(h2fn, [ev1[-1] if j == 0 else None,
                                              ev2[-2] if len(ev2) >= 2 else (ev1[-2] if j == 0 else None)])
                                ev2.append(Cc(lambda e, j=j: e.activation(
                                    gh2[:, j * 512:(j + 1) * 512], psH[j % 2][:], AF.Gelu,
                                    bias=b2c[:, j:j + 1]), [mm]))
                            ev3 = []
                            for j in range(2):
                                def h3fn(e, j=j):
                                    last = None
                                    for k in range(4):
                                        last = e.matmul(psH[j % 2][:],
                                                        w3b[:, k * 256 + j * 128:k * 256 + (j + 1) * 128],
                                                        gh2[:, k * 512:(k + 1) * 512],
                                                        start=(k == 0), stop=(k == 3))
                                    return last
                                mm = T(h3fn, [ev2[-1] if j == 0 else None,
                                              ev2[-2] if j == 0 else None])
                                ev3.append(Cc(lambda e, j=j: e.activation(
                                    gh3[:, j * 512:(j + 1) * 512], psH[j % 2][:], AF.Gelu,
                                    bias=b3c[:, j:j + 1]), [mm]))
                            def tffn(e):
                                last = None
                                for k in range(2):
                                    last = e.matmul(psTf[0:1, 0:512], w4c[:, k:k + 1],
                                                    gh3[:, k * 512:(k + 1) * 512],
                                                    start=(k == 0), stop=(k == 1))
                                return last
                            ttf = T(tffn, [ev3[-1], ev3[0], psTf_free])
                            ttr = Cc(lambda e: e.activation(trow[:], psTf[0:1, 0:512], AF.Tanh,
                                                            scale=0.5, bias=b4h),
                                     [ttf, h1, trow_free])
                            psTf_free = ttr
                            tgr = V(lambda e: e.tensor_scalar(grow[:], trow[:], c1s, c0s,
                                                              ALU.mult, ALU.add),
                                    [ttr, x5_, grow_free])
                            trow_free = tgr
                            tcb = T(lambda e: e.matmul(psCB[:], ones_row[:], grow[:],
                                                       start=True, stop=True), [tgr, cb_free])
                            grow_free = tcb
                            ecb = V(lambda e: e.tensor_copy(cb[:], psCB[:]), [tcb, cb_free],
                                    relax=True)
                            if debug and b == 0 and cpair == 0 and c == 0 and m == 0:
                                pr.dma("sync", dbg["d_dbc"][:, 0:8], cols[:, 0:8], waits=[ecb])
                                pr.dma("sync", dbg["d_vre"][0:1, 0:512], trow[:], waits=[ecb])
                                pr.dma("sync", dbg["d_du"][:, 0:512], cb[:], waits=[ecb])
                            tw_new = []
                            t1_last = None
                            for i in range(8):
                                si = SG[i]
                                sj = (i + 4) % 8
                                t1 = V(lambda e, sj=sj, cur=cur: e.tensor_mul(
                                    twt[:], cb[:], cur[:, sj * 512:(sj + 1) * 512]),
                                    [ecb, cyc_tok[m]], relax=(i > 0))
                                t1_last = t1
                                op = ALU.add if si > 0 else ALU.subtract
                                t2 = V(lambda e, i=i, cur=cur, nxt=nxt, op=op: e.tensor_tensor(
                                    nxt[:, i * 512:(i + 1) * 512], cur[:, i * 512:(i + 1) * 512],
                                    twt[:], op), [t1])
                                tw_new.append(t2)
                            cb_free = t1_last
                            tw_last[m] = tw_new
                            prev_casc_end = [ev3[-1]]
                            if c < 2:
                                cyc_tok[m] = V(lambda e, m=m, nxt=nxt: e.tensor_copy(
                                    sB[m][:], nxt[:]), tw_new + [ev1[-1]], relax=True)
                            else:
                                cyc_tok[m] = None
                    for m in range(2):
                        cc2 = cpair * 2 + m
                        base_t = b * TPB + cc2 * 4
                        ad_last = None
                        for ti in range(4):
                            g = base_t + ti
                            bi = ti % 2
                            xo = xio[bi]
                            ldr = pr.dma("sync", xr[bi][:], src[g * 128:(g + 1) * 128, :],
                                         waits=[xr_free[bi]], sem=f"dX{2 + bi}")
                            def trfn2(e, ti=ti, m=m):
                                last = None
                                for j in range(8):
                                    last = e.transpose(psTr[:, j * 128:(j + 1) * 128],
                                                       sb[m][:, j * 512 + ti * 128:j * 512 + (ti + 1) * 128],
                                                       idf[:])
                                return last
                            tk = T(trfn2, tw_last[m] + [free_psTr])
                            ad = V(lambda e, xo=xo, bi=bi: e.tensor_add(
                                xo[:], psTr[:], xr[bi][:]),
                                [tk, ldr, st_toks[-2] if len(st_toks) >= 2 else None], relax=True)
                            st = pr.dma("sync", dst[g * 128:(g + 1) * 128, :], xo[:], waits=[ad],
                                        sem=f"dY{bi}")
                            st_toks.append(st)
                            xr_free[bi] = ad
                            free_psTr = ad
                            ad_last = ad
                        free_state[m] = [ad_last]
                tok_out_final = [st_toks[-1]]


            pr.add("sync", lambda e: e.nop(),
                   waits=[("x", s, co.counts[s]) for s in
                          ("dL", "dG", "dP", "dX0", "dX1", "dX2", "dX3", "dY0", "dY1")
                          if co.counts[s] > 0])
            endbar = pr.barrier()
            pr.emit()
            return endbar

    def att_phase(prev_bar):
        src, dst = io["att"]
        with contextlib.ExitStack() as ps:
            def S(name, shape, dtype):
                return ps.enter_context(nc.sbuf_tensor("a" + name, shape, dtype))
            wab = S("wab", [128, 8 * 3072], BF16)
            wp64 = S("wp64", [64, 16 * 1024], BF16)
            baq = S("baq", [128, 24], F32)
            bvbc = S("bvbc", [128, 1024], BF16)
            hs = S("hs", [128, 2], F32)
            chsb = S("chsb", [8, 2], F32)
            cbias = S("cbias", [128, 16], F32)
            kT = S("kT", [128, 8 * 2048], BF16)
            vst = S("vst", [128, 16 * 1040], BF16)
            qT = S("qT", [128, 8 * 512], BF16)
            yb64 = S("yb64", [128, 16 * 512], BF16)
            bprow = yb64[64:65, 0:1024]
            obr = yb64[64:65, 1024:1152]
            xlnT = S("xlnT", [128, 8 * 512], BF16)
            xin2 = [S(f"xin{i}", [128, 1024], F32) for i in range(2)]
            xio = [S(f"xio{i}", [128, 1024], F32) for i in range(2)]
            xln4 = S("xln4", [128, 4096], BF16)
            scrap = None
            cols = S("cols", [128, 8], F32)
            ebuf = [xlnT[:, i * 512:(i + 1) * 512] for i in range(5)]
            zbc = S("zbc", [64, 512], F32)
            zrow = zbc[0:1, 0:512]
            zinv = zrow
            psA = ps.enter_context(nc.psum_tensor("apsA", [128, 512], F32))
            psB = ps.enter_context(nc.psum_tensor("apsB", [128, 512], F32))
            psC = ps.enter_context(nc.psum_tensor("apsC", [128, 512], F32))
            psD = ps.enter_context(nc.psum_tensor("apsD", [128, 512], F32))
            psV = ps.enter_context(nc.psum_tensor("apsV", [128, 1024], F32))
            psY = ps.enter_context(nc.psum_tensor("apsY", [65, 512], F32))
            psZ = psV[0:64, 0:512]
            psTb = ps.enter_context(nc.psum_tensor("apsTb", [128, 1024], BF16))

            pr = co.prog()

            def V(fn, waits, relax=False):
                return pr.add("vector", fn, waits=waits, inc="sV", relax=relax)

            def Cc(fn, waits):
                return pr.add("scalar", fn, waits=waits, inc="sC")

            def T(fn, waits):
                return pr.add("tensor", fn, waits=waits, inc="sT")

            wl0 = pr.dma("gpsimd", _wslices_dst(wab, 8, 3072), _wslices_ap(w_attn, DM, 3 * DM),
                         waits=[prev_bar])
            wl1 = pr.dma("gpsimd", wp64[:, :].rearrange("d (h o) -> d h o", h=16),
                         w_proj.rearrange("(h d) o -> d h o", d=64), waits=[prev_bar])
            wl = [wl1]
            l1 = pr.dma("sync", baq[:], _col_ap(b_attn, 3 * DM), waits=[prev_bar],
                        slow_ok=True, sem="dP")
            bvrow = xio[0][0:1, 0:1024]
            chrow = xio[1][0:1, 0:16]
            l2 = pr.dma("sync", bvrow, b_attn[2048:3072].rearrange("(one n) -> one n", one=1),
                        waits=[prev_bar], sem="dP")
            l3 = pr.dma("gpsimd", bprow[:], _row_ap(b_proj, DM), waits=[prev_bar])
            wl = [l3]
            dp_all = l2

            hz = []
            hz.append(V(lambda e: e.memset(obr, 1.0), [prev_bar]))
            hz.append(V(lambda e: e.memset(hs[0:64, 0:1], 1.0), [prev_bar]))
            hz.append(V(lambda e: e.memset(hs[64:128, 0:1], 0.0), [prev_bar]))
            hz.append(V(lambda e: e.memset(hs[0:64, 1:2], 0.0), [prev_bar]))
            hz.append(V(lambda e: e.memset(hs[64:128, 1:2], 1.0), [prev_bar]))
            tq = V(lambda e: e.tensor_mul(baq[:, 16:24], baq[:, 0:8], baq[:, 8:16]), [dp_all])
            tm1 = T(lambda e: e.matmul(psY[0:8, 0:2], baq[:, 16:24], hs[:, 0:2],
                                       start=True, stop=True), [tq] + hz)
            tm2 = V(lambda e: e.tensor_scalar(chsb[:], psY[0:8, 0:2], -0.125, None, ALU.mult), [tm1])
            tm3 = pr.dma("sync", rcscr[0:1, 0:16].rearrange("one (a b) -> (one a) b", a=8),
                         chsb[0:8, 0:2], waits=[tm2], sem="dP", slow_ok=True)
            tm4 = pr.dma("sync", chrow, rcscr[0:1, 0:16], waits=[tm3], sem="dP")
            tm5 = T(lambda e: e.matmul(psA[:, 0:16], ones_row[0:1, 0:128], chrow,
                                       start=True, stop=True), [tm4])
            tm5b = T(lambda e: e.matmul(psV[:, 0:512], ones_row[:], xio[0][0:1, 0:512],
                                        start=True, stop=True), [dp_all, tm5])
            tm5c = T(lambda e: e.matmul(psV[:, 512:1024], ones_row[:], xio[0][0:1, 512:1024],
                                        start=True, stop=True), [tm5b])
            tm6 = V(lambda e: e.tensor_copy(cbias[:], psA[:, 0:16]), [tm5])
            tm7 = tm6
            tm8 = V(lambda e: e.tensor_copy(bvbc[:], psV[:]), [tm5c])
            tm9 = V(lambda e: e.memset(vst[:, 64:16 * 1040:65], 1.0), [prev_bar])
            setup_done = pr.barrier()

            st_toks = []
            last_smm_prev_chunk = setup_done
            last_vmm_prev_chunk = prev_bar
            last_ye_prev_chunk = prev_bar
            last_proj_prev_chunk = setup_done
            prev_exp_chunk = [setup_done]
            slot_free = [prev_bar, prev_bar]
            free_xio = [setup_done, setup_done]
            tok_psTb_free = prev_bar
            prev_expD = setup_done
            pfx = None
            NCH = B_PER_CORE * 4
            for gc in range(NCH):
                b = gc // 4
                cc = gc % 4
                base_g = gc * 4
                ev_xlnT = []
                qkv_x_free = [None]
                if pfx is None:
                    lntoks = []
                    for ti in range(4):
                        g = base_g + ti
                        xslot = xin2[ti % 2]
                        ld = pr.dma("sync", xslot[:], src[g * 128:(g + 1) * 128, :],
                                    waits=[slot_free[ti % 2]], sem=f"dX{ti % 2}")
                        xl = xln4[:, ti * 1024:(ti + 1) * 1024]
                        tok_xln, _, _, _ = ln_tile(pr, xslot, xl, cols, xl, ld)
                        slot_free[ti % 2] = tok_xln
                        qkv_x_free.append(tok_xln)
                        lntoks.append(tok_xln)
                else:
                    lntoks = pfx
                for ti in range(4):
                    def trfn(e, ti=ti):
                        last = None
                        for j in range(8):
                            last = e.transpose(psTb[:, j * 128:(j + 1) * 128],
                                               xln4[:, ti * 1024 + j * 128:
                                                     ti * 1024 + (j + 1) * 128], idb[:])
                        return last
                    tk = T(trfn, [lntoks[ti], tok_psTb_free])
                    def evfn(e, ti=ti):
                        return e.tensor_copy(
                            xlnT[:, :].rearrange("p (j q) -> p j q", j=8)[:, :, ti * 128:(ti + 1) * 128],
                            psTb[:, :].rearrange("p (j q) -> p j q", j=8))
                    ev = V(evfn, [tk, last_vmm_prev_chunk if ti == 0 else None,
                                  last_ye_prev_chunk if ti == 0 else None], relax=True)
                    ev_xlnT.append(ev)
                    tok_psTb_free = ev
                qk_ev = []
                for part, (coff, dstT, dst_off) in enumerate((
                        (0, qT, lambda j, cc=cc: (j * 512, 512)),
                        (1024, kT, lambda j, cc=cc: (j * 2048 + cc * 512, 512)))):
                    for j in range(8):
                        idx = part * 8 + j
                        psX = (psA, psB, psC)[idx % 3]
                        def qkfn(e, coff=coff, j=j, psX=psX):
                            last = None
                            for k in range(8):
                                last = e.matmul(psX[:],
                                                wab[:, k * 3072 + coff + j * 128:
                                                    k * 3072 + coff + (j + 1) * 128],
                                                xlnT[:, k * 512:(k + 1) * 512],
                                                start=(k == 0), stop=(k == 7))
                            return last
                        mm = T(qkfn, ev_xlnT + [wl0 if (part == 0 and j == 0) else None,
                                                qk_ev[-3] if len(qk_ev) >= 3 else None]
                               + (prev_exp_chunk if idx < 3 else []))
                        off, width = dst_off(j)
                        qk_ev.append(Cc(lambda e, dstT=dstT, off=off, width=width, j=j,
                                        part=part, psX=psX: e.activation(
                            dstT[:, off:off + width], psX[:], AF.Identity,
                            bias=baq[:, 8 * part + j:8 * part + j + 1]), [mm, dp_all]))
                v_ev = []
                for ti in range(4):
                    kt_g = cc * 4 + ti
                    def vfn(e, ti=ti):
                        last = None
                        for half in range(2):
                            for k in range(8):
                                last = e.matmul(
                                    psV[:, half * 512:(half + 1) * 512],
                                    xlnT[:, k * 512 + ti * 128:k * 512 + (ti + 1) * 128],
                                    wab[:, k * 3072 + 2048 + half * 512:
                                        k * 3072 + 2048 + (half + 1) * 512],
                                    start=(k == 0), stop=(k == 7))
                        return last
                    mm = T(vfn, ev_xlnT + [v_ev[-1] if v_ev else tm8,
                                           last_proj_prev_chunk if ti == 0 else None])
                    last_vmm = mm
                    def vefn(e, kt_g=kt_g):
                        return e.scalar_tensor_tensor(
                            vst[:, kt_g * 1040:kt_g * 1040 + 1040]
                            .rearrange("p (h c) -> p h c", h=16)[:, :, 0:64],
                            psV[:, :].rearrange("p (h c) -> p h c", h=16),
                            1.0, bvbc[:, :].rearrange("p (h c) -> p h c", h=16),
                            ALU.mult, ALU.add)
                    v_ev.append(V(vefn, [mm, tm9], relax=True))
                if debug and b == 0 and cc == 0:
                    pr.dma("sync", dbg["a_qT"][:, :], qT[:], waits=[qk_ev[-1]])
                    pr.dma("sync", dbg["a_kT"][:, :], kT[:, 0:512], waits=[qk_ev[-1]])
                    pr.dma("sync", dbg["a_vst"][:, :], vst[:, 0:1040], waits=[v_ev[-1]])
                    pr.add("sync", lambda e: e.nop(), waits=[("x", "dL", co.counts["dL"])])
                    dbgbar = pr.barrier()
                    last_smm_prev_chunk = dbgbar
                    last_proj_prev_chunk = dbgbar
                    prev_exp_chunk = [dbgbar]
                nkt = cc * 4 + 4
                ebn = 5
                new_pfx = []
                stt_ = {"last_yevac": last_proj_prev_chunk, "prev_z4": None,
                        "pend": None}
                prev_exp = [None, None, None, prev_expD]
                ytok = {}
                last_smm = None
                for h in range(16):
                    hp = h % 2
                    hg = h // 2

                    def emit_y(kt, h=h, nkt=nkt, cc=cc):
                        delta = kt - cc * 4
                        o = max(delta, 0) * 128
                        def yfn(e, kt=kt, h=h, nkt=nkt, o=o):
                            return e.matmul(psY[:, o:512],
                                            vst[:, kt * 1040 + h * 65:kt * 1040 + (h + 1) * 65],
                                            ebuf[kt % ebn][:, o:512],
                                            start=(kt == 0), stop=(kt == nkt - 1))
                        w = [ytok[(h, kt, "m")]]
                        if kt == 0:
                            w.append(stt_["last_yevac"])
                        t = T(yfn, w)
                        ytok[(h, kt, "y")] = t
                        return t

                    for kt in range(nkt):
                        pp = kt % 4
                        pe = kt % ebn
                        delta = kt - cc * 4
                        o = max(delta, 0) * 128
                        psS = (psA, psB, psC, psD)[pp]
                        def sfn(e, kt=kt, psS=psS, hp=hp, hg=hg, o=o):
                            return e.matmul(
                                psS[:, o:512],
                                kT[64 * hp:64 * hp + 64, hg * 2048 + kt * 128:
                                   hg * 2048 + (kt + 1) * 128],
                                qT[64 * hp:64 * hp + 64, hg * 512 + o:(hg + 1) * 512],
                                start=True, stop=True)
                        smm = T(sfn, [qk_ev[-1] if (h == 0 and kt == 0) else None,
                                      v_ev[-1] if (h == 0 and kt < 3) else None,
                                      prev_exp[pp]])
                        last_smm = smm
                        gk = h * nkt + kt - ebn
                        ebfree = ytok.get((gk // nkt, gk % nkt, "y")) if gk >= 0 else None
                        ex = Cc(lambda e, pe=pe, h=h, psS=psS, o=o: e.activation(
                            ebuf[pe][:, o:512], psS[:, o:512], AF.Exp, scale=0.125,
                            bias=cbias[:, h:h + 1]), [smm, tm7, ebfree])
                        mask_tok = ex
                        if delta >= 0:
                            mask_tok = V(lambda e, pe=pe, delta=delta: e.tensor_mul(
                                ebuf[pe][:, delta * 128:(delta + 1) * 128],
                                ebuf[pe][:, delta * 128:(delta + 1) * 128],
                                trm[:]), [ex], relax=True)
                        prev_exp[pp] = mask_tok
                        ytok[(h, kt, "m")] = mask_tok
                        if kt == 3 and stt_["pend"] is not None:
                            stt_["pend"]()
                            stt_["pend"] = None
                        if kt >= 4:
                            emit_y(kt - 4)
                    emit_y(nkt - 4)
                    emit_y(nkt - 3)
                    emit_y(nkt - 2)
                    y_last = emit_y(nkt - 1)
                    z2 = Cc(lambda e: e.activation(zinv, psY[64:65, :], AF.Ln),
                            [y_last, stt_["prev_z4"]])
                    z3 = Cc(lambda e: e.activation(zinv, zinv, AF.Exp, scale=-1.0), [z2])

                    def mk_fin(h=h, z3=z3, y_last=y_last, prev_ye=stt_["last_yevac"]):
                        def fin():
                            z4 = T(lambda e: e.matmul(psZ, ones_row[0:1, 0:64], zinv,
                                                      start=True, stop=True),
                                   [z3, prev_ye])
                            z5 = V(lambda e: e.tensor_copy(zbc[:], psZ), [z4])
                            ye = V(lambda e, h=h: e.tensor_mul(
                                yb64[0:64, h * 512:(h + 1) * 512],
                                psY[0:64, :], zbc[:]), [z5, y_last], relax=True)
                            stt_["last_yevac"] = ye
                            stt_["prev_z4"] = ye
                        return fin
                    stt_["pend"] = mk_fin()
                    if h == 15:
                        stt_["pend"]()
                        stt_["pend"] = None
                    if gc + 1 < NCH and h in (1, 3, 5, 7):
                        ti_e = (h - 1) // 2
                        bi_e = ti_e % 2
                        g2 = (gc + 1) * 4 + ti_e
                        eld = pr.dma("sync", xio[bi_e][:],
                                     src[g2 * 128:(g2 + 1) * 128, :],
                                     waits=[free_xio[bi_e]], sem=f"dX{2 + bi_e}")
                        xl = xln4[:, ti_e * 1024:(ti_e + 1) * 1024]
                        tokE, _, _, _ = ln_tile(pr, xio[bi_e], xl, cols, xl, eld,
                                                wait_buf=[ev_xlnT[ti_e]])
                        new_pfx.append(tokE)
                        free_xio[bi_e] = tokE
                last_yevac = stt_["last_yevac"]
                proj_last = None
                proj_half_free = [None, None]
                for ti in range(4):
                    g = base_g + ti
                    bi = ti % 2
                    xo = xio[bi]
                    ldr = pr.dma("sync", xin2[bi][:], src[g * 128:(g + 1) * 128, :],
                                 waits=[slot_free[bi], qkv_x_free[-1]], sem=f"dX{bi}")
                    ad = None
                    for half in range(2):
                        def pfn(e, ti=ti, half=half):
                            for h in range(16):
                                e.matmul(psV[:, half * 512:(half + 1) * 512],
                                         yb64[0:64, h * 512 + ti * 128:
                                              h * 512 + (ti + 1) * 128],
                                         wp64[0:64, h * 1024 + half * 512:
                                              h * 1024 + (half + 1) * 512],
                                         start=(h == 0), stop=False)
                            return e.matmul(psV[:, half * 512:(half + 1) * 512],
                                            obr,
                                            bprow[0:1, half * 512:(half + 1) * 512],
                                            start=False, stop=True)
                        mmp = T(pfn, [last_yevac if ti == 0 else None,
                                      v_ev[-1] if ti == 0 else None,
                                      proj_half_free[half],
                                      wl[-1] if not st_toks else None,
                                      st_toks[-2] if (half == 0 and len(st_toks) >= 2) else None])
                        ad = V(lambda e, xo=xo, bi=bi, half=half: e.tensor_add(
                            xo[:, half * 512:(half + 1) * 512],
                            psV[:, half * 512:(half + 1) * 512],
                            xin2[bi][:, half * 512:(half + 1) * 512]),
                            [mmp, ldr if half == 0 else None, free_xio[bi]], relax=True)
                        proj_half_free[half] = ad
                    st = pr.dma("sync", dst[g * 128:(g + 1) * 128, :], xo[:],
                                waits=[ad], sem=f"dY{bi}")
                    st_toks.append(st)
                    free_xio[bi] = st
                    slot_free[bi] = ad
                    proj_last = ad
                pfx = new_pfx if len(new_pfx) == 4 else None
                last_smm_prev_chunk = last_smm
                last_vmm_prev_chunk = last_vmm
                last_ye_prev_chunk = last_yevac
                last_proj_prev_chunk = proj_last
                prev_expD = prev_exp[3]
                prev_exp_chunk = [t for t in prev_exp[0:3] if t is not None]

            pr.add("sync", lambda e: e.nop(),
                   waits=[("x", s, co.counts[s]) for s in
                          ("dL", "dG", "dP", "dX0", "dX1", "dX2", "dX3", "dY0", "dY1")
                          if co.counts[s] > 0])
            endbar = pr.barrier()
            pr.emit()
            return endbar

    cur_bar = bar0
    if "ssm" in parts:
        cur_bar = ssm_phase(cur_bar)
    if "ffn1" in parts:
        cur_bar = ffn_phase(cur_bar, "ffn1")
    if "att" in parts:
        cur_bar = att_phase(cur_bar)
    if "ffn2" in parts:
        cur_bar = ffn_phase(cur_bar, "ffn2")

    pf = co.prog()
    pf.add("sync", lambda e: e.nop(),
           waits=[cur_bar] + [("x", s, co.counts[s]) for s in
                              ("dL", "dG", "dP", "dX0", "dX1", "dX2", "dX3", "dY0", "dY1")
                              if co.counts[s] > 0])
    pf.emit()
    es.close()
    return nc


def _make_consts():
    iden = np.eye(128)
    tri = (np.arange(128)[:, None] <= np.arange(128)[None, :]).astype(np.float32)
    return {
        "c_identb": iden.astype(ml_dtypes.bfloat16),
        "c_identf": iden.astype(np.float32),
        "c_triuf": tri,
        "c_trium": tri.astype(ml_dtypes.bfloat16),
    }


def kernel(**inputs):
    parts = ("ssm", "ffn1", "att", "ffn2")
    nc = build_program(parts)
    consts = _make_consts()
    x = np.ascontiguousarray(np.asarray(inputs["L0_attn_a_u"], dtype=np.float32))
    in_maps = []
    for core in range(NCORES):
        m = {"x": x[2 * core:2 * core + 2].reshape(NT, DM)}
        for k, v in inputs.items():
            if k == "L0_attn_a_u":
                continue
            m[k] = np.ascontiguousarray(np.asarray(v, dtype=np.float32))
        m.update(consts)
        in_maps.append(m)
    res = run_bass_kernel_spmd(nc, in_maps, core_ids=list(range(NCORES)))
    out = np.concatenate([np.asarray(r["y"]).reshape(2, TSEQ, DM) for r in res.results], axis=0)
    return out


# version 17 (final)
# speedup vs baseline: 1.0166x; 1.0077x over previous
import contextlib
import math

import numpy as np
import ml_dtypes

import concourse.bass as bass
import concourse.mybir as mybir
from concourse.bass_utils import run_bass_kernel_spmd

F32 = mybir.dt.float32
BF16 = mybir.dt.bfloat16
AF = mybir.ActivationFunctionType
ALU = mybir.AluOpType

NCORES = 8
B_PER_CORE = 2
TSEQ = 2048
DM = 1024
PSSM = 512
NT = B_PER_CORE * TSEQ
NTILE = NT // 128
TPB = TSEQ // 128
NHEAD = 16
HDIM = 64
EPS = 1e-5

ENGINES = ("tensor", "vector", "scalar", "gpsimd", "sync")


class Prog:

    def __init__(self, nc, sems, counts):
        self.nc = nc
        self.sems = sems
        self.counts = counts
        self.items = []

    def add(self, engine, fn, waits=(), inc=None, k=1, fence=False, relax=False):
        ws = []
        for w in waits:
            if w is None:
                continue
            peng, sname, cnt = w
            if peng == engine:
                continue
            ws.append((sname, cnt))
        tok = None
        if inc is not None:
            self.counts[inc] += k
            tok = (engine, inc, self.counts[inc])
        self.items.append((engine, fn, ws, inc, self.counts[inc] if inc else 0, k,
                           relax))
        return tok

    def dma(self, engine, out, in_, waits=(), slow_ok=False, sem=None):
        if sem is None:
            sem = "dL" if engine == "sync" else "dG"
        return self.add(engine,
                        lambda e, o=out, i=in_: e.dma_start(
                            out=o, in_=i, allow_slow_non_contiguous=slow_ok),
                        waits=waits, inc=sem, k=16)

    def barrier(self):
        toks = []
        for e in ENGINES:
            toks.append(self.add(e, lambda eng: eng.nop(), inc="bar"))
        n = self.counts["bar"]
        return ("__bar__", "bar", n)

    def emit(self):
        with self.nc.Block() as block:
            for engname in ENGINES:
                self._mk(block, engname)

    def _mk(self, block, engname):
        items = [it for it in self.items if it[0] == engname]
        sems = self.sems
        hazard = engname in ("vector", "gpsimd")

        def body(eng):
            waited = {}
            first = True
            for (_, fn, ws, inc, tgt, k, relax) in items:
                padded = False
                for (sname, cnt) in ws:
                    if waited.get(sname, 0) >= cnt:
                        continue
                    eng.wait_ge(sems[sname], cnt)
                    waited[sname] = cnt
                    padded = True
                if padded:
                    eng.nop(cycle_cnt=96)
                if hazard and not first:
                    if relax:
                        eng.nop()
                    else:
                        eng.drain()
                instr = fn(eng)
                if inc is not None:
                    instr.then_inc(sems[inc], k)
                first = False
        getattr(block, engname)(body)


class Core:

    def __init__(self, nc, es):
        self.nc = nc
        self.es = es
        self.sems = {}
        self.counts = {}
        for s in ("dL", "dG", "dP", "dX0", "dX1", "dX2", "dX3", "dY0", "dY1",
                  "sT", "sV", "sC", "sG", "bar"):
            self.sems[s] = es.enter_context(nc.semaphore(s))
            self.counts[s] = 0

    def prog(self):
        return Prog(self.nc, self.sems, self.counts)


def _col_ap(dram_ap, n):
    total = 1
    for s in dram_ap.shape:
        total *= s
    assert total == n
    flat = dram_ap.rearrange(" ".join(f"a{i}" for i in range(len(dram_ap.shape)))
                             + " -> (" + " ".join(f"a{i}" for i in range(len(dram_ap.shape))) + ")") \
        if len(dram_ap.shape) > 1 else dram_ap
    return flat.rearrange("(j p) -> p j", p=128)


def _row_ap(dram_ap, n):
    flat = dram_ap
    if len(dram_ap.shape) > 1:
        flat = dram_ap.rearrange(" ".join(f"a{i}" for i in range(len(dram_ap.shape)))
                                 + " -> (" + " ".join(f"a{i}" for i in range(len(dram_ap.shape))) + ")")
    return flat.rearrange("(one n) -> one n", one=1)


def _wslices_ap(w_ap, kin, kout):
    return w_ap.rearrange("(k p) o -> p k o", p=128)


def _wslices_dst(sb_handle, ktiles, kout):
    return sb_handle[:, :].rearrange("p (k o) -> p k o", k=ktiles)


def build_program(parts=("ssm", "ffn1", "att", "ffn2"), debug=False, dbg_tile=0):
    nc = bass.Bass("TRN2", target_bir_lowering=False)
    es = contextlib.ExitStack()
    co = Core(nc, es)

    dt_in = {}
    def din(name, shape, dtype=F32):
        dt_in[name] = nc.dram_tensor(name, list(shape), dtype, kind="ExternalInput").ap()
        return dt_in[name]

    x_in = din("x", (NT, DM))
    lam_re = din("L0_attn_a_Lambda_re", (PSSM,))
    lam_im = din("L0_attn_a_Lambda_im", (PSSM,))
    Bw = din("L0_attn_a_B", (PSSM, DM, 2))
    Cw = din("L0_attn_a_C", (DM, PSSM, 2))
    Dw = din("L0_attn_a_D", (DM,))
    lstep = din("L0_attn_a_log_step", (PSSM, 1))
    w_attn = din("L0_attn_b_w_attn", (DM, 3 * DM))
    b_attn = din("L0_attn_b_b_attn", (3 * DM,))
    w_proj = din("L0_attn_b_w_proj", (DM, DM))
    b_proj = din("L0_attn_b_b_proj", (DM,))
    cn_w1 = din("L0_ffn_cn_w1", (DM, DM))
    cn_b1 = din("L0_ffn_cn_b1", (DM,))
    cn_w2 = din("L0_ffn_cn_w2", (DM, 512))
    cn_b2 = din("L0_ffn_cn_b2", (512,))
    cn_w3 = din("L0_ffn_cn_w3", (512, 256))
    cn_b3 = din("L0_ffn_cn_b3", (256,))
    cn_w4 = din("L0_ffn_cn_w4", (256, 1))
    cn_b4 = din("L0_ffn_cn_b4", (1,))
    gc_w1 = din("L0_ffn_gc_w1", (DM, 512))
    gc_b1 = din("L0_ffn_gc_b1", (512,))
    gc_w2 = din("L0_ffn_gc_w2", (512, 256))
    gc_b2 = din("L0_ffn_gc_b2", (256,))
    gc_w3 = din("L0_ffn_gc_w3", (256, 1))
    gc_b3 = din("L0_ffn_gc_b3", (1,))
    a_range = din("L0_ffn_adaptive_range", (1,))
    identb = din("c_identb", (128, 128), BF16)
    identf = din("c_identf", (128, 128), F32)
    triuf = din("c_triuf", (128, 128), F32)
    trium = din("c_trium", (128, 128), BF16)

    y_out = nc.dram_tensor("y", [NT, DM], F32, kind="ExternalOutput").ap()
    r0 = nc.dram_tensor("r0", [NT, DM], F32, kind="Internal").ap()
    r1 = nc.dram_tensor("r1", [NT, DM], F32, kind="Internal").ap()
    rcscr = nc.dram_tensor("rcscr", [1, DM], F32, kind="Internal").ap()
    dbg = {}
    if debug:
        for nm, shape, dt in (("d_pcol", [128, 160], F32), ("d_ainvre", [128, 512], F32),
                              ("d_ainvim", [128, 512], F32), ("d_are", [128, 512], F32),
                              ("d_aim", [128, 512], F32), ("d_bbre", [128, 512], BF16),
                              ("d_bbim", [128, 512], BF16), ("d_cret", [128, 1024], BF16),
                              ("d_xln", [128, 1024], BF16), ("d_xlnT", [128, 1024], BF16),
                              ("d_vre", [128, 512], F32), ("d_vim", [128, 512], F32),
                              ("d_xsre", [128, 512], BF16), ("d_xsim", [128, 512], BF16),
                              ("d_du", [128, 1024], F32), ("d_dbc", [128, 1024], F32), ("d_xo", [128, 1024], F32),
                              ("a_qT", [128, 4096], BF16), ("a_kT", [128, 512], BF16),
                              ("a_vst", [128, 1040], BF16), ("a_e00", [128, 512], BF16),
                              ("a_yb", [64, 512], BF16), ("a_zbc", [64, 512], F32),
                              ("f_mrow", [1, 1024], F32), ("f_mcol", [128, 8], F32),
                              ("f_gcol1", [128, 4], F32), ("f_rows", [1, 16], F32),
                              ("f_rstore", [128, 32], F32)):
            dbg[nm] = nc.dram_tensor(nm, shape, dt, kind="ExternalOutput").ap()

    chain = {"ssm": (x_in, r0), "ffn1": (r0, r1), "att": (r1, r0), "ffn2": (r0, y_out)}
    io = {p: list(chain[p]) for p in parts}
    io[parts[0]][0] = x_in
    io[parts[-1]][1] = y_out

    sb = nc.alloc_sbuf_tensor
    psa = nc.alloc_psum_tensor

    ones_row = sb("ones_row", [1, 128], F32)
    ones_col = sb("ones_col", [128, 1], F32)
    idb = sb("idb", [128, 128], BF16)
    idf = sb("idf", [128, 128], F32)
    trf = sb("trf", [128, 128], F32)
    trm = sb("trm", [128, 128], BF16)

    pg = co.prog()
    t_ld = []
    for dst, src in ((idb, identb), (idf, identf), (trf, triuf), (trm, trium)):
        t_ld.append(pg.dma("sync", dst[:], src[:]))
    tok_ones = pg.add("vector", lambda e: e.memset(ones_row[:], 1.0), inc="sV")
    tok_onesc = pg.add("vector", lambda e: e.memset(ones_col[:], 1.0), inc="sV")
    bar0 = pg.barrier()
    pg.emit()

    ln_consts = {}

    def ln_tile(pr, xt, xln_bf, cols, dummy, wait_x, xlnf=None, wait_buf=()):
        sx, sxx, m, biasc, r, nrm = (cols[:, i:i + 1] for i in range(6))
        t1 = pr.add("scalar", lambda e: e.activation(dummy[:], xt[:], AF.Copy, accum_out=sx),
                    waits=[wait_x] + list(wait_buf), inc="sC")
        t2 = pr.add("scalar", lambda e: e.activation(dummy[:], xt[:], AF.Square, accum_out=sxx),
                    waits=[wait_x], inc="sC")
        t3 = pr.add("vector", lambda e: e.tensor_scalar(m, sx, 1.0 / DM, None, ALU.mult),
                    waits=[t1], inc="sV")
        t4 = pr.add("scalar", lambda e: e.activation(biasc, m, AF.Square), waits=[t3], inc="sC")
        t5 = pr.add("vector", lambda e: e.tensor_scalar(biasc, biasc, -1.0, EPS, ALU.mult, ALU.add),
                    waits=[t4], inc="sV")
        t6 = pr.add("scalar", lambda e: e.activation(r, sxx, AF.Ln, bias=biasc, scale=1.0 / DM),
                    waits=[t2, t5], inc="sC")
        t7 = pr.add("scalar", lambda e: e.activation(r, r, AF.Exp, scale=-0.5), waits=[t6], inc="sC")
        t8 = pr.add("vector", lambda e: e.tensor_scalar(nrm, r, m, -1.0, ALU.mult, ALU.mult),
                    waits=[t7], inc="sV")
        tok = pr.add("scalar", lambda e: e.activation(xln_bf[:], xt[:], AF.Identity, bias=nrm, scale=r),
                     waits=[t8], inc="sC")
        tok2 = tok
        if xlnf is not None:
            tok2 = pr.add("scalar", lambda e: e.activation(xlnf[:], xt[:], AF.Identity, bias=nrm, scale=r),
                          waits=[t8], inc="sC")
        return tok, tok2, r, nrm

    def ln_stats(pr, xt, cols, rcol, ncol, scrap, wait_x):
        sx, sxx, m, biasc = (cols[:, i:i + 1] for i in range(4))
        t1 = pr.add("scalar", lambda e: e.activation(scrap[:], xt[:], AF.Copy, accum_out=sx),
                    waits=[wait_x], inc="sC")
        t2 = pr.add("scalar", lambda e: e.activation(scrap[:], xt[:], AF.Square, accum_out=sxx),
                    waits=[wait_x], inc="sC")
        t3 = pr.add("vector", lambda e: e.tensor_scalar(m, sx, 1.0 / DM, None, ALU.mult),
                    waits=[t1], inc="sV")
        t4 = pr.add("scalar", lambda e: e.activation(biasc, m, AF.Square), waits=[t3], inc="sC")
        t5 = pr.add("vector", lambda e: e.tensor_scalar(biasc, biasc, -1.0, EPS, ALU.mult, ALU.add),
                    waits=[t4], inc="sV")
        t6 = pr.add("scalar", lambda e: e.activation(rcol, sxx, AF.Ln, bias=biasc, scale=1.0 / DM),
                    waits=[t2, t5], inc="sC")
        t7 = pr.add("scalar", lambda e: e.activation(rcol, rcol, AF.Exp, scale=-0.5),
                    waits=[t6], inc="sC")
        t8 = pr.add("vector", lambda e: e.tensor_scalar(ncol, rcol, m, -1.0, ALU.mult, ALU.mult),
                    waits=[t7], inc="sV")
        return t8

    def transpose8(pr, src_bf, psT, dst_bf, wait_src, wait_dst_free, blocks=8, ident=None,
                   relax=False):
        idt = ident if ident is not None else idb
        def tfn(e):
            last = None
            for j in range(blocks):
                last = e.transpose(psT[:, j * 128:(j + 1) * 128], src_bf[:, j * 128:(j + 1) * 128], idt[:])
            return last
        tk = pr.add("tensor", tfn, waits=[wait_src, wait_dst_free], inc="sT")
        ev = pr.add("vector", lambda e: e.tensor_copy(dst_bf[:, 0:blocks * 128], psT[:, 0:blocks * 128]),
                    waits=[tk], inc="sV", relax=relax)
        return ev

    def ssm_phase(prev_bar):
        src, dst = io["ssm"]
        with contextlib.ExitStack() as ps:
            def S(name, shape, dtype):
                return ps.enter_context(nc.sbuf_tensor(name, shape, dtype))
            pcol = S("pcol", [128, 160], F32)
            AinvT_re = S("AinvT_re", [128, 512], F32)
            AinvT_im = S("AinvT_im", [128, 512], F32)
            Are = S("Are", [128, 512], F32)
            Aim = S("Aim", [128, 512], F32)
            AinvP_re = S("AinvP_re", [128, 512], F32)
            AinvP_im = S("AinvP_im", [128, 512], F32)
            BbT_re = S("BbT_re", [128, 8 * 512], BF16)
            BbT_im = S("BbT_im", [128, 8 * 512], BF16)
            CreT = S("CreT", [128, 4 * 1024], BF16)
            CimNT = S("CimNT", [128, 4 * 1024], BF16)
            Dbc = S("Dbc", [128, 1024], F32)
            braw = S("braw", [128, 2048], F32)
            bstg_re = S("bstg_re", [128, 1024], BF16)
            bstg_im = S("bstg_im", [128, 1024], BF16)
            xinb = [S(f"xin{i}", [128, 1024], F32) for i in range(2)]
            xoutb = [S(f"xout{i}", [128, 1024], F32) for i in range(2)]
            xln = S("xln", [128, 1024], BF16)
            xlnf2 = [S(f"xlnf{i}", [128, 1024], F32) for i in range(2)]
            xlnT = S("xlnT", [128, 1024], BF16)
            dummy = S("scrap", [128, 1024], F32)
            lncols = S("lncols", [128, 8], F32)
            vre2 = [S(f"vre{i}", [128, 512], F32) for i in range(2)]
            vim2 = [S(f"vim{i}", [128, 512], F32) for i in range(2)]
            tmp1 = S("tmp1", [128, 512], F32)
            tmp2 = S("tmp2", [128, 512], F32)
            tmp3 = S("tmp3", [128, 512], F32)
            tmp4 = S("tmp4", [128, 512], F32)
            xs_re = S("xs_re", [128, 512], BF16)
            xs_im = S("xs_im", [128, 512], BF16)
            du2 = [S(f"du{i}", [128, 1024], F32) for i in range(2)]
            psT = ps.enter_context(nc.psum_tensor("psT", [128, 1024], BF16))
            psBUre = ps.enter_context(nc.psum_tensor("psBUre", [128, 512], F32))
            psBUim = ps.enter_context(nc.psum_tensor("psBUim", [128, 512], F32))
            psSre = ps.enter_context(nc.psum_tensor("psSre", [128, 512], F32))
            psSim = ps.enter_context(nc.psum_tensor("psSim", [128, 512], F32))
            psYS = ps.enter_context(nc.psum_tensor("psYS", [128, 1024], F32))

            c = {}
            for i, nm in enumerate(("Lre", "Lim", "lst", "lr", "stp", "th", "sh2", "cth",
                                    "sth", "em", "er", "are", "aim", "nre", "ml", "inv",
                                    "fre", "fim", "ainvre", "ainvim", "pwre", "pwim",
                                    "a128re", "a128im", "acre", "acim", "t1", "t2",
                                    "t3", "t4", "t5", "t6", "q1", "q2", "q3")):
                c[nm] = pcol[:, 4 * i:4 * i + 4]

            pr = co.prog()

            w0 = pr.dma("sync", c["Lre"], _col_ap(lam_re, 512), waits=[prev_bar], slow_ok=True, sem="dP")
            w1 = pr.dma("sync", c["Lim"], _col_ap(lam_im, 512), waits=[prev_bar], slow_ok=True, sem="dP")
            w2 = pr.dma("sync", c["lst"], _col_ap(lstep, 512), waits=[prev_bar], slow_ok=True, sem="dP")
            drow_t = pr.dma("sync", xinb[0][0:1, 0:1024], _row_ap(Dw, 1024), waits=[prev_bar], sem="dP")
            dp_all = drow_t
            w0 = w1 = w2 = drow_t = dp_all

            def V(fn, waits, fence=False, relax=False):
                return pr.add("vector", fn, waits=waits, inc="sV", fence=fence, relax=relax)

            def Cc(fn, waits):
                return pr.add("scalar", fn, waits=waits, inc="sC")

            t = V(lambda e: e.tensor_scalar_min(c["lr"], c["Lre"], -1e-4), [w0])
            tstp = Cc(lambda e: e.activation(c["stp"], c["lst"], AF.Exp), [w2])
            tth = V(lambda e: e.tensor_mul(c["th"], c["Lim"], c["stp"]), [w1, tstp])
            trr = V(lambda e: e.tensor_mul(c["t1"], c["lr"], c["stp"]), [t, tstp])
            tsh = Cc(lambda e: e.activation(c["sh2"], c["th"], AF.Sin, scale=0.5), [tth])
            tsth = Cc(lambda e: e.activation(c["sth"], c["th"], AF.Sin), [tth])
            tsq = Cc(lambda e: e.activation(c["sh2"], c["sh2"], AF.Square), [tsh])
            tcth = V(lambda e: e.tensor_scalar(c["cth"], c["sh2"], -2.0, 1.0, ALU.mult, ALU.add), [tsq])
            fact = [40320.0, 5040.0, 720.0, 120.0, 24.0, 6.0, 2.0, 1.0]
            tt = V(lambda e: e.tensor_scalar(c["em"], c["t1"], 1.0 / fact[0], 1.0 / fact[1],
                                             ALU.mult, ALU.add), [trr])
            for kf in range(2, 8):
                tt = V(lambda e: e.tensor_mul(c["em"], c["em"], c["t1"]), [tt])
                tt = V(lambda e, k=kf: e.tensor_scalar(c["em"], c["em"], 1.0 / fact[k], None, ALU.add), [tt])
            tt = V(lambda e: e.tensor_mul(c["em"], c["em"], c["t1"]), [tt])
            ter = V(lambda e: e.tensor_scalar(c["er"], c["em"], 1.0, None, ALU.add), [tt])
            tare = V(lambda e: e.tensor_mul(c["are"], c["er"], c["cth"]), [ter, tcth])
            taim = V(lambda e: e.tensor_mul(c["aim"], c["er"], c["sth"]), [ter, tsth])
            tn1 = V(lambda e: e.tensor_mul(c["nre"], c["em"], c["cth"]), [tt, tcth])
            tn2 = V(lambda e: e.tensor_scalar(c["t2"], c["sh2"], -2.0, None, ALU.mult), [tsq])
            tnre = V(lambda e: e.tensor_add(c["nre"], c["nre"], c["t2"]), [tn1, tn2])
            tl1 = Cc(lambda e: e.activation(c["q1"], c["lr"], AF.Square), [t])
            tl2 = Cc(lambda e: e.activation(c["q2"], c["Lim"], AF.Square), [w1])
            tl3 = V(lambda e: e.tensor_add(c["ml"], c["q1"], c["q2"]), [tl1, tl2])
            tl4 = Cc(lambda e: e.activation(c["ml"], c["ml"], AF.Ln), [tl3])
            tinv = Cc(lambda e: e.activation(c["inv"], c["ml"], AF.Exp, scale=-1.0), [tl4])
            tf1 = V(lambda e: e.tensor_mul(c["t3"], c["nre"], c["lr"]), [tnre])
            tf2 = V(lambda e: e.tensor_mul(c["t4"], c["aim"], c["Lim"]), [taim])
            tf3 = V(lambda e: e.tensor_add(c["t3"], c["t3"], c["t4"]), [tf1, tf2])
            tfre = V(lambda e: e.tensor_mul(c["fre"], c["t3"], c["inv"]), [tf3, tinv])
            tf4 = V(lambda e: e.tensor_mul(c["t3"], c["aim"], c["lr"]), [taim])
            tf5 = V(lambda e: e.tensor_mul(c["t4"], c["nre"], c["Lim"]), [tnre])
            tf6 = V(lambda e: e.tensor_sub(c["t3"], c["t3"], c["t4"]), [tf4, tf5])
            tfim = V(lambda e: e.tensor_mul(c["fim"], c["t3"], c["inv"]), [tf6, tinv])
            ta1 = Cc(lambda e: e.activation(c["q1"], c["are"], AF.Square), [tare, tl3])
            ta2 = Cc(lambda e: e.activation(c["q2"], c["aim"], AF.Square), [taim, tl3])
            ta3 = V(lambda e: e.tensor_add(c["q3"], c["q1"], c["q2"]), [ta1, ta2])
            ta4 = Cc(lambda e: e.activation(c["q3"], c["q3"], AF.Ln), [ta3])
            ta5 = Cc(lambda e: e.activation(c["q3"], c["q3"], AF.Exp, scale=-1.0), [ta4])
            tai1 = V(lambda e: e.tensor_mul(c["ainvre"], c["are"], c["q3"]), [ta5, tare])
            tai2 = V(lambda e: e.tensor_scalar(c["t6"], c["q3"], -1.0, None, ALU.mult), [ta5])
            tai3 = V(lambda e: e.tensor_mul(c["ainvim"], c["aim"], c["t6"]), [tai2, taim])

            def build_table(Tre, Tim, base_re, base_im, waits):
                toks = []
                tz1 = V(lambda e: e.memset(Tre[:, 0:512:128], 1.0), waits)
                tz2 = V(lambda e: e.memset(Tim[:, 0:512:128], 0.0), waits)
                tp1 = V(lambda e: e.tensor_copy(c["pwre"], base_re), waits)
                tp2 = V(lambda e: e.tensor_copy(c["pwim"], base_im), waits)
                last = [tz1, tz2, tp1, tp2]
                for k in range(7):
                    w = 1 << k
                    nl = []
                    for cc in range(4):
                        o = cc * 128
                        pr_ = c["pwre"][:, cc:cc + 1]
                        pi_ = c["pwim"][:, cc:cc + 1]
                        fc = w < 8
                        a1 = V(lambda e, o=o, w=w, pr_=pr_: e.tensor_scalar(
                            tmp1[:, 0:w], Tre[:, o:o + w], pr_, None, ALU.mult), last, fence=fc)
                        a2 = V(lambda e, o=o, w=w, pi_=pi_: e.tensor_scalar(
                            tmp2[:, 0:w], Tim[:, o:o + w], pi_, None, ALU.mult), last, fence=fc)
                        a3 = V(lambda e, o=o, w=w: e.tensor_sub(
                            Tre[:, o + w:o + 2 * w], tmp1[:, 0:w], tmp2[:, 0:w]), [a1, a2], fence=fc)
                        a4 = V(lambda e, o=o, w=w, pi_=pi_: e.tensor_scalar(
                            tmp1[:, 0:w], Tre[:, o:o + w], pi_, None, ALU.mult), [a3], fence=fc)
                        a5 = V(lambda e, o=o, w=w, pr_=pr_: e.tensor_scalar(
                            tmp2[:, 0:w], Tim[:, o:o + w], pr_, None, ALU.mult), [a4], fence=fc)
                        a6 = V(lambda e, o=o, w=w: e.tensor_add(
                            Tim[:, o + w:o + 2 * w], tmp1[:, 0:w], tmp2[:, 0:w]), [a5], fence=fc)
                        nl.append(a6)
                    s1 = V(lambda e: e.tensor_mul(c["t1"], c["pwre"], c["pwre"]), nl, fence=True)
                    s2 = V(lambda e: e.tensor_mul(c["t2"], c["pwim"], c["pwim"]), nl)
                    s3 = V(lambda e: e.tensor_mul(c["t3"], c["pwre"], c["pwim"]), nl)
                    s4 = V(lambda e: e.tensor_sub(c["pwre"], c["t1"], c["t2"]), [s1, s2], fence=True)
                    s5 = V(lambda e: e.tensor_scalar(c["pwim"], c["t3"], 2.0, None, ALU.mult), [s3, s4], fence=True)
                    last = [s4, s5]
                return last

            tA = build_table(Are, Aim, c["are"], c["aim"], [tare, taim, tnre, tfre, tfim])
            tA2 = V(lambda e: e.tensor_copy(c["a128re"], c["pwre"]), tA)
            tA3 = V(lambda e: e.tensor_copy(c["a128im"], c["pwim"]), tA)
            tI = build_table(AinvP_re, AinvP_im, c["ainvre"], c["ainvim"],
                             [tai1, tai3, tA2, tA3])
            tIT = []
            prev_psf = None
            for part, (srcT, dstT) in enumerate(((AinvP_re, AinvT_re), (AinvP_im, AinvT_im))):
                psf = psYS
                def tfn(e, srcT=srcT):
                    last = None
                    for cc in range(4):
                        last = e.transpose(psf[:, cc * 128:(cc + 1) * 128],
                                           srcT[:, cc * 128:(cc + 1) * 128], idf[:])
                    return last
                tk = pr.add("tensor", tfn, waits=tI + ([prev_psf] if prev_psf else []), inc="sT")
                ev = V(lambda e, dstT=dstT: e.tensor_copy(dstT[:], psf[:, 0:512]), [tk])
                prev_psf = ev
                tIT.append(ev)

            bb_done = []
            prev_ev = tIT
            prev_braw_rd = None
            for j in range(4):
                lb = pr.dma("sync", braw[:], Bw.rearrange("p h c -> p (h c)")[j * 128:(j + 1) * 128, :],
                            waits=[prev_bar] + ([prev_braw_rd] if prev_braw_rd else []), sem="dX0")
                d1 = V(lambda e: e.tensor_copy(tmp1[:, 0:512], braw[:, 0:1024:2]), [lb])
                d2 = V(lambda e: e.tensor_copy(tmp2[:, 0:512], braw[:, 1:1024:2]), [lb])
                d3 = V(lambda e: e.tensor_copy(tmp3[:, 0:512], braw[:, 1024:2048:2]), [lb])
                d4 = V(lambda e: e.tensor_copy(tmp4[:, 0:512], braw[:, 1025:2048:2]), [lb])
                prev_braw_rd = d4
                fre_j = c["fre"][:, j:j + 1]
                fim_j = c["fim"][:, j:j + 1]
                e1 = V(lambda e, f=fre_j: e.tensor_scalar(dummy[:, 0:512], tmp1[:, 0:512], f, None, ALU.mult), [d1])
                e2 = V(lambda e, f=fim_j: e.tensor_scalar(du2[0][:, 0:512], tmp2[:, 0:512], f, None, ALU.mult), [d2])
                e3 = V(lambda e: e.tensor_sub(bstg_re[:, 0:512], dummy[:, 0:512], du2[0][:, 0:512]), [e1, e2])
                e4 = V(lambda e, f=fre_j: e.tensor_scalar(dummy[:, 0:512], tmp3[:, 0:512], f, None, ALU.mult), [d3])
                e5 = V(lambda e, f=fim_j: e.tensor_scalar(du2[0][:, 0:512], tmp4[:, 0:512], f, None, ALU.mult), [d4])
                e6 = V(lambda e: e.tensor_sub(bstg_re[:, 512:1024], dummy[:, 0:512], du2[0][:, 0:512]), [e4, e5])
                e7 = V(lambda e, f=fim_j: e.tensor_scalar(dummy[:, 0:512], tmp1[:, 0:512], f, None, ALU.mult), [e3])
                e8 = V(lambda e, f=fre_j: e.tensor_scalar(du2[0][:, 0:512], tmp2[:, 0:512], f, None, ALU.mult), [e3])
                e9 = V(lambda e: e.tensor_add(bstg_im[:, 0:512], dummy[:, 0:512], du2[0][:, 0:512]), [e7, e8])
                ea = V(lambda e, f=fim_j: e.tensor_scalar(dummy[:, 0:512], tmp3[:, 0:512], f, None, ALU.mult), [e6])
                eb = V(lambda e, f=fre_j: e.tensor_scalar(du2[0][:, 0:512], tmp4[:, 0:512], f, None, ALU.mult), [e6])
                ec = V(lambda e: e.tensor_add(bstg_im[:, 512:1024], dummy[:, 0:512], du2[0][:, 0:512]), [e9, ea, eb])
                for (stg, dstT, wsrc) in ((bstg_re, BbT_re, e6), (bstg_im, BbT_im, ec)):
                    def tfn(e, stg=stg):
                        last = None
                        for k in range(8):
                            last = e.transpose(psT[:, k * 128:(k + 1) * 128],
                                               stg[:, k * 128:(k + 1) * 128], idb[:])
                        return last
                    tk = pr.add("tensor", tfn, waits=[wsrc, ec] + prev_ev, inc="sT")
                    def evfn(e, dstT=dstT, j=j):
                        return e.tensor_copy(
                            dstT[:, :].rearrange("p (k q) -> p k q", k=8)[:, :, j * 128:(j + 1) * 128],
                            psT[:, 0:1024].rearrange("p (k q) -> p k q", k=8))
                    ev = V(evfn, [tk])
                    prev_ev = [ev]
                    bb_done.append(ev)

            for k in range(8):
                lc = pr.dma("sync", braw[:, 0:1024], Cw.rearrange("h p c -> h (p c)")[k * 128:(k + 1) * 128, :],
                            waits=[prev_bar, prev_braw_rd], sem="dX0")
                d1 = V(lambda e: e.tensor_copy(bstg_re[:, 0:512], braw[:, 0:1024:2]), [lc] + prev_ev)
                d2 = V(lambda e: e.tensor_scalar(bstg_im[:, 0:512], braw[:, 1:1024:2], -1.0, None, ALU.mult),
                       [lc])
                prev_braw_rd = d2
                for (stg, dstT, wsrc) in ((bstg_re, CreT, d1), (bstg_im, CimNT, d2)):
                    def tfn(e, stg=stg):
                        last = None
                        for j in range(4):
                            last = e.transpose(psT[:, j * 128:(j + 1) * 128],
                                               stg[:, j * 128:(j + 1) * 128], idb[:])
                        return last
                    tk = pr.add("tensor", tfn, waits=[wsrc, d1, d2] + prev_ev, inc="sT")
                    def evfn(e, dstT=dstT, k=k):
                        return e.tensor_copy(
                            dstT[:, :].rearrange("p (j q) -> p j q", j=4)[:, :, k * 128:(k + 1) * 128],
                            psT[:, 0:512].rearrange("p (j q) -> p j q", j=4))
                    ev = V(evfn, [tk])
                    prev_ev = [ev]

            tdm = pr.add("tensor", lambda e: (
                e.matmul(psBUre[:], ones_row[:], xinb[0][0:1, 0:512], start=True, stop=True),
                e.matmul(psBUim[:], ones_row[:], xinb[0][0:1, 512:1024], start=True, stop=True))[-1],
                waits=[drow_t, tok_ones], inc="sT")
            tdv1 = V(lambda e: e.tensor_copy(Dbc[:, 0:512], psBUre[:]), [tdm])
            tdv2 = V(lambda e: e.tensor_copy(Dbc[:, 512:1024], psBUim[:]), [tdm])

            prologue_done = pr.barrier()

            for b in range(B_PER_CORE):
                zc1 = V(lambda e: e.memset(c["acre"], 0.0), [prologue_done])
                zc2 = V(lambda e: e.memset(c["acim"], 0.0), [prologue_done])
                stt = {
                    "free_xin": [prologue_done, prologue_done],
                    "free_xout": [prologue_done, prologue_done],
                    "xlnT_free": prologue_done,
                    "bu_free": prologue_done,
                    "s_free": prologue_done,
                    "ys_free": [tdv1, tdv2],
                    "xs_free": prologue_done,
                    "carry": [zc1, zc2],
                    "xln_free": [prologue_done],
                    "prev_o1": prologue_done,
                }
                A = {}

                def stageA1(t, b=b):
                    g = b * TPB + t
                    bi = g % 2
                    m = t % 2
                    xt = xinb[bi]
                    ld = pr.dma("sync", xt[:], src[g * 128:(g + 1) * 128, :],
                                waits=[stt["free_xin"][bi]], sem=f"dX{bi}")
                    tok_xln, tok_xlnf, _, _ = ln_tile(pr, xt, xln, lncols, dummy, ld,
                                                      xlnf=xlnf2[m], wait_buf=stt["xln_free"])
                    A[("ln", t)] = (tok_xln, tok_xlnf)
                    A[("xt", t)] = (xt, ld)
                    A[("bi", t)] = bi

                def stageA2(t, b=b):
                    m = t % 2
                    tok_xln, tok_xlnf = A[("ln", t)]
                    ev_xlnT = transpose8(pr, xln, psT, xlnT, tok_xln, stt["xlnT_free"],
                                         relax=True)

                    def bufn(e):
                        last = None
                        for j in range(8):
                            last = e.matmul(psBUre[:], xlnT[:, j * 128:(j + 1) * 128],
                                            BbT_re[:, j * 512:(j + 1) * 512],
                                            start=(j == 0), stop=(j == 7))
                        for j in range(8):
                            last = e.matmul(psBUim[:], xlnT[:, j * 128:(j + 1) * 128],
                                            BbT_im[:, j * 512:(j + 1) * 512],
                                            start=(j == 0), stop=(j == 7))
                        return last
                    t_bu = pr.add("tensor", bufn, waits=[ev_xlnT, stt["bu_free"]], inc="sT")
                    v1 = V(lambda e: e.tensor_mul(tmp1[:], AinvT_re[:], psBUre[:]), [t_bu], relax=True)
                    v2 = V(lambda e: e.tensor_mul(tmp2[:], AinvT_im[:], psBUim[:]), [t_bu], relax=True)
                    v3 = V(lambda e, m=m: e.tensor_sub(vre2[m][:], tmp1[:], tmp2[:]),
                           [v1, v2, A.get(("sc", t - 2))], relax=True)
                    v4 = V(lambda e: e.tensor_mul(tmp1[:], AinvT_im[:], psBUre[:]), [v3], relax=True)
                    v5 = V(lambda e: e.tensor_mul(tmp2[:], AinvT_re[:], psBUim[:]), [v3], relax=True)
                    v6 = V(lambda e, m=m: e.tensor_add(vim2[m][:], tmp1[:], tmp2[:]), [v4, v5], relax=True)
                    stt["bu_free"] = v6
                    g1 = pr.add("gpsimd", lambda e, m=m: e.tensor_tensor(
                        du2[m][:], Dbc[:], xlnf2[m][:], ALU.mult),
                        waits=[tok_xlnf, stt["prev_o1"]], inc="sG")
                    stt["prev_o1"] = g1
                    stt["xlnT_free"] = t_bu
                    stt["xln_free"] = [ev_xlnT]
                    A[("v", t)] = (v3, v6)
                    A[("g1", t)] = g1

                def stageB(t, b=b):
                    g = b * TPB + t
                    m = t % 2
                    bi = A[("bi", t)]
                    xt, ld = A[("xt", t)]
                    xo = xoutb[bi]
                    v3, v6 = A[("v", t)]
                    g1 = A[("g1", t)]

                    def scfn(e, m=m):
                        last = None
                        for cc in range(4):
                            last = e.matmul(psSre[:, cc * 128:(cc + 1) * 128],
                                            vre2[m][:, cc * 128:(cc + 1) * 128], trf[:],
                                            start=True, stop=True)
                        for cc in range(4):
                            last = e.matmul(psSim[:, cc * 128:(cc + 1) * 128],
                                            vim2[m][:, cc * 128:(cc + 1) * 128], trf[:],
                                            start=True, stop=True)
                        return last
                    t_sc = pr.add("tensor", scfn, waits=[v3, v6, stt["s_free"]], inc="sT")
                    A[("sc", t)] = t_sc
                    rs = []
                    for cc in range(4):
                        o = cc * 128
                        acr = c["acre"][:, cc:cc + 1]
                        aci = c["acim"][:, cc:cc + 1]
                        rs.append(V(lambda e, o=o, acr=acr: e.scalar_tensor_tensor(
                            tmp1[:, o:o + 128], psSre[:, o:o + 128], acr, Are[:, o:o + 128],
                            ALU.add, ALU.mult), [t_sc, stt["xs_free"]] + stt["carry"], relax=True))
                        rs.append(V(lambda e, o=o, aci=aci: e.scalar_tensor_tensor(
                            tmp2[:, o:o + 128], psSim[:, o:o + 128], aci, Aim[:, o:o + 128],
                            ALU.add, ALU.mult), [t_sc] + stt["carry"], relax=True))
                        rs.append(V(lambda e, o=o, acr=acr: e.scalar_tensor_tensor(
                            tmp3[:, o:o + 128], psSre[:, o:o + 128], acr, Aim[:, o:o + 128],
                            ALU.add, ALU.mult), [t_sc] + stt["carry"], relax=True))
                        rs.append(V(lambda e, o=o, aci=aci: e.scalar_tensor_tensor(
                            tmp4[:, o:o + 128], psSim[:, o:o + 128], aci, Are[:, o:o + 128],
                            ALU.add, ALU.mult), [t_sc] + stt["carry"], relax=True))
                    x1 = V(lambda e: e.tensor_sub(xs_re[:], tmp1[:], tmp2[:]), rs, relax=True)
                    x2 = V(lambda e: e.tensor_add(xs_im[:], tmp3[:], tmp4[:]), rs, relax=True)
                    u1 = V(lambda e: e.tensor_add(c["t1"], psSre[:, 127:512:128], c["acre"]), rs)
                    u2 = V(lambda e: e.tensor_add(c["t2"], psSim[:, 127:512:128], c["acim"]), rs)
                    u3 = V(lambda e: e.tensor_mul(c["t3"], c["a128re"], c["t1"]), [u1])
                    u4 = V(lambda e: e.tensor_mul(c["t4"], c["a128im"], c["t2"]), [u2])
                    u5 = V(lambda e: e.tensor_sub(c["acre"], c["t3"], c["t4"]), [u3, u4])
                    u6 = V(lambda e: e.tensor_mul(c["t3"], c["a128im"], c["t1"]), [u5])
                    u7 = V(lambda e: e.tensor_mul(c["t4"], c["a128re"], c["t2"]), [u5])
                    u8 = V(lambda e: e.tensor_add(c["acim"], c["t3"], c["t4"]), [u6, u7])
                    stt["carry"] = [u5, u8]
                    stt["s_free"] = u8

                    def ysfn(e):
                        last = None
                        for half in range(2):
                            for cc in range(4):
                                last = e.matmul(psYS[:, half * 512:(half + 1) * 512],
                                                xs_re[:, cc * 128:(cc + 1) * 128],
                                                CreT[:, cc * 1024 + half * 512: cc * 1024 + (half + 1) * 512],
                                                start=(cc == 0), stop=False)
                            for cc in range(4):
                                last = e.matmul(psYS[:, half * 512:(half + 1) * 512],
                                                xs_im[:, cc * 128:(cc + 1) * 128],
                                                CimNT[:, cc * 1024 + half * 512: cc * 1024 + (half + 1) * 512],
                                                start=False, stop=(cc == 3))
                        return last
                    t_ys = pr.add("tensor", ysfn, waits=[x1, x2] + stt["ys_free"], inc="sT")
                    stt["xs_free"] = t_ys
                    o1 = V(lambda e, xo=xo, m=m: e.scalar_tensor_tensor(
                        xo[:], psYS[:], 1.0, du2[m][:], ALU.mult, ALU.add),
                        [t_ys, g1, stt["free_xout"][bi]], relax=True)
                    o2 = V(lambda e, xo=xo, xt=xt: e.tensor_add(xo[:], xo[:], xt[:]), [o1])
                    stt["ys_free"] = [o2]
                    stt["free_xin"][bi] = o2
                    st = pr.dma("sync", dst[g * 128:(g + 1) * 128, :], xo[:], waits=[o2],
                                sem=f"dY{bi}")
                    stt["free_xout"][bi] = st

                stageA1(0)
                stageA2(0)
                for t in range(TPB):
                    if t + 1 < TPB:
                        stageA1(t + 1)
                    stageB(t)
                    if t + 1 < TPB:
                        stageA2(t + 1)

            pr.add("sync", lambda e: e.nop(),
                   waits=[("x", s, co.counts[s]) for s in
                          ("dL", "dG", "dP", "dX0", "dX1", "dX2", "dX3", "dY0", "dY1")
                          if co.counts[s] > 0])
            endbar = pr.barrier()
            pr.emit()
            return endbar

    def ffn_phase(prev_bar, which):
        src, dst = io[which]
        SG = (1, 1, -1, -1, -1, -1, 1, 1)

        with contextlib.ExitStack() as ps:
            def S(name, shape, dtype):
                return ps.enter_context(nc.sbuf_tensor(which + name, shape, dtype))
            w1b = S("w1b", [128, 8 * 1024], BF16)
            w2b = S("w2b", [128, 8 * 512], BF16)
            w3b = S("w3b", [128, 4 * 256], BF16)
            w4c = S("w4c", [128, 2], BF16)
            gw1b = S("gw1b", [128, 8 * 512], BF16)
            gw2b = S("gw2b", [128, 4 * 256], BF16)
            gw3c = S("gw3c", [128, 2], BF16)
            b1c = S("b1c", [128, 8], F32)
            b2c = S("b2c", [128, 4], F32)
            b3c = S("b3c", [128, 2], F32)
            gb1c = S("gb1c", [128, 4], F32)
            gb2c = S("gb2c", [128, 2], F32)
            rows = S("rows", [1, 16], F32)
            b4h = rows[0:1, 0:1]; gb3h = rows[0:1, 1:2]; arr = rows[0:1, 2:3]
            c1s = rows[0:1, 3:4]; c0s = rows[0:1, 4:5]; gfr = rows[0:1, 5:6]
            rt1 = rows[0:1, 6:7]; rt2 = rows[0:1, 7:8]; snrm = rows[0:1, 8:9]
            rstore = S("rstore", [128, 32], F32)
            cols = S("cols", [128, 8], F32)
            scrap = S("scrap2", [128, 1024], F32)
            sa = [S(f"sta{i}", [128, 8 * 512], F32) for i in range(2)]
            sb = [S(f"stb{i}", [128, 8 * 512], F32) for i in range(2)]
            sB = [S(f"stB{i}", [128, 8 * 512], BF16) for i in range(2)]
            gh1 = S("gh1", [128, 8 * 512], BF16)
            gh2 = S("gh2", [128, 4 * 512], BF16)
            gh3 = S("gh3", [128, 2 * 512], BF16)
            cb = S("cb", [128, 512], F32)
            twt = S("twt", [128, 512], F32)
            grow = S("grow", [1, 512], F32)
            c1col = cols[:, 6:7]
            c0col = cols[:, 7:8]
            xs2 = [S(f"xs2{i}", [128, 1024], F32) for i in range(2)]
            xr = [S(f"xr{i}", [128, 1024], F32) for i in range(2)]
            xio = [S(f"xio{i}", [128, 1024], F32) for i in range(2)]
            xlnf = S("fxlnf", [128, 1024], F32)
            mrow = S("mrow", [1, 1024], F32)
            mcol = S("mcol", [128, 8], F32)
            mcolb = S("mcolb", [128, 8], BF16)
            gcol1 = S("gcol1", [128, 4], F32)
            gcol1b = S("gcol1b", [128, 4], BF16)
            gcol2b = S("gcol2b", [128, 2], BF16)
            trow = S("trow", [1, 512], F32)
            psH = [ps.enter_context(nc.psum_tensor(which + f"psH{i}", [128, 512], F32))
                   for i in range(2)]
            psTf = ps.enter_context(nc.psum_tensor(which + "psTf", [1, 512], F32))
            psCB = ps.enter_context(nc.psum_tensor(which + "psCB", [128, 512], F32))
            psTr = ps.enter_context(nc.psum_tensor(which + "psTr", [128, 1024], F32))
            psG1 = ps.enter_context(nc.psum_tensor(which + "psG1", [1, 512], F32))
            psG2 = ps.enter_context(nc.psum_tensor(which + "psG2", [1, 512], F32))

            pr = co.prog()

            def V(fn, waits, relax=False):
                return pr.add("vector", fn, waits=waits, inc="sV", relax=relax)

            def Cc(fn, waits):
                return pr.add("scalar", fn, waits=waits, inc="sC")

            def T(fn, waits):
                return pr.add("tensor", fn, waits=waits, inc="sT")

            def G(fn, waits):
                return pr.add("gpsimd", fn, waits=waits, inc="sG")

            wl_raw = []
            wl = wl_raw
            wl.append(pr.dma("gpsimd", _wslices_dst(w1b, 8, DM), _wslices_ap(cn_w1, DM, DM), waits=[prev_bar]))
            wl.append(pr.dma("gpsimd", _wslices_dst(w2b, 8, 512), _wslices_ap(cn_w2, DM, 512), waits=[prev_bar]))
            wl.append(pr.dma("gpsimd", _wslices_dst(w3b, 4, 256), _wslices_ap(cn_w3, 512, 256), waits=[prev_bar]))
            wl.append(pr.dma("gpsimd", w4c[:], _col_ap(cn_w4, 256), waits=[prev_bar], slow_ok=True))
            wl.append(pr.dma("gpsimd", _wslices_dst(gw1b, 8, 512), _wslices_ap(gc_w1, DM, 512), waits=[prev_bar]))
            wl.append(pr.dma("gpsimd", _wslices_dst(gw2b, 4, 256), _wslices_ap(gc_w2, 512, 256), waits=[prev_bar]))
            wl.append(pr.dma("gpsimd", gw3c[:], _col_ap(gc_w3, 256), waits=[prev_bar], slow_ok=True))
            wl = [wl_raw[-1]]
            bl = []
            for dstb, srcb, n in ((b1c, cn_b1, DM), (b2c, cn_b2, 512), (b3c, cn_b3, 256),
                                  (gb1c, gc_b1, 512), (gb2c, gc_b2, 256)):
                bl.append(pr.dma("sync", dstb[:], _col_ap(srcb, n), waits=[prev_bar],
                                 slow_ok=True, sem="dP"))
            lb4 = pr.dma("sync", b4h[:], _row_ap(cn_b4, 1), waits=[prev_bar], slow_ok=True, sem="dP")
            lg3 = pr.dma("sync", gb3h[:], _row_ap(gc_b3, 1), waits=[prev_bar], slow_ok=True, sem="dP")
            lar = pr.dma("sync", arr[:], _row_ap(a_range, 1), waits=[prev_bar], slow_ok=True, sem="dP")
            dp_all = lar
            bl = [dp_all]
            h1 = V(lambda e: e.tensor_scalar(b4h, b4h, 0.5, None, ALU.mult), [dp_all])
            h2 = V(lambda e: e.tensor_scalar(gb3h, gb3h, 0.5, None, ALU.mult), [dp_all])
            h3 = V(lambda e: e.tensor_scalar(c1s, arr, 0.3, None, ALU.mult), [dp_all])

            tok_out_final = []
            for b in range(B_PER_CORE):
                bstart = tok_out_final[-1] if tok_out_final else prev_bar
                free_x = [bstart, bstart]
                t_cs_last = None
                st_toks = []
                for t in range(TPB):
                    g = b * TPB + t
                    bi = t % 2
                    xt = xio[bi]
                    ld = pr.dma("sync", xt[:], src[g * 128:(g + 1) * 128, :], waits=[free_x[bi]],
                                sem=f"dX{bi}")
                    rcol = rstore[:, 2 * t:2 * t + 1]
                    ncol = rstore[:, 2 * t + 1:2 * t + 2]
                    s_tok = ln_stats(pr, xt, cols, rcol, ncol, scrap, ld)
                    def csfn(e, xt=xt, rcol=rcol, ncol=ncol, t=t):
                        e.matmul(psG1[:], rcol, xt[:, 0:512], start=(t == 0), stop=(t == TPB - 1))
                        e.matmul(psG2[:], rcol, xt[:, 512:1024], start=(t == 0), stop=(t == TPB - 1))
                        return e.matmul(psTf[0:1, 0:1], ncol, ones_col[:, 0:1],
                                        start=(t == 0), stop=(t == TPB - 1))
                    t_cs = T(csfn, [s_tok, prev_bar if t < 2 else None,
                                    tok_out_final[-1] if (t < 2 and tok_out_final) else None])
                    free_x[bi] = t_cs
                    t_cs_last = t_cs
                m1 = V(lambda e: e.tensor_copy(snrm, psTf[0:1, 0:1]), [t_cs_last])
                m2 = V(lambda e: e.tensor_scalar(mrow[0:1, 0:512], psG1[:], snrm, 1.0 / TSEQ,
                                                 ALU.add, ALU.mult), [m1])
                m3 = V(lambda e: e.tensor_scalar(mrow[0:1, 512:1024], psG2[:], snrm, 1.0 / TSEQ,
                                                 ALU.add, ALU.mult), [m1])
                m4a = pr.dma("sync", rcscr[0:1, :], mrow[0:1, :], waits=[m2, m3], sem="dP")
                m4 = pr.dma("sync", mcol[:], _col_ap(rcscr, DM), waits=[m4a], slow_ok=True, sem="dP")
                gfb0 = pr.barrier()
                m5 = V(lambda e: e.tensor_copy(mcolb[:], mcol[:]), [m4, gfb0])
                def g1fn(e):
                    last = None
                    for j in range(4):
                        for k in range(8):
                            last = e.matmul(psH[0][:, j:j + 1],
                                            gw1b[:, k * 512 + j * 128:k * 512 + (j + 1) * 128],
                                            mcolb[:, k:k + 1], start=(k == 0), stop=(k == 7))
                    return last
                tg1 = T(g1fn, [m5] + wl)
                ge1 = []
                for j in range(4):
                    ge1.append(Cc(lambda e, j=j: e.activation(gcol1[:, j:j + 1], psH[0][:, j:j + 1],
                                                              AF.Gelu, bias=gb1c[:, j:j + 1]),
                                  [tg1] + bl))
                gc1 = V(lambda e: e.tensor_copy(gcol1b[:], gcol1[:]), ge1)
                def g2fn(e):
                    last = None
                    for j in range(2):
                        for k in range(4):
                            last = e.matmul(psH[1][:, j:j + 1],
                                            gw2b[:, k * 256 + j * 128:k * 256 + (j + 1) * 128],
                                            gcol1b[:, k:k + 1], start=(k == 0), stop=(k == 3))
                    return last
                tg2 = T(g2fn, [gc1])
                ge2 = []
                for j in range(2):
                    ge2.append(Cc(lambda e, j=j: e.activation(scrap[:, j:j + 1], psH[1][:, j:j + 1],
                                                              AF.Gelu, bias=gb2c[:, j:j + 1]), [tg2]))
                gc2 = V(lambda e: e.tensor_copy(gcol2b[:], scrap[:, 0:2]), ge2)
                def g3fn(e):
                    last = None
                    for k in range(2):
                        last = e.matmul(psTf[0:1, 1:2], gw3c[:, k:k + 1], gcol2b[:, k:k + 1],
                                        start=(k == 0), stop=(k == 1))
                    return last
                tg3 = T(g3fn, [gc2])
                dpre = V(lambda e: e.tensor_copy(rows[0:1, 9:10], psTf[0:1, 1:2]), [tg3])
                tgf = Cc(lambda e: e.activation(gfr, psTf[0:1, 1:2], AF.Tanh, scale=0.5,
                                                bias=gb3h), [tg3, h2, dpre])
                x1_ = V(lambda e: e.tensor_scalar(gfr, gfr, 0.5, 0.5, ALU.mult, ALU.add), [tgf])
                x2_ = V(lambda e: e.tensor_mul(rt1, gfr, arr), [x1_, h3])
                x3_ = V(lambda e: e.tensor_scalar(rt2, arr, -0.7, None, ALU.mult), [x2_])
                x4_ = V(lambda e: e.tensor_scalar(rt1, rt1, 1.4, 0.1, ALU.mult, ALU.add), [x3_])
                x5_ = V(lambda e: e.tensor_add(c0s, rt1, rt2), [x4_])
                gfb1 = pr.barrier()
                x5_ = gfb1
                tcc = T(lambda e: e.matmul(psH[0][:, 0:2], ones_row[0:1, 0:128],
                                           rows[0:1, 3:5], start=True, stop=True), [x5_])
                ecc = V(lambda e: e.tensor_copy(cols[:, 6:8], psH[0][:, 0:2]), [tcc])
                if debug and b == 0:
                    pr.dma("sync", dbg["f_mrow"][:, :], mrow[:], waits=[x5_])
                    pr.dma("sync", dbg["f_mcol"][:, :], mcol[:], waits=[x5_])
                    pr.dma("sync", dbg["f_gcol1"][:, :], gcol1[:], waits=[x5_])
                    pr.dma("sync", dbg["f_rows"][:, :], rows[:], waits=[x5_])
                    pr.dma("sync", dbg["f_rstore"][:, :], rstore[:], waits=[x5_])

                slot2_free = [x5_, x5_]
                xr_free = [x5_, x5_]
                free_state = [[x5_], [x5_]]
                free_psTr = x5_
                prev_casc_end = [ecc]
                cb_free = ecc
                psTf_free = x5_
                trow_free = x5_
                grow_free = x5_
                xlnf_free = x5_
                for cpair in range(2):
                    setups = []
                    rc0s = []
                    for m in range(2):
                        cc2 = cpair * 2 + m
                        base_t = b * TPB + cc2 * 4
                        setup_ev = []
                        for ti in range(4):
                            g = base_t + ti
                            xslot = xs2[ti % 2]
                            ld = pr.dma("sync", xslot[:], src[g * 128:(g + 1) * 128, :],
                                        waits=[slot2_free[ti % 2]], sem=f"dX{ti % 2}")
                            t_idx = cc2 * 4 + ti
                            nrm = Cc(lambda e, xslot=xslot, t_idx=t_idx: e.activation(
                                xlnf[:], xslot[:], AF.Identity,
                                bias=rstore[:, 2 * t_idx + 1:2 * t_idx + 2],
                                scale=rstore[:, 2 * t_idx:2 * t_idx + 1]),
                                [ld, setup_ev[-1] if setup_ev else xlnf_free])
                            slot2_free[ti % 2] = nrm
                            def trfn(e):
                                last = None
                                for j in range(8):
                                    last = e.transpose(psTr[:, j * 128:(j + 1) * 128],
                                                       xlnf[:, j * 128:(j + 1) * 128], idf[:])
                                return last
                            tk = T(trfn, [nrm, free_psTr])
                            def evfn(e, ti=ti, m=m):
                                return e.tensor_copy(
                                    sa[m][:, :].rearrange("p (j q) -> p j q", j=8)[:, :, ti * 128:(ti + 1) * 128],
                                    psTr[:, :].rearrange("p (j q) -> p j q", j=8))
                            ev = V(evfn, [tk] + free_state[m], relax=True)
                            setup_ev.append(ev)
                            free_psTr = ev
                            xlnf_free = ev
                        rc0s.append(V(lambda e, m=m: e.tensor_copy(sB[m][:], sa[m][:]),
                                      setup_ev, relax=True))
                        setups.append(setup_ev)

                    cyc_tok = [rc0s[0], rc0s[1]]
                    tw_last = [setups[0], setups[1]]
                    for c in range(3):
                        for m in range(2):
                            cur = sa[m] if c % 2 == 0 else sb[m]
                            nxt = sb[m] if c % 2 == 0 else sa[m]
                            ev1 = []
                            for j in range(8):
                                def h1fn(e, j=j, m=m):
                                    last = None
                                    for k in range(8):
                                        last = e.matmul(psH[j % 2][:],
                                                        w1b[:, k * 1024 + j * 128:k * 1024 + (j + 1) * 128],
                                                        sB[m][:, k * 512:(k + 1) * 512],
                                                        start=(k == 0), stop=(k == 7))
                                    return last
                                mm = T(h1fn, [cyc_tok[m],
                                              ev1[-2] if len(ev1) >= 2 else (prev_casc_end[-1] if j < 2 else None)]
                                       + (wl if (c == 0 and m == 0 and j == 0) else []))
                                ev1.append(Cc(lambda e, j=j: e.activation(
                                    gh1[:, j * 512:(j + 1) * 512], psH[j % 2][:], AF.Gelu,
                                    bias=b1c[:, j:j + 1]), [mm]))
                            ev2 = []
                            for j in range(4):
                                def h2fn(e, j=j):
                                    last = None
                                    for k in range(8):
                                        last = e.matmul(psH[j % 2][:],
                                                        w2b[:, k * 512 + j * 128:k * 512 + (j + 1) * 128],
                                                        gh1[:, k * 512:(k + 1) * 512],
                                                        start=(k == 0), stop=(k == 7))
                                    return last
                                mm = T(h2fn, [ev1[-1] if j == 0 else None,
                                              ev2[-2] if len(ev2) >= 2 else (ev1[-2] if j == 0 else None)])
                                ev2.append(Cc(lambda e, j=j: e.activation(
                                    gh2[:, j * 512:(j + 1) * 512], psH[j % 2][:], AF.Gelu,
                                    bias=b2c[:, j:j + 1]), [mm]))
                            ev3 = []
                            for j in range(2):
                                def h3fn(e, j=j):
                                    last = None
                                    for k in range(4):
                                        last = e.matmul(psH[j % 2][:],
                                                        w3b[:, k * 256 + j * 128:k * 256 + (j + 1) * 128],
                                                        gh2[:, k * 512:(k + 1) * 512],
                                                        start=(k == 0), stop=(k == 3))
                                    return last
                                mm = T(h3fn, [ev2[-1] if j == 0 else None,
                                              ev2[-2] if j == 0 else None])
                                ev3.append(Cc(lambda e, j=j: e.activation(
                                    gh3[:, j * 512:(j + 1) * 512], psH[j % 2][:], AF.Gelu,
                                    bias=b3c[:, j:j + 1]), [mm]))
                            ttf1 = T(lambda e: e.matmul(psTf[0:1, 0:512], w4c[:, 0:1],
                                                        gh3[:, 0:512], start=True, stop=False),
                                     [ev3[0], psTf_free])
                            ttf = T(lambda e: e.matmul(psTf[0:1, 0:512], w4c[:, 1:2],
                                                       gh3[:, 512:1024], start=False, stop=True),
                                    [ev3[-1], ttf1])
                            ttr = Cc(lambda e: e.activation(trow[:], psTf[0:1, 0:512], AF.Tanh,
                                                            scale=0.5, bias=b4h),
                                     [ttf, h1, trow_free])
                            psTf_free = ttr
                            tgr = V(lambda e: e.tensor_scalar(grow[:], trow[:], c1s, c0s,
                                                              ALU.mult, ALU.add),
                                    [ttr, x5_, grow_free])
                            trow_free = tgr
                            tcb = T(lambda e: e.matmul(psCB[:], ones_row[:], grow[:],
                                                       start=True, stop=True), [tgr, cb_free])
                            grow_free = tcb
                            ecb = V(lambda e: e.tensor_copy(cb[:], psCB[:]), [tcb, cb_free],
                                    relax=True)
                            if debug and b == 0 and cpair == 0 and c == 0 and m == 0:
                                pr.dma("sync", dbg["d_dbc"][:, 0:8], cols[:, 0:8], waits=[ecb])
                                pr.dma("sync", dbg["d_vre"][0:1, 0:512], trow[:], waits=[ecb])
                                pr.dma("sync", dbg["d_du"][:, 0:512], cb[:], waits=[ecb])
                            tw_new = []
                            t1_last = None
                            for i in range(8):
                                si = SG[i]
                                sj = (i + 4) % 8
                                t1 = V(lambda e, sj=sj, cur=cur: e.tensor_mul(
                                    twt[:], cb[:], cur[:, sj * 512:(sj + 1) * 512]),
                                    [ecb, cyc_tok[m]], relax=(i > 0))
                                t1_last = t1
                                op = ALU.add if si > 0 else ALU.subtract
                                t2 = V(lambda e, i=i, cur=cur, nxt=nxt, op=op: e.tensor_tensor(
                                    nxt[:, i * 512:(i + 1) * 512], cur[:, i * 512:(i + 1) * 512],
                                    twt[:], op), [t1])
                                tw_new.append(t2)
                            cb_free = t1_last
                            tw_last[m] = tw_new
                            prev_casc_end = [ev3[-1]]
                            if c < 2:
                                cyc_tok[m] = V(lambda e, m=m, nxt=nxt: e.tensor_copy(
                                    sB[m][:], nxt[:]), tw_new + [ev1[-1]], relax=True)
                            else:
                                cyc_tok[m] = None
                    for m in range(2):
                        cc2 = cpair * 2 + m
                        base_t = b * TPB + cc2 * 4
                        ad_last = None
                        for ti in range(4):
                            g = base_t + ti
                            bi = ti % 2
                            xo = xio[bi]
                            ldr = pr.dma("sync", xr[bi][:], src[g * 128:(g + 1) * 128, :],
                                         waits=[xr_free[bi]], sem=f"dX{2 + bi}")
                            def trfn2(e, ti=ti, m=m):
                                last = None
                                for j in range(8):
                                    last = e.transpose(psTr[:, j * 128:(j + 1) * 128],
                                                       sb[m][:, j * 512 + ti * 128:j * 512 + (ti + 1) * 128],
                                                       idf[:])
                                return last
                            tk = T(trfn2, tw_last[m] + [free_psTr])
                            ad = V(lambda e, xo=xo, bi=bi: e.tensor_add(
                                xo[:], psTr[:], xr[bi][:]),
                                [tk, ldr, st_toks[-2] if len(st_toks) >= 2 else None], relax=True)
                            st = pr.dma("sync", dst[g * 128:(g + 1) * 128, :], xo[:], waits=[ad],
                                        sem=f"dY{bi}")
                            st_toks.append(st)
                            xr_free[bi] = ad
                            free_psTr = ad
                            ad_last = ad
                        free_state[m] = [ad_last]
                tok_out_final = [st_toks[-1]]


            pr.add("sync", lambda e: e.nop(),
                   waits=[("x", s, co.counts[s]) for s in
                          ("dL", "dG", "dP", "dX0", "dX1", "dX2", "dX3", "dY0", "dY1")
                          if co.counts[s] > 0])
            endbar = pr.barrier()
            pr.emit()
            return endbar

    def att_phase(prev_bar):
        src, dst = io["att"]
        with contextlib.ExitStack() as ps:
            def S(name, shape, dtype):
                return ps.enter_context(nc.sbuf_tensor("a" + name, shape, dtype))
            wab = S("wab", [128, 8 * 3072], BF16)
            wp64 = S("wp64", [64, 16 * 1024], BF16)
            baq = S("baq", [128, 24], F32)
            bvbc = S("bvbc", [128, 1024], BF16)
            hs = S("hs", [128, 2], F32)
            chsb = S("chsb", [8, 2], F32)
            cbias = S("cbias", [128, 16], F32)
            kT = S("kT", [128, 8 * 2048], BF16)
            vst = S("vst", [128, 16 * 1040], BF16)
            qT = S("qT", [128, 8 * 512], BF16)
            yb64 = S("yb64", [128, 16 * 512], BF16)
            bprow = yb64[64:65, 0:1024]
            obr = yb64[64:65, 1024:1152]
            xlnT = S("xlnT", [128, 8 * 512], BF16)
            xin2 = [S(f"xin{i}", [128, 1024], F32) for i in range(2)]
            xio = [S(f"xio{i}", [128, 1024], F32) for i in range(2)]
            xln4 = S("xln4", [128, 4096], BF16)
            scrap = None
            cols = S("cols", [128, 8], F32)
            ebuf = [xlnT[:, i * 512:(i + 1) * 512] for i in range(5)]
            zbc = S("zbc", [64, 512], F32)
            zrow = zbc[0:1, 0:512]
            zinv = zrow
            psA = ps.enter_context(nc.psum_tensor("apsA", [128, 512], F32))
            psB = ps.enter_context(nc.psum_tensor("apsB", [128, 512], F32))
            psC = ps.enter_context(nc.psum_tensor("apsC", [128, 512], F32))
            psD = ps.enter_context(nc.psum_tensor("apsD", [128, 512], F32))
            psV = ps.enter_context(nc.psum_tensor("apsV", [128, 1024], F32))
            psY = ps.enter_context(nc.psum_tensor("apsY", [65, 512], F32))
            psZ = psV[0:64, 0:512]
            psTb = ps.enter_context(nc.psum_tensor("apsTb", [128, 1024], BF16))

            pr = co.prog()

            def V(fn, waits, relax=False):
                return pr.add("vector", fn, waits=waits, inc="sV", relax=relax)

            def Cc(fn, waits):
                return pr.add("scalar", fn, waits=waits, inc="sC")

            def T(fn, waits):
                return pr.add("tensor", fn, waits=waits, inc="sT")

            wl0 = pr.dma("gpsimd", _wslices_dst(wab, 8, 3072), _wslices_ap(w_attn, DM, 3 * DM),
                         waits=[prev_bar])
            wl1 = pr.dma("gpsimd", wp64[:, :].rearrange("d (h o) -> d h o", h=16),
                         w_proj.rearrange("(h d) o -> d h o", d=64), waits=[prev_bar])
            wl = [wl1]
            l1 = pr.dma("sync", baq[:], _col_ap(b_attn, 3 * DM), waits=[prev_bar],
                        slow_ok=True, sem="dP")
            bvrow = xio[0][0:1, 0:1024]
            chrow = xio[1][0:1, 0:16]
            l2 = pr.dma("sync", bvrow, b_attn[2048:3072].rearrange("(one n) -> one n", one=1),
                        waits=[prev_bar], sem="dP")
            l3 = pr.dma("gpsimd", bprow[:], _row_ap(b_proj, DM), waits=[prev_bar])
            wl = [l3]
            dp_all = l2

            hz = []
            hz.append(V(lambda e: e.memset(obr, 1.0), [prev_bar]))
            hz.append(V(lambda e: e.memset(hs[0:64, 0:1], 1.0), [prev_bar]))
            hz.append(V(lambda e: e.memset(hs[64:128, 0:1], 0.0), [prev_bar]))
            hz.append(V(lambda e: e.memset(hs[0:64, 1:2], 0.0), [prev_bar]))
            hz.append(V(lambda e: e.memset(hs[64:128, 1:2], 1.0), [prev_bar]))
            tq = V(lambda e: e.tensor_mul(baq[:, 16:24], baq[:, 0:8], baq[:, 8:16]), [dp_all])
            tm1 = T(lambda e: e.matmul(psY[0:8, 0:2], baq[:, 16:24], hs[:, 0:2],
                                       start=True, stop=True), [tq] + hz)
            tm2 = V(lambda e: e.tensor_scalar(chsb[:], psY[0:8, 0:2], -0.125, None, ALU.mult), [tm1])
            tm3 = pr.dma("sync", rcscr[0:1, 0:16].rearrange("one (a b) -> (one a) b", a=8),
                         chsb[0:8, 0:2], waits=[tm2], sem="dP", slow_ok=True)
            tm4 = pr.dma("sync", chrow, rcscr[0:1, 0:16], waits=[tm3], sem="dP")
            tm5 = T(lambda e: e.matmul(psA[:, 0:16], ones_row[0:1, 0:128], chrow,
                                       start=True, stop=True), [tm4])
            tm5b = T(lambda e: e.matmul(psV[:, 0:512], ones_row[:], xio[0][0:1, 0:512],
                                        start=True, stop=True), [dp_all, tm5])
            tm5c = T(lambda e: e.matmul(psV[:, 512:1024], ones_row[:], xio[0][0:1, 512:1024],
                                        start=True, stop=True), [tm5b])
            tm6 = V(lambda e: e.tensor_copy(cbias[:], psA[:, 0:16]), [tm5])
            tm7 = tm6
            tm8 = V(lambda e: e.tensor_copy(bvbc[:], psV[:]), [tm5c])
            tm9 = V(lambda e: e.memset(vst[:, 64:16 * 1040:65], 1.0), [prev_bar])
            setup_done = pr.barrier()

            st_toks = []
            last_smm_prev_chunk = setup_done
            last_vmm_prev_chunk = prev_bar
            last_ye_prev_chunk = prev_bar
            last_proj_prev_chunk = setup_done
            prev_exp_chunk = [setup_done]
            slot_free = [prev_bar, prev_bar]
            free_xio = [setup_done, setup_done]
            tok_psTb_free = prev_bar
            prev_expD = setup_done
            pfx = None
            NCH = B_PER_CORE * 4
            for gc in range(NCH):
                b = gc // 4
                cc = gc % 4
                base_g = gc * 4
                ev_xlnT = []
                qkv_x_free = [None]
                if pfx is None:
                    lntoks = []
                    for ti in range(4):
                        g = base_g + ti
                        xslot = xin2[ti % 2]
                        ld = pr.dma("sync", xslot[:], src[g * 128:(g + 1) * 128, :],
                                    waits=[slot_free[ti % 2]], sem=f"dX{ti % 2}")
                        xl = xln4[:, ti * 1024:(ti + 1) * 1024]
                        tok_xln, _, _, _ = ln_tile(pr, xslot, xl, cols, xl, ld)
                        slot_free[ti % 2] = tok_xln
                        qkv_x_free.append(tok_xln)
                        lntoks.append(tok_xln)
                else:
                    lntoks = pfx
                for ti in range(4):
                    def trfn(e, ti=ti):
                        last = None
                        for j in range(8):
                            last = e.transpose(psTb[:, j * 128:(j + 1) * 128],
                                               xln4[:, ti * 1024 + j * 128:
                                                     ti * 1024 + (j + 1) * 128], idb[:])
                        return last
                    tk = T(trfn, [lntoks[ti], tok_psTb_free])
                    def evfn(e, ti=ti):
                        return e.tensor_copy(
                            xlnT[:, :].rearrange("p (j q) -> p j q", j=8)[:, :, ti * 128:(ti + 1) * 128],
                            psTb[:, :].rearrange("p (j q) -> p j q", j=8))
                    ev = V(evfn, [tk, last_vmm_prev_chunk if ti == 0 else None,
                                  last_ye_prev_chunk if ti == 0 else None], relax=True)
                    ev_xlnT.append(ev)
                    tok_psTb_free = ev
                qk_ev = []
                for part, (coff, dstT, dst_off) in enumerate((
                        (0, qT, lambda j, cc=cc: (j * 512, 512)),
                        (1024, kT, lambda j, cc=cc: (j * 2048 + cc * 512, 512)))):
                    for j in range(8):
                        idx = part * 8 + j
                        psX = (psA, psB, psC)[idx % 3]
                        def qkfn(e, coff=coff, j=j, psX=psX):
                            last = None
                            for k in range(8):
                                last = e.matmul(psX[:],
                                                wab[:, k * 3072 + coff + j * 128:
                                                    k * 3072 + coff + (j + 1) * 128],
                                                xlnT[:, k * 512:(k + 1) * 512],
                                                start=(k == 0), stop=(k == 7))
                            return last
                        mm = T(qkfn, ev_xlnT + [wl0 if (part == 0 and j == 0) else None,
                                                qk_ev[-3] if len(qk_ev) >= 3 else None]
                               + (prev_exp_chunk if idx < 3 else []))
                        off, width = dst_off(j)
                        qk_ev.append(Cc(lambda e, dstT=dstT, off=off, width=width, j=j,
                                        part=part, psX=psX: e.activation(
                            dstT[:, off:off + width], psX[:], AF.Identity,
                            bias=baq[:, 8 * part + j:8 * part + j + 1]), [mm, dp_all]))
                v_ev = []
                for ti in range(4):
                    kt_g = cc * 4 + ti
                    def vfn(e, ti=ti):
                        last = None
                        for half in range(2):
                            for k in range(8):
                                last = e.matmul(
                                    psV[:, half * 512:(half + 1) * 512],
                                    xlnT[:, k * 512 + ti * 128:k * 512 + (ti + 1) * 128],
                                    wab[:, k * 3072 + 2048 + half * 512:
                                        k * 3072 + 2048 + (half + 1) * 512],
                                    start=(k == 0), stop=(k == 7))
                        return last
                    mm = T(vfn, ev_xlnT + [v_ev[-1] if v_ev else tm8,
                                           last_proj_prev_chunk if ti == 0 else None])
                    last_vmm = mm
                    def vefn(e, kt_g=kt_g):
                        return e.scalar_tensor_tensor(
                            vst[:, kt_g * 1040:kt_g * 1040 + 1040]
                            .rearrange("p (h c) -> p h c", h=16)[:, :, 0:64],
                            psV[:, :].rearrange("p (h c) -> p h c", h=16),
                            1.0, bvbc[:, :].rearrange("p (h c) -> p h c", h=16),
                            ALU.mult, ALU.add)
                    v_ev.append(V(vefn, [mm, tm9], relax=True))
                if debug and b == 0 and cc == 0:
                    pr.dma("sync", dbg["a_qT"][:, :], qT[:], waits=[qk_ev[-1]])
                    pr.dma("sync", dbg["a_kT"][:, :], kT[:, 0:512], waits=[qk_ev[-1]])
                    pr.dma("sync", dbg["a_vst"][:, :], vst[:, 0:1040], waits=[v_ev[-1]])
                    pr.add("sync", lambda e: e.nop(), waits=[("x", "dL", co.counts["dL"])])
                    dbgbar = pr.barrier()
                    last_smm_prev_chunk = dbgbar
                    last_proj_prev_chunk = dbgbar
                    prev_exp_chunk = [dbgbar]
                nkt = cc * 4 + 4
                ebn = 5
                new_pfx = []
                stt_ = {"last_yevac": last_proj_prev_chunk, "prev_z4": None,
                        "pend": None}
                prev_exp = [None, None, None, prev_expD]
                ytok = {}
                last_smm = None
                for h in range(16):
                    hp = h % 2
                    hg = h // 2

                    def emit_y(kt, h=h, nkt=nkt, cc=cc):
                        delta = kt - cc * 4
                        o = max(delta, 0) * 128
                        def yfn(e, kt=kt, h=h, nkt=nkt, o=o):
                            return e.matmul(psY[:, o:512],
                                            vst[:, kt * 1040 + h * 65:kt * 1040 + (h + 1) * 65],
                                            ebuf[kt % ebn][:, o:512],
                                            start=(kt == 0), stop=(kt == nkt - 1))
                        w = [ytok[(h, kt, "m")]]
                        if kt == 0:
                            w.append(stt_["last_yevac"])
                        t = T(yfn, w)
                        ytok[(h, kt, "y")] = t
                        return t

                    for kt in range(nkt):
                        pp = kt % 4
                        pe = kt % ebn
                        delta = kt - cc * 4
                        o = max(delta, 0) * 128
                        psS = (psA, psB, psC, psD)[pp]
                        def sfn(e, kt=kt, psS=psS, hp=hp, hg=hg, o=o):
                            return e.matmul(
                                psS[:, o:512],
                                kT[64 * hp:64 * hp + 64, hg * 2048 + kt * 128:
                                   hg * 2048 + (kt + 1) * 128],
                                qT[64 * hp:64 * hp + 64, hg * 512 + o:(hg + 1) * 512],
                                start=True, stop=True)
                        smm = T(sfn, [qk_ev[-1] if (h == 0 and kt == 0) else None,
                                      v_ev[-1] if (h == 0 and kt < 3) else None,
                                      prev_exp[pp]])
                        last_smm = smm
                        gk = h * nkt + kt - ebn
                        ebfree = ytok.get((gk // nkt, gk % nkt, "y")) if gk >= 0 else None
                        ex = Cc(lambda e, pe=pe, h=h, psS=psS, o=o: e.activation(
                            ebuf[pe][:, o:512], psS[:, o:512], AF.Exp, scale=0.125,
                            bias=cbias[:, h:h + 1]), [smm, tm7, ebfree])
                        mask_tok = ex
                        if delta >= 0:
                            mask_tok = V(lambda e, pe=pe, delta=delta: e.tensor_mul(
                                ebuf[pe][:, delta * 128:(delta + 1) * 128],
                                ebuf[pe][:, delta * 128:(delta + 1) * 128],
                                trm[:]), [ex], relax=True)
                        prev_exp[pp] = mask_tok
                        ytok[(h, kt, "m")] = mask_tok
                        if kt == 3 and stt_["pend"] is not None:
                            stt_["pend"]()
                            stt_["pend"] = None
                        if kt >= 4:
                            emit_y(kt - 4)
                    emit_y(nkt - 4)
                    emit_y(nkt - 3)
                    emit_y(nkt - 2)
                    y_last = emit_y(nkt - 1)
                    z2 = Cc(lambda e: e.activation(zinv, psY[64:65, :], AF.Ln),
                            [y_last, stt_["prev_z4"]])
                    z3 = Cc(lambda e: e.activation(zinv, zinv, AF.Exp, scale=-1.0), [z2])

                    def mk_fin(h=h, z3=z3, y_last=y_last, prev_ye=stt_["last_yevac"]):
                        def fin():
                            z4 = T(lambda e: e.matmul(psZ, ones_row[0:1, 0:64], zinv,
                                                      start=True, stop=True),
                                   [z3, prev_ye])
                            z5 = V(lambda e: e.tensor_copy(zbc[:], psZ), [z4])
                            ye = V(lambda e, h=h: e.tensor_mul(
                                yb64[0:64, h * 512:(h + 1) * 512],
                                psY[0:64, :], zbc[:]), [z5, y_last], relax=True)
                            stt_["last_yevac"] = ye
                            stt_["prev_z4"] = ye
                        return fin
                    stt_["pend"] = mk_fin()
                    if h == 15:
                        stt_["pend"]()
                        stt_["pend"] = None
                    if gc + 1 < NCH and h in (1, 3, 5, 7):
                        ti_e = (h - 1) // 2
                        bi_e = ti_e % 2
                        g2 = (gc + 1) * 4 + ti_e
                        eld = pr.dma("sync", xio[bi_e][:],
                                     src[g2 * 128:(g2 + 1) * 128, :],
                                     waits=[free_xio[bi_e]], sem=f"dX{2 + bi_e}")
                        xl = xln4[:, ti_e * 1024:(ti_e + 1) * 1024]
                        tokE, _, _, _ = ln_tile(pr, xio[bi_e], xl, cols, xl, eld,
                                                wait_buf=[ev_xlnT[ti_e]])
                        new_pfx.append(tokE)
                        free_xio[bi_e] = tokE
                last_yevac = stt_["last_yevac"]
                proj_last = None
                proj_half_free = [None, None]
                for ti in range(4):
                    g = base_g + ti
                    bi = ti % 2
                    xo = xio[bi]
                    ldr = pr.dma("sync", xin2[bi][:], src[g * 128:(g + 1) * 128, :],
                                 waits=[slot_free[bi], qkv_x_free[-1]], sem=f"dX{bi}")
                    ad = None
                    for half in range(2):
                        def pfn(e, ti=ti, half=half):
                            for h in range(16):
                                e.matmul(psV[:, half * 512:(half + 1) * 512],
                                         yb64[0:64, h * 512 + ti * 128:
                                              h * 512 + (ti + 1) * 128],
                                         wp64[0:64, h * 1024 + half * 512:
                                              h * 1024 + (half + 1) * 512],
                                         start=(h == 0), stop=False)
                            return e.matmul(psV[:, half * 512:(half + 1) * 512],
                                            obr,
                                            bprow[0:1, half * 512:(half + 1) * 512],
                                            start=False, stop=True)
                        mmp = T(pfn, [last_yevac if ti == 0 else None,
                                      v_ev[-1] if ti == 0 else None,
                                      proj_half_free[half],
                                      wl[-1] if not st_toks else None,
                                      st_toks[-2] if (half == 0 and len(st_toks) >= 2) else None])
                        ad = V(lambda e, xo=xo, bi=bi, half=half: e.tensor_add(
                            xo[:, half * 512:(half + 1) * 512],
                            psV[:, half * 512:(half + 1) * 512],
                            xin2[bi][:, half * 512:(half + 1) * 512]),
                            [mmp, ldr if half == 0 else None, free_xio[bi]], relax=True)
                        proj_half_free[half] = ad
                    st = pr.dma("sync", dst[g * 128:(g + 1) * 128, :], xo[:],
                                waits=[ad], sem=f"dY{bi}")
                    st_toks.append(st)
                    free_xio[bi] = st
                    slot_free[bi] = ad
                    proj_last = ad
                pfx = new_pfx if len(new_pfx) == 4 else None
                last_smm_prev_chunk = last_smm
                last_vmm_prev_chunk = last_vmm
                last_ye_prev_chunk = last_yevac
                last_proj_prev_chunk = proj_last
                prev_expD = prev_exp[3]
                prev_exp_chunk = [t for t in prev_exp[0:3] if t is not None]

            pr.add("sync", lambda e: e.nop(),
                   waits=[("x", s, co.counts[s]) for s in
                          ("dL", "dG", "dP", "dX0", "dX1", "dX2", "dX3", "dY0", "dY1")
                          if co.counts[s] > 0])
            endbar = pr.barrier()
            pr.emit()
            return endbar

    cur_bar = bar0
    if "ssm" in parts:
        cur_bar = ssm_phase(cur_bar)
    if "ffn1" in parts:
        cur_bar = ffn_phase(cur_bar, "ffn1")
    if "att" in parts:
        cur_bar = att_phase(cur_bar)
    if "ffn2" in parts:
        cur_bar = ffn_phase(cur_bar, "ffn2")

    pf = co.prog()
    pf.add("sync", lambda e: e.nop(),
           waits=[cur_bar] + [("x", s, co.counts[s]) for s in
                              ("dL", "dG", "dP", "dX0", "dX1", "dX2", "dX3", "dY0", "dY1")
                              if co.counts[s] > 0])
    pf.emit()
    es.close()
    return nc


def _make_consts():
    iden = np.eye(128)
    tri = (np.arange(128)[:, None] <= np.arange(128)[None, :]).astype(np.float32)
    return {
        "c_identb": iden.astype(ml_dtypes.bfloat16),
        "c_identf": iden.astype(np.float32),
        "c_triuf": tri,
        "c_trium": tri.astype(ml_dtypes.bfloat16),
    }


def kernel(**inputs):
    parts = ("ssm", "ffn1", "att", "ffn2")
    nc = build_program(parts)
    consts = _make_consts()
    x = np.ascontiguousarray(np.asarray(inputs["L0_attn_a_u"], dtype=np.float32))
    in_maps = []
    for core in range(NCORES):
        m = {"x": x[2 * core:2 * core + 2].reshape(NT, DM)}
        for k, v in inputs.items():
            if k == "L0_attn_a_u":
                continue
            m[k] = np.ascontiguousarray(np.asarray(v, dtype=np.float32))
        m.update(consts)
        in_maps.append(m)
    res = run_bass_kernel_spmd(nc, in_maps, core_ids=list(range(NCORES)))
    out = np.concatenate([np.asarray(r["y"]).reshape(2, TSEQ, DM) for r in res.results], axis=0)
    return out
